# version 05 (final)
# baseline (speedup 1.0000x reference)
import numpy as np
import ml_dtypes
from contextlib import ExitStack

import concourse.bass as bass
import concourse.mybir as mybir
from concourse.ap import AP
from concourse.bass_utils import run_bass_kernel_spmd

F32 = mybir.dt.float32
BF16 = mybir.dt.bfloat16
I32 = mybir.dt.int32
AF = mybir.ActivationFunctionType
ALU = mybir.AluOpType
AXX = mybir.AxisListType.X
bfloat16 = ml_dtypes.bfloat16

S = 8192
DM = 2048
NCORE = 8
T = 1024
NTT = 8
HPC = 2
DH = 128
GU = 16384
H0 = 8192
INNER = 4096
NOF = 32
I1 = 8192
MSK = -1.0e38
EPS_Q = 1.0e-05
EPS_RMS = float(np.float32(1.1920929e-07))
BF_HALF = 0.5009765625
QK_SCALE = float(np.float32(1.0) / np.float32(np.sqrt(np.float32(128.0))))

_DEBUG_TAPS = ()
_DBG_DRAM = ()
_DBG_Y = None
_SKIP_PH0 = False
_SKIP_BG = False
_NO_PIPE = False
_DBG_QKV = False
_SKIP_LN = False
_DBG_PROJONLY = False
_BUILD_UPTO = "E"


class KB:
    ENGS = ("tensor", "vector", "scalar", "sync", "gpsimd")

    def __init__(self, nc, es):
        self.nc = nc
        self.es = es
        self.ops = {e: [] for e in self.ENGS}
        self.cnt = {}
        self._names = {}
        self.bg = []

    def sem(self, name):
        s = self.es.enter_context(self.nc.semaphore(name))
        self.cnt[name] = 0
        self._names[name] = s
        return s

    def S(self, name):
        return self._names[name]

    def bump(self, name, amt=1):
        self.cnt[name] += amt
        return self.cnt[name]

    def val(self, name):
        return self.cnt[name]

    def emit(self, eng, build, waits=(), inc=None, inc_amt=1):
        waits = tuple((self._names[w], v) for (w, v) in waits)
        tgt = None
        if inc is not None:
            self.bump(inc, inc_amt)
            tgt = self._names[inc]

        def f(e, _b=build, _w=waits, _t=tgt, _a=inc_amt):
            for (s, v) in _w:
                e.wait_ge(s, v)
            ins = _b(e)
            if _t is not None:
                ins.then_inc(_t, _a)

        self.ops[eng].append(f)

    def wait(self, eng, sem_name, value):
        s = self._names[sem_name]
        self.ops[eng].append(lambda e, _s=s, _v=value: e.wait_ge(_s, _v))

    def add_bg(self, eng, build, waits=(), inc=None, inc_amt=1):
        waits = tuple((self._names[w], v) for (w, v) in waits)
        tgt = None
        if inc is not None:
            self.bump(inc, inc_amt)
            tgt = self._names[inc]

        def f(e, _b=build, _w=waits, _t=tgt, _a=inc_amt):
            for (s, v) in _w:
                e.wait_ge(s, v)
            ins = _b(e)
            if _t is not None:
                ins.then_inc(_t, _a)

        self.bg.append((eng, f))

    def drain_bg(self, n=None):
        take = self.bg if n is None else self.bg[:n]
        self.bg = [] if n is None else self.bg[n:]
        for eng, f in take:
            self.ops[eng].append(f)


def _sb(nc, es, name, shape, dt):
    return es.enter_context(nc.sbuf_tensor(name, shape, dt))


def _psum(nc, es, name, shape):
    return es.enter_context(nc.psum_tensor(name, shape, F32))


def build_nc():
    nc = bass.Bass("TRN2", target_bir_lowering=False, debug=False,
                   detect_race_conditions=False)
    es = ExitStack()
    k = KB(nc, es)

    DI = {}

    def din(name, shape, dt=F32):
        DI[name] = nc.dram_tensor(name, shape, dt, kind="ExternalInput").ap()

    din("x_full", [S, DM])
    din("x_own", [T, DM])
    din("wq", [128, 16, 256])
    din("wk", [128, 16, 256])
    din("wv", [128, 16, 256])
    din("bq", [128, 2])
    din("bk", [128, 2])
    din("bv", [128, 2])
    din("wproj_t", [16, 128, DM])
    din("bproj_b", [128, DM])
    din("w_gu_sl", [4, 128, 4, DM])
    din("w_p0_sl", [2, 128, H0])
    din("w_in_t", [NOF, 128, 16, 128])
    din("w_gt_t", [NOF, 128, 16, 128])
    din("w_out_t", [16, NOF, 128, 128])
    din("conv_w", [128, NOF, 4])
    din("conv_b", [128, NOF, 1])
    din("dtw_blk", [NOF, 128, 128])
    din("dtb_pt", [128, NOF, 1])
    din("alog_pt", [128, NOF, 1])
    din("bp_pt", [128, NOF, 1])
    din("cp_pt", [128, NOF, 1])
    din("dp_pt", [128, NOF, 1])
    din("w_g1_sl", [8, 128, DM])
    din("w_u1_sl", [8, 128, DM])
    din("w_d1_sl", [2, 128, I1])
    din("tri", [128, 128])
    din("idf", [128, 128])
    din("idb", [128, 128], BF16)
    din("rank_i", [1, 1], I32)
    din("prev_i", [1, 1], I32)
    din("keep_b", [128, NCORE])
    din("halo_m", [128, 1])

    y_out = nc.dram_tensor("y", [T, DM], F32, kind="ExternalOutput").ap()
    taps = {}
    for tname, tshape in _DEBUG_TAPS:
        taps[tname] = nc.dram_tensor("tap_" + tname, tshape, F32,
                                     kind="ExternalOutput").ap()

    dr = {}

    def dint(name, shape, dt=F32):
        if name in _DBG_DRAM:
            dr[name] = nc.dram_tensor(name, shape, dt, kind="ExternalOutput").ap()
        else:
            dr[name] = nc.dram_tensor(name, shape, dt).ap()

    dint("xT_d", [DM, S])
    dint("ao_in", [S, 256])
    dint("ao_out", [NCORE * S, 256])
    dint("q1a_in", [DM, DM], BF16)
    dint("q1a_out", [GU, DM], BF16)
    dint("q1b_in", [256, H0], BF16)
    dint("q1b_out", [DM, H0], BF16)
    dint("whT_d", [DM, GU], BF16)
    dint("wp0T_d", [H0, DM], BF16)
    dint("cc2_in", [1, 4])
    dint("cc2_out", [NCORE, 4])
    dint("q3a_in", [T, DM], BF16)
    dint("q3a_out", [I1, DM], BF16)
    dint("q3b_in", [T, DM], BF16)
    dint("q3b_out", [I1, DM], BF16)
    dint("q3c_in", [256, I1], BF16)
    dint("q3c_out", [DM, I1], BF16)
    dint("wg1T_d", [DM, I1], BF16)
    dint("wu1T_d", [DM, I1], BF16)
    dint("wd1T_d", [I1, DM], BF16)
    dint("cc5_in", [1, NOF * 128 * 3])
    dint("cc5_out", [NCORE, NOF * 128 * 3])
    dint("cc6_in", [1, NOF * 128 * 2])
    dint("cc6_out", [NCORE, NOF * 128 * 2])
    dint("ao_my", [NCORE, T, 256])
    if _DBG_QKV:
        dr["dbg_q"] = nc.dram_tensor("dbg_q", [128, S], F32, kind="ExternalOutput").ap()
        dr["dbg_k"] = nc.dram_tensor("dbg_k", [128, S], F32, kind="ExternalOutput").ap()
        dr["dbg_v"] = nc.dram_tensor("dbg_v", [128, 64, 128], F32, kind="ExternalOutput").ap()
    dint("x2_d", [128, NTT, DM])
    if "x1_d" in _DBG_DRAM:
        dint("x1_d", [128, NTT, DM])
    if "aoin_dbg" in _DBG_DRAM:
        dint("aoin_dbg", [S, 256])
    dint("gate_d", [128, NOF, T])
    dint("proj_d", [128, NOF, T])
    dint("X_d", [128, NOF, T])
    dint("dtv_d", [128, NOF, T])
    dint("z_d", [128, NOF, T])

    for s in ("ld", "lw", "mm", "ev", "ex", "tp", "pv", "mg", "st", "cc",
              "qa", "qb", "qc", "wt", "so", "gd", "gld", "gst"):
        k.sem(s)

    with es:
        ident_f = _sb(nc, es, "ident_f", [128, 128], F32)
        tri_sb = _sb(nc, es, "tri_sb", [128, 128], F32)
        ident_b = _sb(nc, es, "ident_b", [128, 128], BF16)
        k.emit("sync", lambda e: e.dma_start(out=ident_f[:], in_=DI["idf"]),
               inc="ld", inc_amt=16)
        k.emit("sync", lambda e: e.dma_start(out=tri_sb[:], in_=DI["tri"]),
               inc="ld", inc_amt=16)
        k.emit("sync", lambda e: e.dma_start(out=ident_b[:], in_=DI["idb"]),
               inc="ld", inc_amt=16)
        consts_ld = k.val("ld")
        ws_b = _sb(nc, es, "ws_b", [128, 4], F32)
        ps_x = _psum(nc, es, "ps_x", [128, 16])

        if not _SKIP_PH0:
            _phase0(nc, k, DI, dr, es, ws_b, ps_x)
        p0_gate = [("qb", k.val("qb")), ("gst", k.val("gst"))]
        st_a0 = _phaseA0(nc, k, DI, dr, es, ident_f, consts_ld, p0_gate)
        if not (_SKIP_PH0 or _SKIP_BG):
            _queue_pretranspose(nc, k, dr, es)
        st_attn = _phaseA(nc, k, DI, dr, es, ident_f, tri_sb, taps, st_a0)
        if _BUILD_UPTO == "A":
            _finish_dump(nc, k, y_out, None, dr)
        else:
            es_bc = ExitStack()
            with es_bc:
                strm = _sb(nc, es_bc, "strm", [128, NTT, DM], F32)
                _phaseB(nc, k, DI, dr, es_bc, ident_f, strm, taps, st_attn)
                if _BUILD_UPTO == "B":
                    _finish_dump(nc, k, y_out, strm, dr)
                    done = True
                else:
                    _phaseC(nc, k, DI, dr, es_bc, ident_f, ident_b, strm, taps)
                    done = False
            if not done and _BUILD_UPTO == "C":
                _finish_dump(nc, k, y_out, None, dr)
            elif not done:
                es_de = ExitStack()
                with es_de:
                    x3 = _sb(nc, es_de, "x3buf", [128, NTT, DM], F32)
                    _phaseD(nc, k, DI, dr, es_de, ident_f, x3, taps)
                    if _BUILD_UPTO == "D":
                        _finish_dump(nc, k, y_out, x3, dr)
                    else:
                        _phaseE(nc, k, DI, dr, es_de, ident_b, ident_f, ws_b, x3, y_out, taps)
                        k.wait("gpsimd", "so", k.val("so"))

        with nc.Block() as block:
            @block.tensor
            def _(e):
                for f in k.ops["tensor"]:
                    f(e)

            @block.vector
            def _(e):
                for f in k.ops["vector"]:
                    f(e)

            @block.scalar
            def _(e):
                for f in k.ops["scalar"]:
                    f(e)

            @block.sync
            def _(e):
                for f in k.ops["sync"]:
                    f(e)

            @block.gpsimd
            def _(e):
                for f in k.ops["gpsimd"]:
                    f(e)
    return nc


def _finish_dump(nc, k, y_out, strm, dr):
    if _DBG_Y == "ao_my":
        src_ = dr["ao_my"].rearrange("m t c -> (m t) c")
        k.emit("gpsimd", lambda e: e.dma_start(
            out=y_out[:].rearrange("t d -> (t d)"),
            in_=src_.rearrange("a b -> (a b)")),
            waits=[("ev", k.val("ev")), ("gld", k.val("gld"))],
            inc="so", inc_amt=16)
        k.wait("gpsimd", "so", k.val("so"))
        return
    if strm is not None:
        for tt in range(NTT):
            k.emit("gpsimd",
                   lambda e, _t=tt: e.dma_start(
                       out=y_out[_t * 128:(_t + 1) * 128, :], in_=strm[:, _t, :]),
                   waits=[("ev", k.val("ev"))], inc="so", inc_amt=16)
    else:
        for tt in range(NTT):
            k.emit("gpsimd",
                   lambda e, _t=tt: e.dma_start(
                       out=y_out[_t * 128:(_t + 1) * 128, :],
                       in_=dr["x2_d"][:, _t, :]),
                   waits=([("st", k.val("st"))] if tt == 0 else []),
                   inc="so", inc_amt=16)
    k.wait("gpsimd", "so", k.val("so"))


def _phaseA0(nc, k, DI, dr, es_top, ident_f, consts_ld, p0_gate):
    es = ExitStack()
    x_full = DI["x_full"]
    xT_d = dr["xT_d"]
    with es:
        xa = [_sb(nc, es, f"a0_x{i}", [128, DM], F32) for i in range(2)]
        xt = [_sb(nc, es, f"a0_t{i}", [128, 16, 128], F32) for i in range(2)]
        pst = [_psum(nc, es, f"a0_ps{i}", [128, 128]) for i in range(2)]
        tp_at = []
        st_at = []
        for g in range(64):
            b = g % 2
            w = ([("ld", consts_ld)] + p0_gate) if g < 2 else [("tp", tp_at[g - 2])]
            k.emit("sync", lambda e, _b=b, _g=g: e.dma_start(
                out=xa[_b][:], in_=x_full[_g * 128:(_g + 1) * 128, :]),
                waits=w, inc="ld", inc_amt=16)
            ldv = k.val("ld")
            for fc in range(16):
                pb = fc % 2
                w2 = [("ld", ldv)] if fc < 2 else [("ld", ldv), ("ev", k.val("ev") - 1)]
                k.emit("tensor", lambda e, _b=b, _fc=fc, _pb=pb: e.transpose(
                    pst[_pb][:], xa[_b][:, _fc * 128:(_fc + 1) * 128], ident_f[:]),
                    waits=w2, inc="tp")
                tpv = k.val("tp")
                w3 = [("tp", tpv)]
                if g >= 2 and fc == 0:
                    w3.append(("st", st_at[g - 2]))
                k.emit("vector", lambda e, _b=b, _fc=fc, _pb=pb: e.tensor_copy(
                    xt[_b][:, _fc, :], pst[_pb][:]), waits=w3, inc="ev")
            tp_at.append(k.val("tp"))
            evv = k.val("ev")
            dst = AP(tensor=xT_d.tensor, offset=g * 128,
                     ap=[[S, 128], [128 * S, 16], [1, 128]])
            k.emit("sync", lambda e, _b=b, _d=dst: e.dma_start(out=_d, in_=xt[_b][:]),
                   waits=[("ev", evv)], inc="st", inc_amt=16)
            st_at.append(k.val("st"))
    return k.val("st")


def _phase0(nc, k, DI, dr, es_top, ws_b, ps_x):
    es = ExitStack()
    with es:
        wsl = _sb(nc, es, "p0_wsl", [128, 8192], F32)
        wbf = _sb(nc, es, "p0_wbf", [128, 8192], BF16)
        scl = _sb(nc, es, "p0_scl", [128, 128], F32)
        sclb = _sb(nc, es, "p0_sclb", [128, 128], BF16)
        thr = _sb(nc, es, "p0_thr", [128, 128], F32)
        t_a = _sb(nc, es, "p0_ta", [128, 8192], F32)
        t_b = _sb(nc, es, "p0_tb", [128, 8192], F32)
        what = _sb(nc, es, "p0_what", [128, 8192], BF16)
        dtmp = _sb(nc, es, "p0_dtmp", [128, 8192], BF16)
        acc3 = _sb(nc, es, "p0_acc3", [128, 4], F32)
        red1 = _sb(nc, es, "p0_red1", [128, 4], F32)
        part4 = _sb(nc, es, "p0_part4", [1, 4], F32)
        g8 = _sb(nc, es, "p0_g8", [1, 4, 8], F32)
        ones_r = _sb(nc, es, "p0_ones", [1, 128], F32)
        ones_c = _sb(nc, es, "p0_onec", [128, 1], F32)

        def quant_l0(src_ap, dst_ap, width, src_3d=False):
            ngrp = width // 64
            w = [("qb", k.val("qb"))]
            dst_sb = wsl[:, :width]
            if src_3d:
                dst_sb = wsl[:, :width].rearrange("p (j d) -> p j d", d=DM)
            k.emit("gpsimd", lambda e, _s=src_ap, _d=dst_sb: e.dma_start(out=_d, in_=_s),
                   waits=w, inc="gld", inc_amt=16)
            ldv = k.val("gld")
            W = wsl[:, :width]
            WB = wbf[:, :width]
            k.emit("vector", lambda e: e.tensor_copy(WB, W), waits=[("gld", ldv)])
            gv = wbf[:, :width].rearrange("p (g q) -> p g q", q=64)
            SC = scl[:, :ngrp]
            SCB = sclb[:, :ngrp]
            TH = thr[:, :ngrp]
            k.emit("vector", lambda e: e.tensor_reduce(SC, gv, AXX, ALU.add,
                                                       apply_absolute_value=True),
                   inc="qa")
            k.emit("vector", lambda e: e.tensor_scalar_mul(SCB, SC, 1.0 / 64.0),
                   waits=[("qa", k.val("qa"))], inc="qa")
            k.emit("vector", lambda e: e.tensor_scalar_max(SCB, SCB, 1e-08),
                   waits=[("qa", k.val("qa"))], inc="qa")
            k.emit("vector", lambda e: e.tensor_scalar_mul(TH, SCB, BF_HALF),
                   waits=[("qa", k.val("qa"))], inc="qa")
            A = t_a[:, :width]
            B = t_b[:, :width]
            WH = what[:, :width]
            DT = dtmp[:, :width]
            thb = TH.rearrange("p (g o) -> p g o", o=1).broadcast_to([128, ngrp, 64])
            scb = SCB.rearrange("p (g o) -> p g o", o=1).broadcast_to([128, ngrp, 64])
            wbg = WB.rearrange("p (g q) -> p g q", q=64)
            Ag = A.rearrange("p (g q) -> p g q", q=64)
            Bg = B.rearrange("p (g q) -> p g q", q=64)
            WHg = WH.rearrange("p (g q) -> p g q", q=64)
            k.emit("vector", lambda e: e.tensor_tensor(Ag, wbg, thb, ALU.is_gt),
                   waits=[("qa", k.val("qa"))], inc="qa")
            k.emit("vector", lambda e: e.tensor_scalar_mul(thr[:, :ngrp], thr[:, :ngrp], -1.0),
                   waits=[("qa", k.val("qa"))], inc="qa")
            k.emit("vector", lambda e: e.tensor_tensor(Bg, wbg, thb, ALU.is_lt),
                   waits=[("qa", k.val("qa"))], inc="qa")
            k.emit("vector", lambda e: e.tensor_sub(A, A, B))
            k.emit("vector", lambda e: e.tensor_tensor(WHg, Ag, scb, ALU.mult),
                   waits=[("gst", k.val("gst"))])
            k.emit("vector", lambda e: e.tensor_sub(DT, WH, WB))
            k.emit("vector", lambda e: e.tensor_add(WH, WB, DT), inc="qb")
            qv = k.val("qb")
            WHout = WH if not src_3d else WH.rearrange("p (j d) -> p j d", d=DM)
            k.emit("gpsimd", lambda e, _d=dst_ap, _w=WHout: e.dma_start(out=_d, in_=_w),
                   waits=[("qb", qv)], inc="gst", inc_amt=16)

        gu_dst = dr["q1a_in"].rearrange("(u j p) d -> u p j d", u=4, j=4, p=128)
        for part in range(4):
            quant_l0(DI["w_gu_sl"][part], gu_dst[part], 8192, src_3d=True)
        p0_dst = dr["q1b_in"].rearrange("(u p) h -> u p h", u=2)
        for part in range(2):
            quant_l0(DI["w_p0_sl"][part], p0_dst[part], H0)
        st_q1 = k.val("gst")
        k.emit("gpsimd", lambda e: e.collective_compute(
            "AllGather", ALU.bypass, replica_groups=[list(range(NCORE))],
            ins=[dr["q1a_in"].opt()], outs=[dr["q1a_out"].opt()]),
            waits=[("gst", st_q1)], inc="cc")
        k.emit("gpsimd", lambda e: e.collective_compute(
            "AllGather", ALU.bypass, replica_groups=[list(range(NCORE))],
            ins=[dr["q1b_in"].opt()], outs=[dr["q1b_out"].opt()]),
            inc="cc")
        k.CC1 = k.val("cc")

        k.emit("vector", lambda e: e.memset(acc3[:], 0.0))
        k.emit("vector", lambda e: e.memset(ones_r[:], 1.0))
        k.emit("vector", lambda e: e.memset(ones_c[:], 1.0))

        def accum_abs(src_ap, col, width, src_3d=False):
            dst_sb = wsl[:, :width]
            if src_3d:
                dst_sb = wsl[:, :width].rearrange("p (j d) -> p j d", d=DM)
            k.emit("gpsimd", lambda e, _s=src_ap, _d=dst_sb: e.dma_start(out=_d, in_=_s),
                   waits=[("qa", k.val("qa"))], inc="gld", inc_amt=16)
            ldv = k.val("gld")
            k.emit("vector", lambda e, _w=width: e.tensor_reduce(
                red1[:, 0:1], wsl[:, :_w], AXX, ALU.add, apply_absolute_value=True),
                waits=[("gld", ldv)], inc="qa")
            k.emit("vector", lambda e, _c=col: e.tensor_add(
                acc3[:, _c:_c + 1], acc3[:, _c:_c + 1], red1[:, 0:1]),
                waits=[("qa", k.val("qa"))], inc="qa")

        for part in range(2):
            accum_abs(DI["w_g1_sl"].rearrange("(u v) p d -> u p v d", u=2)[part],
                      0, 4 * DM, src_3d=True)
            accum_abs(DI["w_u1_sl"].rearrange("(u v) p d -> u p v d", u=2)[part],
                      1, 4 * DM, src_3d=True)
        for part in range(2):
            accum_abs(DI["w_d1_sl"][part], 2, I1)
        qv = k.val("qa")
        k.emit("tensor", lambda e: e.matmul(ps_x[0:1, 0:4], ones_c[:],
                                            acc3[:], start=True, stop=True),
               waits=[("qa", qv)], inc="mm")
        mmv = k.val("mm")
        k.emit("vector", lambda e: e.tensor_copy(part4[:], ps_x[0:1, 0:4]),
               waits=[("mm", mmv)], inc="qb")
        qbv = k.val("qb")
        k.emit("gpsimd", lambda e: e.dma_start(out=dr["cc2_in"], in_=part4[:]),
               waits=[("qb", qbv)], inc="gst", inc_amt=16)
        stv = k.val("gst")
        k.emit("gpsimd", lambda e: e.collective_compute(
            "AllGather", ALU.bypass, replica_groups=[list(range(NCORE))],
            ins=[dr["cc2_in"].opt()], outs=[dr["cc2_out"].opt()]),
            waits=[("gst", stv)], inc="cc")
        cc2v = k.val("cc")
        k.emit("gpsimd", lambda e: e.dma_start(
            out=g8[:],
            in_=AP(tensor=dr["cc2_out"].tensor, offset=0, ap=[[32, 1], [1, 4], [4, 8]]),
            allow_slow_non_contiguous=True),
            waits=[("cc", cc2v)], inc="gld", inc_amt=16)
        ldv = k.val("gld")
        k.emit("vector", lambda e: e.tensor_reduce(part4[:], g8[:], AXX, ALU.add),
               waits=[("gld", ldv)], inc="qa")
        k.emit("vector", lambda e: e.tensor_scalar(part4[:], part4[:],
                                                   1.0 / 16777216.0, EPS_Q,
                                                   ALU.mult, ALU.add),
               waits=[("qa", k.val("qa"))], inc="qa")
        qv2 = k.val("qa")
        k.emit("tensor", lambda e: e.matmul(ps_x[:, 0:4], ones_r[0:1, :], part4[0:1, :],
                                            start=True, stop=True),
               waits=[("qa", qv2)], inc="mm")
        mmv2 = k.val("mm")
        k.emit("vector", lambda e: e.tensor_copy(ws_b[:], ps_x[:, 0:4]),
               waits=[("mm", mmv2)], inc="qb")
        k.WS_QB = k.val("qb")

        def quant_l1(src_ap, dst_ap, col, width, src_3d=False):
            dst_sb = wsl[:, :width]
            if src_3d:
                dst_sb = wsl[:, :width].rearrange("p (j d) -> p j d", d=DM)
                dst_ap = dst_ap
            k.emit("gpsimd", lambda e, _s=src_ap, _d=dst_sb: e.dma_start(out=_d, in_=_s),
                   waits=[("qb", k.val("qb"))], inc="gld", inc_amt=16)
            ldv = k.val("gld")
            W = wsl[:, :width]
            A = t_a[:, :width]
            B = t_b[:, :width]
            WH = what[:, :width]
            WHout = WH if not src_3d else what[:, :width].rearrange(
                "p (j d) -> p j d", d=DM)
            k.emit("vector", lambda e, _c=col: e.tensor_scalar_mul(
                red1[:, 1:2], ws_b[:, _c:_c + 1], 0.5),
                waits=[("gld", ldv), ("qb", k.WS_QB)], inc="qa")
            k.emit("vector", lambda e: e.tensor_scalar(A, W, red1[:, 1:2], None, ALU.is_gt),
                   waits=[("qa", k.val("qa"))], inc="qa")
            k.emit("vector", lambda e: e.tensor_scalar_mul(red1[:, 2:3], red1[:, 1:2], -1.0),
                   waits=[("qa", k.val("qa"))], inc="qa")
            k.emit("vector", lambda e: e.tensor_scalar(B, W, red1[:, 2:3], None, ALU.is_lt),
                   waits=[("qa", k.val("qa"))], inc="qa")
            k.emit("vector", lambda e: e.tensor_sub(WH, A, B),
                   waits=[("gst", k.val("gst"))], inc="qb")
            qv_ = k.val("qb")
            k.emit("gpsimd", lambda e, _d=dst_ap, _w=WHout: e.dma_start(out=_d, in_=_w),
                   waits=[("qb", qv_)], inc="gst", inc_amt=16)

        g1_dst = dr["q3a_in"].rearrange("(u v p) d -> u p v d", u=2, v=4, p=128)
        u1_dst = dr["q3b_in"].rearrange("(u v p) d -> u p v d", u=2, v=4, p=128)
        d1_dst = dr["q3c_in"].rearrange("(u p) h -> u p h", u=2)
        for part in range(2):
            quant_l1(DI["w_g1_sl"].rearrange("(u v) p d -> u p v d", u=2)[part],
                     g1_dst[part], 0, 4 * DM, src_3d=True)
        for part in range(2):
            quant_l1(DI["w_u1_sl"].rearrange("(u v) p d -> u p v d", u=2)[part],
                     u1_dst[part], 1, 4 * DM, src_3d=True)
        for part in range(2):
            quant_l1(DI["w_d1_sl"][part], d1_dst[part], 2, I1)
        st_q3 = k.val("gst")
        for a, b in (("q3a_in", "q3a_out"), ("q3b_in", "q3b_out"),
                     ("q3c_in", "q3c_out")):
            k.emit("gpsimd", lambda e, _a=a, _b=b: e.collective_compute(
                "AllGather", ALU.bypass, replica_groups=[list(range(NCORE))],
                ins=[dr[_a].opt()], outs=[dr[_b].opt()]),
                waits=[("gst", st_q3)], inc="cc")
        k.CC3 = k.val("cc")


def _queue_pretranspose(nc, k, dr, es_top):
    RT = 2048
    tsb = [_sb(nc, es_top, f"ptr_t{i}", [128, RT], BF16) for i in range(2)]

    def queue_mat(src, dst, R, Cc, wait_sem, wait_val):
        for rb in range(R // RT):
            for cb in range(Cc // 128):
                idx = rb * (Cc // 128) + cb
                b = idx % 2
                if k.val("wt") >= 32:
                    w = [(wait_sem, wait_val), ("wt", k.val("wt") - 32)]
                else:
                    w = [(wait_sem, wait_val)]
                sap = AP(tensor=src.tensor, offset=rb * RT * Cc + cb * 128,
                         ap=[[Cc, RT], [1, 128]])
                k.add_bg("sync", lambda e, _s=sap, _b=b: e.dma_start(
                    out=tsb[_b][:], in_=_s, transpose=True), waits=w,
                    inc="wt", inc_amt=16)
                wtv = k.val("wt")
                dap = AP(tensor=dst.tensor, offset=cb * 128 * R + rb * RT,
                         ap=[[R, 128], [1, RT]])
                k.add_bg("sync", lambda e, _d=dap, _b=b: e.dma_start(
                    out=_d, in_=tsb[_b][:]), waits=[("wt", wtv)],
                    inc="wt", inc_amt=16)

    queue_mat(dr["q1a_out"], dr["whT_d"], GU, DM, "cc", k.CC1)
    queue_mat(dr["q1b_out"], dr["wp0T_d"], DM, H0, "cc", k.CC1)
    k.WT_L0 = k.val("wt")
    queue_mat(dr["q3a_out"], dr["wg1T_d"], I1, DM, "cc", k.CC3)
    queue_mat(dr["q3b_out"], dr["wu1T_d"], I1, DM, "cc", k.CC3)
    queue_mat(dr["q3c_out"], dr["wd1T_d"], DM, I1, "cc", k.CC3)
    k.WT_L1 = k.val("wt")


def _phaseA(nc, k, DI, dr, es_top, ident_f, tri_sb, taps, st_a0):
    es = ExitStack()
    with es:
        qT = _sb(nc, es, "a_qT", [128, S], F32)
        kT = _sb(nc, es, "a_kT", [128, S], F32)
        vS = _sb(nc, es, "a_vS", [128, 64, 128], F32)
        wq_sb = _sb(nc, es, "a_wq", [128, 16, 256], F32)
        wk_sb = _sb(nc, es, "a_wk", [128, 16, 256], F32)
        wv_sb = _sb(nc, es, "a_wv", [128, 16, 256], F32)
        bq_sb = _sb(nc, es, "a_bq", [128, 2], F32)
        bk_sb = _sb(nc, es, "a_bk", [128, 2], F32)
        bv_sb = _sb(nc, es, "a_bv", [128, 2], F32)
        dum_sb = _sb(nc, es, "a_dum", [128, 2], F32)
        xc = [_sb(nc, es, f"a_xc{i}", [128, 16, 256], F32) for i in range(2)]
        vtmp = _sb(nc, es, "a_vtmp", [128, 256], F32)
        probs = [_sb(nc, es, f"a_pb{i}", [128, 512], F32) for i in range(2)]
        pT = _sb(nc, es, "a_pT", [128, 512], F32)
        oacc = _sb(nc, es, "a_oacc", [128, 128], F32)
        rmax = _sb(nc, es, "a_rmax", [128, 1], F32)
        rsum = _sb(nc, es, "a_rsum", [128, 1], F32)
        tmax = [_sb(nc, es, f"a_tmax{i}", [128, 1], F32) for i in range(2)]
        nmax = [_sb(nc, es, f"a_nmax{i}", [128, 1], F32) for i in range(2)]
        nneg = [_sb(nc, es, f"a_nneg{i}", [128, 1], F32) for i in range(2)]
        facin = [_sb(nc, es, f"a_facin{i}", [128, 1], F32) for i in range(2)]
        fac = [_sb(nc, es, f"a_fac{i}", [128, 1], F32) for i in range(2)]
        tsum = [_sb(nc, es, f"a_tsum{i}", [128, 1], F32) for i in range(2)]
        rcp = _sb(nc, es, "a_rcp", [128, 1], F32)
        ofin = [_sb(nc, es, f"a_ofin{i}", [128, 128], F32) for i in range(2)]
        ps_s = [_psum(nc, es, f"a_pss{i}", [128, 512]) for i in range(2)]
        ps_t = [_psum(nc, es, f"a_pst{i}", [128, 128]) for i in range(2)]
        ps_o = [_psum(nc, es, f"a_pso{i}", [128, 128]) for i in range(2)]
        ps_v = _psum(nc, es, "a_psv", [128, 256])

        for nm, sb in (("wq", wq_sb), ("wk", wk_sb), ("wv", wv_sb),
                       ("bq", bq_sb), ("bk", bk_sb), ("bv", bv_sb)):
            k.emit("sync", lambda e, _n=nm, _s=sb: e.dma_start(out=_s[:], in_=DI[_n]),
                   inc="ld", inc_amt=16)
        w_ld = k.val("ld")
        xT_d = dr["xT_d"]
        ao_in = dr["ao_in"]
        bg_per_qt = 8

        qkv_mm_last = 0
        for h in range(HPC):
            mm_at_tb = []
            ev_at_tb = []
            for tb in range(32):
                b = tb % 2
                if h == 0 and tb < 2:
                    w = [("st", st_a0), ("ld", w_ld)]
                elif tb < 2:
                    w = [("mm", qkv_mm_last)]
                else:
                    w = [("ev", ev_at_tb[tb - 2])]
                src = AP(tensor=xT_d.tensor, offset=tb * 256,
                         ap=[[S, 128], [128 * S, 16], [1, 256]])
                k.emit("sync", lambda e, _b=b, _s=src: e.dma_start(out=xc[_b][:], in_=_s),
                       waits=w, inc="ld", inc_amt=16)
                ldv = k.val("ld") + 16
                for oi in range(3):
                    wsb, bsb = ((wq_sb, bq_sb), (wk_sb, bk_sb), (wv_sb, bv_sb))[oi]
                    pq = (ps_s[0], ps_s[1], ps_v)[oi]
                    for kc in range(16):
                        wts = []
                        if kc == 0:
                            wts = [("ld", ldv)]
                            if tb >= 1 or h > 0:
                                wts.append(("ev", k.val("ev")))
                        k.emit("tensor", lambda e, _w=wsb, _kc=kc, _h=h, _b=b, _p=pq,
                               _sl=pq is ps_v: e.matmul(
                                   (_p[:, 0:256]), _w[:, _kc, 128 * _h:128 * _h + 128],
                                   xc[_b][:, _kc, :], start=(_kc == 0), stop=(_kc == 15)),
                               waits=wts, inc=("mm" if kc == 15 else None))
                    mmv = k.val("mm")
                    if oi < 2:
                        dst = (qT, kT)[oi]
                        scale = QK_SCALE if oi == 0 else 1.0
                        k.emit("vector", lambda e, _d=dst, _tb=tb, _p=pq, _b2=bsb, _h=h,
                               _sc=scale: e.tensor_scalar(
                                   _d[:, _tb * 256:(_tb + 1) * 256], _p[:, 0:256],
                                   _b2[:, _h:_h + 1], _sc, ALU.add, ALU.mult),
                               waits=[("mm", mmv)], inc="ev")
                    else:
                        k.emit("vector", lambda e, _p=pq, _b2=bv_sb, _h=h: e.tensor_scalar_add(
                            vtmp[:], _p[:, 0:256], _b2[:, _h:_h + 1]),
                            waits=[("mm", mmv), ("tp", k.val("tp"))], inc="ev")
                        evv = k.val("ev")
                        for j in range(2):
                            pb = j % 2
                            k.emit("tensor", lambda e, _j=j, _pb=pb: e.transpose(
                                ps_t[_pb][:], vtmp[:, _j * 128:(_j + 1) * 128], ident_f[:]),
                                waits=[("ev", evv), ("ev", k.val("ev"))], inc="tp")
                            tpv = k.val("tp")
                            k.emit("vector", lambda e, _tb=tb, _j=j, _pb=pb: e.tensor_copy(
                                vS[:, _tb * 2 + _j, :], ps_t[_pb][:]),
                                waits=[("tp", tpv)], inc="ev")
                mm_at_tb.append(k.val("mm"))
                ev_at_tb.append(k.val("ev"))
            k.emit("sync", lambda e: e.dma_start(out=dum_sb[:], in_=DI["bq"]),
                   inc="ld", inc_amt=16)
            qkv_mm_last = k.val("mm")
            qkv_ev = k.val("ev")
            if _DBG_QKV and h == 0:
                k.emit("gpsimd", lambda e: e.dma_start(out=dr["dbg_q"], in_=qT[:]),
                       waits=[("ev", qkv_ev)], inc="gst", inc_amt=16)
                k.emit("gpsimd", lambda e: e.dma_start(out=dr["dbg_k"], in_=kT[:]),
                       inc="gst", inc_amt=16)
                k.emit("gpsimd", lambda e: e.dma_start(out=dr["dbg_v"], in_=vS[:]),
                       inc="gst", inc_amt=16)

            for qt in range(64):
                groups = []
                nfull = (qt * 128) // 512
                rem = qt * 128 - nfull * 512
                for i in range(nfull):
                    groups.append((i * 512, 512, False))
                if rem:
                    groups.append((nfull * 512, rem, False))
                groups.append((qt * 128, 128, True))
                ng = len(groups)

                k.emit("vector", lambda e: e.memset(oacc[:], 0.0),
                       waits=([("ev", qkv_ev)] if qt == 0 else []))
                k.emit("vector", lambda e: e.memset(rmax[:], MSK))
                k.emit("vector", lambda e: e.memset(rsum[:], 0.0), inc="ev")

                st_g = {}

                def emit_scores(gi):
                    k0, kn, masked = groups[gi]
                    sb_ = gi % 2
                    if gi < 2:
                        w = [("ev", k.val("ev")), ("ex", k.val("ex"))]
                    else:
                        w = [("ex", st_g[gi - 2]["ex"])]
                    k.emit("tensor", lambda e, _q=qt, _k0=k0, _kn=kn, _s=sb_: e.matmul(
                        ps_s[_s][:, :_kn], qT[:, _q * 128:(_q + 1) * 128],
                        kT[:, _k0:_k0 + _kn], start=True, stop=True),
                        waits=w, inc="mm")
                    st_g[gi] = {"mm": k.val("mm")}

                def emit_rest(gi):
                    k0, kn, masked = groups[gi]
                    sb_ = gi % 2
                    mmv = st_g[gi]["mm"]
                    wst = [("mm", mmv)]
                    if gi >= 2:
                        wst.append(("ex", st_g[gi - 2]["ex"]))
                    if masked:
                        k.emit("vector", lambda e, _s=sb_, _kn=kn: e.tensor_add(
                            ps_s[_s][:, :_kn], ps_s[_s][:, :_kn], tri_sb[:]),
                            waits=wst, inc="ev")
                        mw = [("ev", k.val("ev"))]
                    else:
                        mw = wst
                    k.emit("vector", lambda e, _s=sb_, _kn=kn: e.reduce_max(
                        tmax[_s][:], ps_s[_s][:, :_kn], AXX), waits=mw, inc="ev")
                    k.emit("vector", lambda e, _s=sb_: e.tensor_max(
                        nmax[_s][:], rmax[:], tmax[_s][:]),
                        waits=[("ev", k.val("ev"))], inc="ev")
                    k.emit("vector", lambda e, _s=sb_: e.tensor_sub(
                        facin[_s][:], rmax[:], nmax[_s][:]),
                        waits=[("ev", k.val("ev"))], inc="ev")
                    k.emit("vector", lambda e, _s=sb_: e.tensor_scalar_mul(
                        nneg[_s][:], nmax[_s][:], -1.0),
                        waits=[("ev", k.val("ev"))], inc="ev")
                    k.emit("vector", lambda e, _s=sb_: e.tensor_copy(
                        rmax[:], nmax[_s][:]),
                        waits=[("ev", k.val("ev"))], inc="ev")
                    evv = k.val("ev")
                    wex = [("ev", evv)]
                    if gi >= 2:
                        wex.append(("tp", st_g[gi - 2]["tp"]))
                    k.emit("scalar", lambda e, _s=sb_: e.activation(
                        fac[_s][:], facin[_s][:], AF.Exp, bias=0.0, scale=1.0),
                        waits=wex)
                    k.emit("scalar", lambda e, _s=sb_, _kn=kn: e.activation(
                        probs[_s][:, :_kn], ps_s[_s][:, :_kn], AF.Exp, bias=nneg[_s][:],
                        scale=1.0, accum_out=tsum[_s][:]), inc="ex")
                    exv = k.val("ex")
                    st_g[gi]["ex"] = exv
                    k.emit("vector", lambda e, _s=sb_: e.scalar_tensor_tensor(
                        rsum[:], rsum[:], fac[_s][:], tsum[_s][:], ALU.mult, ALU.add),
                        waits=[("ex", exv)], inc="ev")
                    nch = (kn + 127) // 128
                    for j in range(nch):
                        pb = j % 2
                        w2 = [("ex", exv), ("ev", k.val("ev"))]
                        if j == 0:
                            w2.append(("pv", k.val("pv")))
                        k.emit("tensor", lambda e, _s=sb_, _j=j, _pb=pb: e.transpose(
                            ps_t[_pb][:], probs[_s][:, _j * 128:(_j + 1) * 128], ident_f[:]),
                            waits=w2, inc="tp")
                        tpv = k.val("tp")
                        k.emit("vector", lambda e, _j=j, _pb=pb: e.tensor_copy(
                            pT[:, _j * 128:(_j + 1) * 128], ps_t[_pb][:]),
                            waits=[("tp", tpv)], inc="ev")
                    st_g[gi]["tp"] = k.val("tp")
                    evv2 = k.val("ev")
                    ob = gi % 2
                    for j in range(nch):
                        w3 = []
                        if j == 0:
                            w3 = [("ev", evv2), ("mg", max(0, k.val("mg") - 1))]
                        k.emit("tensor", lambda e, _j=j, _k0=k0, _ob=ob, _n=nch: e.matmul(
                            ps_o[_ob][:, :], pT[:, _j * 128:(_j + 1) * 128],
                            vS[:, _k0 // 128 + _j, :], start=(_j == 0), stop=(_j == _n - 1)),
                            waits=w3, inc=("pv" if j == nch - 1 else None))
                    pvv = k.val("pv")
                    k.emit("vector", lambda e, _ob=ob, _s=sb_: e.scalar_tensor_tensor(
                        oacc[:], oacc[:], fac[_s][:], ps_o[_ob][:, :], ALU.mult, ALU.add),
                        waits=[("pv", pvv)], inc="mg")

                if _NO_PIPE:
                    for gi in range(ng):
                        emit_scores(gi)
                        emit_rest(gi)
                else:
                    emit_scores(0)
                    for gi in range(ng):
                        if gi + 1 < ng:
                            emit_scores(gi + 1)
                        emit_rest(gi)
                mgv = k.val("mg")
                k.emit("vector", lambda e: e.reciprocal(rcp[:], rsum[:]),
                       waits=[("mg", mgv)])
                ob2 = (h * 64 + qt) % 2
                wfin = [("st", max(st_a0, k.val("st") - 16))]
                k.emit("vector", lambda e, _o=ob2: e.tensor_scalar_mul(
                    ofin[_o][:], oacc[:], rcp[:]), waits=wfin, inc="ev")
                evf = k.val("ev")
                k.emit("scalar", lambda e, _q=qt, _h=h, _o=ob2: e.dma_start(
                    out=ao_in[_q * 128:(_q + 1) * 128, _h * 128:(_h + 1) * 128],
                    in_=ofin[_o][:]), waits=[("ev", evf)], inc="st", inc_amt=16)
                if (h == 0 and qt >= 16) or h == 1:
                    k.drain_bg(bg_per_qt)
        k.drain_bg(None)
        return k.val("st")


def _phaseB(nc, k, DI, dr, es_top, ident_f, strm, taps, st_attn):
    es = ExitStack()
    with es:
        aob = _sb(nc, es, "b_aob", [128, NCORE, 256], F32)
        aoT = _sb(nc, es, "b_aoT", [128, 16, T], F32)
        wpt = [_sb(nc, es, f"b_wp{i}", [128, DM], F32) for i in range(2)]
        xow = _sb(nc, es, "b_xow", [128, 4, DM], F32)
        bpr = _sb(nc, es, "b_bpr", [128, DM], F32)
        sx = _sb(nc, es, "b_sx", [128, 1], F32)
        sxx = _sb(nc, es, "b_sxx", [128, 1], F32)
        mu = _sb(nc, es, "b_mu", [128, 1], F32)
        var = _sb(nc, es, "b_var", [128, 1], F32)
        inv = _sb(nc, es, "b_inv", [128, 1], F32)
        t1 = _sb(nc, es, "b_t1", [128, 1], F32)
        epsv = _sb(nc, es, "b_eps", [128, 1], F32)
        b_dum = _sb(nc, es, "b_dum", [128, 2], F32)
        rki = _sb(nc, es, "b_rki", [1, 1], I32)
        ps_t = [_psum(nc, es, f"b_pst{i}", [128, 128]) for i in range(2)]
        ps_p = [_psum(nc, es, f"b_psp{i}", [128, 512]) for i in range(4)]

        k.emit("sync", lambda e: e.dma_start(out=bpr[:], in_=DI["bproj_b"]),
               waits=[("st", st_attn)], inc="ld", inc_amt=16)
        k.emit("sync", lambda e: e.dma_start(out=rki[:], in_=DI["rank_i"]),
               inc="ld", inc_amt=16)
        ld_res = k.val("ld")
        k.emit("vector", lambda e: e.memset(epsv[:], 1e-05))

        if "aoin_dbg" in _DBG_DRAM:
            k.emit("gpsimd", lambda e: e.dma_start(
                out=dr["aoin_dbg"], in_=dr["ao_in"]),
                waits=[("st", st_attn)], inc="gst", inc_amt=16)
        k.emit("gpsimd", lambda e: e.collective_compute(
            "AllGather", ALU.bypass, replica_groups=[list(range(NCORE))],
            ins=[dr["ao_in"].opt()], outs=[dr["ao_out"].opt()]),
            waits=[("st", st_attn)], inc="cc")
        cc4 = k.val("cc")

        ao_t = dr["ao_out"].tensor
        ao_my = dr["ao_my"]

        def ld_slice(e):
            with e.register("r_b0") as r1, e.register("r_o0") as r2:
                e.load(r1, rki[0:1, 0:1])
                e.reg_mul(r1, r1, T * 256)
                for m in range(NCORE):
                    e.reg_add(r2, r1, m * S * 256)
                    e.dma_start(
                        out=ao_my[m],
                        in_=AP(tensor=ao_t, offset=r2, ap=[[256, T], [1, 256]])
                    ).then_inc(k.S("gld"), 16)
        k.cnt["gld"] += 16 * NCORE
        k.emit("gpsimd", lambda e: ld_slice(e), waits=[("cc", cc4), ("ld", ld_res)])
        ld_slc = k.val("gld")

        tp_at_tt = []
        for tt in range(NTT):
            wv = [("gld", ld_slc)]
            if tt >= 1:
                wv.append(("tp", tp_at_tt[tt - 1]))
            first = True
            for m in range(NCORE):
                k.emit("gpsimd", lambda e, _m=m, _tt=tt: e.dma_start(
                    out=aob[:, _m, :],
                    in_=ao_my[_m, _tt * 128:(_tt + 1) * 128, :]),
                    waits=(wv if first else []), inc="gld", inc_amt=16)
                first = False
            ldv = k.val("gld")
            aof = aob[:].rearrange("p m c -> p (m c)")
            for fc in range(16):
                pb = fc % 2
                w = [("gld", ldv)]
                if tt > 0 or fc >= 2:
                    w.append(("ev", k.val("ev") - 1))
                k.emit("tensor", lambda e, _fc=fc, _pb=pb, _a=aof: e.transpose(
                    ps_t[_pb][:], _a[:, _fc * 128:(_fc + 1) * 128], ident_f[:]),
                    waits=w, inc="tp")
                tpv = k.val("tp")
                k.emit("vector", lambda e, _tt=tt, _fc=fc, _pb=pb: e.tensor_copy(
                    aoT[:, _fc, _tt * 128:(_tt + 1) * 128], ps_t[_pb][:]),
                    waits=[("tp", tpv)], inc="ev")
            tp_at_tt.append(k.val("tp"))
        aoT_ev = k.val("ev")

        mm_seq = []
        for half in range(2):
            for i4 in range(4):
                tt = half * 4 + i4
                w0 = []
                if half == 0 and i4 == 0:
                    w0 = [("st", st_attn)]
                elif half == 1 and i4 == 0:
                    w0 = [("ev", k.val("ev"))]
                k.emit("sync", lambda e, _i=i4, _tt=tt: e.dma_start(
                    out=xow[:, _i, :],
                    in_=DI["x_own"][_tt * 128:(_tt + 1) * 128, :]),
                    waits=w0, inc="ld", inc_amt=16)
            xow_ld = k.val("ld")
            for nb in range(4):
                for kc in range(16):
                    b = kc % 2
                    idx = len(mm_seq)
                    w = []
                    if idx == 0:
                        w.append(("st", st_attn))
                    if idx >= 2:
                        w.append(("mm", mm_seq[idx - 2]))
                    k.emit("sync", lambda e, _kc=kc, _b=b: e.dma_start(
                        out=wpt[_b][:], in_=DI["wproj_t"][_kc]), waits=w,
                        inc="lw", inc_amt=16)
                    lwv = k.val("lw") + 16
                    for i4 in range(4):
                        tt = half * 4 + i4
                        w2 = []
                        if i4 == 0:
                            w2 = [("lw", lwv)]
                            if kc == 0:
                                w2.append(("ev", aoT_ev if (nb == 0 and half == 0)
                                           else k.val("ev")))
                        k.emit("tensor", lambda e, _i=i4, _tt=tt, _kc=kc, _nb=nb,
                               _b=b: e.matmul(
                                   ps_p[_i][:, :], aoT[:, _kc, _tt * 128:(_tt + 1) * 128],
                                   wpt[_b][:, _nb * 512:(_nb + 1) * 512],
                                   start=(_kc == 0), stop=(_kc == 15)),
                               waits=w2, inc=("mm" if i4 == 3 else None))
                    mm_seq.append(k.val("mm"))
                if half == 0 and nb == 3:
                    k.emit("sync", lambda e: e.dma_start(
                        out=b_dum[:], in_=DI["bq"]), inc="lw", inc_amt=16)
                mmv = k.val("mm")
                for i4 in range(4):
                    tt = half * 4 + i4
                    if _DBG_PROJONLY == "proj":
                        k.emit("vector", lambda e, _i=i4, _tt=tt, _nb=nb: e.tensor_copy(
                            strm[:, _tt, _nb * 512:(_nb + 1) * 512], ps_p[_i][:, :]),
                            waits=([("mm", mmv), ("ld", xow_ld)] if i4 == 0 else []),
                            inc="ev")
                        continue
                    if _DBG_PROJONLY == "resid":
                        k.emit("vector", lambda e, _i=i4, _tt=tt, _nb=nb: e.tensor_add(
                            strm[:, _tt, _nb * 512:(_nb + 1) * 512], ps_p[_i][:, :],
                            xow[:, _i, _nb * 512:(_nb + 1) * 512]),
                            waits=([("mm", mmv), ("ld", xow_ld)] if i4 == 0 else []),
                            inc="ev")
                        continue
                    if _DBG_PROJONLY == "bias":
                        k.emit("vector", lambda e, _i=i4, _tt=tt, _nb=nb: e.tensor_add(
                            strm[:, _tt, _nb * 512:(_nb + 1) * 512], ps_p[_i][:, :],
                            bpr[:, _nb * 512:(_nb + 1) * 512]),
                            waits=([("mm", mmv), ("ld", xow_ld)] if i4 == 0 else []),
                            inc="ev")
                        continue
                    k.emit("vector", lambda e, _i=i4, _tt=tt, _nb=nb: e.tensor_add(
                        strm[:, _tt, _nb * 512:(_nb + 1) * 512], ps_p[_i][:, :],
                        xow[:, _i, _nb * 512:(_nb + 1) * 512]),
                        waits=([("mm", mmv), ("ld", xow_ld)] if i4 == 0 else []))
                    k.emit("vector", lambda e, _tt=tt, _nb=nb: e.tensor_add(
                        strm[:, _tt, _nb * 512:(_nb + 1) * 512],
                        strm[:, _tt, _nb * 512:(_nb + 1) * 512],
                        bpr[:, _nb * 512:(_nb + 1) * 512]), inc="ev")
        k.emit("sync", lambda e: e.dma_start(out=b_dum[:], in_=DI["bq"]),
               inc="lw", inc_amt=16)
        prj_ev = k.val("ev")

        if _SKIP_LN:
            return
        for tt in range(NTT):
            w = [("ev", prj_ev)] if tt == 0 else [("ev", k.val("ev"))]
            k.emit("scalar", lambda e, _tt=tt: e.activation(
                wpt[0][:], strm[:, _tt, :], AF.Copy, bias=0.0, scale=1.0,
                accum_out=sx[:]), waits=w, inc="ex")
            k.emit("scalar", lambda e, _tt=tt: e.activation(
                wpt[1][:], strm[:, _tt, :], AF.Square, bias=0.0, scale=1.0,
                accum_out=sxx[:]), inc="ex")
            exv = k.val("ex")
            k.emit("vector", lambda e: e.tensor_scalar_mul(mu[:], sx[:], 1.0 / DM),
                   waits=[("ex", exv)], inc="ev")
            k.emit("vector", lambda e: e.tensor_scalar_mul(var[:], sxx[:], 1.0 / DM),
                   inc="ev")
            k.emit("vector", lambda e: e.tensor_mul(t1[:], mu[:], mu[:]),
                   waits=[("ev", k.val("ev"))], inc="ev")
            k.emit("vector", lambda e: e.tensor_sub(var[:], var[:], t1[:]),
                   waits=[("ev", k.val("ev"))], inc="ev")
            evv = k.val("ev")
            k.emit("scalar", lambda e: e.activation(t1[:], var[:], AF.Sqrt,
                                                    bias=epsv[:], scale=1.0),
                   waits=[("ev", evv)], inc="ex")
            exv2 = k.val("ex")
            k.emit("vector", lambda e: e.reciprocal(inv[:], t1[:]),
                   waits=[("ex", exv2)], inc="ev")
            k.emit("vector", lambda e, _tt=tt: e.tensor_scalar(
                strm[:, _tt, :], strm[:, _tt, :], mu[:], inv[:],
                ALU.subtract, ALU.mult), waits=[("ev", k.val("ev"))], inc="ev")
            if "x1_d" in _DBG_DRAM:
                k.emit("scalar", lambda e, _tt=tt: e.dma_start(
                    out=dr["x1_d"][:, _tt, :], in_=strm[:, _tt, :]),
                    waits=[("ev", k.val("ev"))], inc="st", inc_amt=16)



def _phaseC(nc, k, DI, dr, es_top, ident_f, ident_b, strm, taps):
    es = ExitStack()
    with es:
        x1Th = _sb(nc, es, "c_x1th", [128, 16, 128], BF16)
        x1Tl = _sb(nc, es, "c_x1tl", [128, 16, 128], BF16)
        x1hi = _sb(nc, es, "c_x1hi", [128, DM], BF16)
        x1lo = _sb(nc, es, "c_x1lo", [128, DM], BF16)
        wqd = [_sb(nc, es, f"c_wqd{i}", [128, 16, 256], BF16) for i in range(2)]
        wdd = [_sb(nc, es, f"c_wdd{i}", [128, DM], BF16) for i in range(2)]
        g_sb = _sb(nc, es, "c_g", [128, H0], F32)
        h_sb = g_sb
        hnh = _sb(nc, es, "c_hnh", [128, H0], BF16)
        hnl = _sb(nc, es, "c_hnl", [128, H0], BF16)
        hTh = _sb(nc, es, "c_hth", [128, 32, 128], BF16)
        hTl = _sb(nc, es, "c_htl", [128, 32, 128], BF16)
        acc16 = _sb(nc, es, "c_acc16", [128, 32], F32)
        sq_scr = _sb(nc, es, "c_sqscr", [128, 256], F32)
        s1 = _sb(nc, es, "c_s1", [128, 1], F32)
        epsr = _sb(nc, es, "c_epsr", [128, 1], F32)
        sx = _sb(nc, es, "c_sx", [128, 1], F32)
        sxx = _sb(nc, es, "c_sxx", [128, 1], F32)
        mu = _sb(nc, es, "c_mu", [128, 1], F32)
        var = _sb(nc, es, "c_var", [128, 1], F32)
        inv = _sb(nc, es, "c_inv", [128, 1], F32)
        t1 = _sb(nc, es, "c_t1", [128, 1], F32)
        eps5 = _sb(nc, es, "c_eps5", [128, 1], F32)
        c_dum = _sb(nc, es, "c_dum", [128, 2], F32)

        ps_g = [_psum(nc, es, f"c_psg{i}", [128, 512]) for i in range(4)]
        ps_t = [es.enter_context(nc.psum_tensor(f"c_pst{i}", [128, 128], BF16))
                for i in range(2)]
        b_done = k.val("ev")
        whT = dr["whT_d"].tensor
        wp0T = dr["wp0T_d"].tensor
        x2_d = dr["x2_d"]

        k.emit("vector", lambda e: e.memset(epsr[:], EPS_RMS), waits=[("ev", b_done)])
        k.emit("vector", lambda e: e.memset(eps5[:], 1e-05), inc="ev")
        dn_seq = []

        for tt in range(NTT):
            wv = [("ev", k.val("ev"))]
            if tt == 0:
                wv.append(("wt", k.WT_L0))
            k.emit("vector", lambda e, _tt=tt: e.tensor_copy(x1hi[:], strm[:, _tt, :]),
                   waits=wv)
            k.emit("vector", lambda e, _tt=tt: e.tensor_tensor(
                x1lo[:], strm[:, _tt, :], x1hi[:], ALU.subtract), inc="ev")
            sp_ev = k.val("ev")
            for fc in range(16):
                for which, (srcb, dstb) in enumerate(((x1hi, x1Th), (x1lo, x1Tl))):
                    pb = (2 * fc + which) % 2
                    w = [("ev", sp_ev)]
                    if fc > 0 or which > 0 or tt > 0:
                        w.append(("ev", k.val("ev") - 1))
                    k.emit("tensor", lambda e, _fc=fc, _pb=pb, _s=srcb: e.transpose(
                        ps_t[_pb][:], _s[:, _fc * 128:(_fc + 1) * 128], ident_b[:]),
                        waits=w, inc="tp")
                    k.emit("vector", lambda e, _fc=fc, _pb=pb, _d=dstb: e.tensor_copy(
                        _d[:, _fc, :], ps_t[_pb][:]), waits=[("tp", k.val("tp"))],
                        inc="ev")
            xT_ev = k.val("ev")

            for up in range(2):
                for jq in range(32):
                    b = jq % 2
                    pb_ = jq % 4
                    w = []
                    if tt == 0 and up == 0 and jq < 2:
                        w = [("ev", b_done)]
                    else:
                        w = [("mm", k.val("mm") - 1)]
                    src = AP(tensor=whT, offset=up * H0 + jq * 256,
                             ap=[[GU, 128], [128 * GU, 16], [1, 256]])
                    k.emit("sync", lambda e, _b=b, _s=src: e.dma_start(
                        out=wqd[_b][:], in_=_s), waits=w, inc="lw", inc_amt=16)
                    lwv = k.val("lw") + 16
                    for pas in range(2):
                        xop = (x1Th, x1Tl)[pas]
                        for kc in range(16):
                            w2 = []
                            if pas == 0 and kc == 0:
                                w2 = [("lw", lwv), ("ev", xT_ev)]
                                w2.append(("ev", k.val("ev")))
                            k.emit("tensor", lambda e, _x=xop, _kc=kc, _b=b, _p=pb_,
                                   _pas=pas: e.matmul(
                                       ps_g[_p][:, 0:256], _x[:, _kc, :],
                                       wqd[_b][:, _kc, :],
                                       start=(_kc == 0 and _pas == 0),
                                       stop=(_kc == 15 and _pas == 1)),
                                   waits=w2,
                                   inc=("mm" if (kc == 15 and pas == 1) else None))
                    mmv = k.val("mm")
                    if up == 0:
                        k.emit("scalar", lambda e, _jq=jq, _p=pb_: e.activation(
                            g_sb[:, _jq * 256:(_jq + 1) * 256], ps_g[_p][:, 0:256],
                            AF.Silu, bias=0.0, scale=1.0),
                            waits=[("mm", mmv)], inc="ev")
                    else:
                        k.emit("vector", lambda e, _jq=jq, _p=pb_: e.tensor_tensor(
                            h_sb[:, _jq * 256:(_jq + 1) * 256], ps_g[_p][:, 0:256],
                            g_sb[:, _jq * 256:(_jq + 1) * 256], ALU.mult),
                            waits=[("mm", mmv)], inc="ev")
                        k.emit("scalar", lambda e, _jq=jq: e.activation(
                            sq_scr[:],
                            h_sb[:, _jq * 256:(_jq + 1) * 256],
                            AF.Square, bias=0.0, scale=1.0,
                            accum_out=acc16[:, _jq:_jq + 1]),
                            waits=[("ev", k.val("ev"))], inc="ex")
            h_ex = k.val("ex")

            k.emit("vector", lambda e: e.tensor_reduce(s1[:], acc16[:], AXX, ALU.add),
                   waits=[("ex", h_ex)], inc="ev")
            k.emit("vector", lambda e: e.tensor_scalar(
                s1[:], s1[:], 1.0 / H0, epsr[:], ALU.mult, ALU.add),
                waits=[("ev", k.val("ev"))], inc="ev")
            sev = k.val("ev")
            k.emit("scalar", lambda e: e.activation(t1[:], s1[:], AF.Sqrt,
                                                    bias=0.0, scale=1.0),
                   waits=[("ev", sev)], inc="ex")
            k.emit("vector", lambda e: e.reciprocal(inv[:], t1[:]),
                   waits=[("ex", k.val("ex"))], inc="ev")
            k.emit("vector", lambda e: e.tensor_scalar_mul(hnh[:], h_sb[:], inv[:]),
                   waits=[("ev", k.val("ev"))], inc="ev")
            k.emit("vector", lambda e: e.scalar_tensor_tensor(
                hnl[:], h_sb[:], inv[:], hnh[:], ALU.mult, ALU.subtract), inc="ev")
            hn_ev = k.val("ev")
            for hh in range(2):
                for fc in range(32):
                    for which, (srcb, dstb) in enumerate(((hnh, hTh), (hnl, hTl))):
                        pb = (2 * fc + which) % 2
                        w = [("ev", hn_ev), ("ev", k.val("ev") - 1)]
                        if fc == 0 and which == 0 and hh == 1:
                            w.append(("mm", k.val("mm")))
                        k.emit("tensor", lambda e, _fc=fc, _hh=hh, _pb=pb, _s=srcb:
                               e.transpose(ps_t[_pb][:],
                                           _s[:, (_hh * 32 + _fc) * 128:
                                              (_hh * 32 + _fc + 1) * 128],
                                           ident_b[:]),
                               waits=w, inc="tp")
                        k.emit("vector", lambda e, _fc=fc, _pb=pb, _d=dstb:
                               e.tensor_copy(_d[:, _fc, :], ps_t[_pb][:]),
                               waits=[("tp", k.val("tp"))], inc="ev")
                hT_ev = k.val("ev")

                for kci in range(32):
                    kc = hh * 32 + kci
                    b = kc % 2
                    idx = len(dn_seq)
                    w = []
                    if idx >= 2:
                        w.append(("mm", dn_seq[idx - 2]))
                    src = AP(tensor=wp0T, offset=kc * 128 * DM,
                             ap=[[DM, 128], [1, DM]])
                    k.emit("sync", lambda e, _b=b, _s=src: e.dma_start(
                        out=wdd[_b][:], in_=_s), waits=w, inc="lw", inc_amt=16)
                    lwv = k.val("lw") + 16
                    for pas in range(2):
                        hop = (hTh, hTl)[pas]
                        for nb in range(4):
                            w2 = []
                            if pas == 0 and nb == 0:
                                w2 = [("lw", lwv), ("ev", hT_ev)]
                                if kc == 0:
                                    w2.append(("ev", k.val("ev")))
                            k.emit("tensor", lambda e, _h=hop, _kci=kci, _nb=nb, _b=b,
                                   _pas=pas, _kc=kc: e.matmul(
                                       ps_g[_nb][:, :], _h[:, _kci, :],
                                       wdd[_b][:, _nb * 512:(_nb + 1) * 512],
                                       start=(_kc == 0 and _pas == 0),
                                       stop=(_kc == 63 and _pas == 1)),
                                   waits=w2,
                                   inc=("mm" if (pas == 1 and nb == 3) else None))
                    dn_seq.append(k.val("mm"))
            if tt == NTT - 1:
                k.emit("sync", lambda e: e.dma_start(
                    out=c_dum[:], in_=DI["bq"]), inc="lw", inc_amt=16)
            mmv = k.val("mm")
            for nb in range(4):
                k.emit("vector", lambda e, _tt=tt, _nb=nb: e.tensor_add(
                    strm[:, _tt, _nb * 512:(_nb + 1) * 512],
                    strm[:, _tt, _nb * 512:(_nb + 1) * 512], ps_g[_nb][:, :]),
                    waits=([("mm", mmv)] if nb == 0 else []), inc="ev")
            res_ev = k.val("ev")
            k.emit("scalar", lambda e, _tt=tt: e.activation(
                g_sb[:, 0:DM], strm[:, _tt, :], AF.Copy, bias=0.0, scale=1.0,
                accum_out=sx[:]), waits=[("ev", res_ev)], inc="ex")
            k.emit("scalar", lambda e, _tt=tt: e.activation(
                g_sb[:, 0:DM], strm[:, _tt, :], AF.Square, bias=0.0, scale=1.0,
                accum_out=sxx[:]), inc="ex")
            exv = k.val("ex")
            k.emit("vector", lambda e: e.tensor_scalar_mul(mu[:], sx[:], 1.0 / DM),
                   waits=[("ex", exv)], inc="ev")
            k.emit("vector", lambda e: e.tensor_scalar_mul(var[:], sxx[:], 1.0 / DM),
                   inc="ev")
            k.emit("vector", lambda e: e.tensor_mul(t1[:], mu[:], mu[:]),
                   waits=[("ev", k.val("ev"))], inc="ev")
            k.emit("vector", lambda e: e.tensor_sub(var[:], var[:], t1[:]),
                   waits=[("ev", k.val("ev"))], inc="ev")
            k.emit("scalar", lambda e: e.activation(t1[:], var[:], AF.Sqrt,
                                                    bias=eps5[:], scale=1.0),
                   waits=[("ev", k.val("ev"))], inc="ex")
            k.emit("vector", lambda e: e.reciprocal(inv[:], t1[:]),
                   waits=[("ex", k.val("ex"))], inc="ev")
            k.emit("vector", lambda e, _tt=tt: e.tensor_scalar(
                strm[:, _tt, :], strm[:, _tt, :], mu[:], inv[:],
                ALU.subtract, ALU.mult), waits=[("ev", k.val("ev"))], inc="ev")
            x2_ev = k.val("ev")
            k.emit("scalar", lambda e, _tt=tt: e.dma_start(
                out=x2_d[:, _tt, :], in_=strm[:, _tt, :]),
                waits=[("ev", x2_ev)], inc="st", inc_amt=16)
        k.C_ST = k.val("st")
        k.C_EV = k.val("ev")



def _phaseD(nc, k, DI, dr, es_d, ident_f, x3, taps):
    proj_d = dr["proj_d"]
    X_d = dr["X_d"]
    gate_d = dr["gate_d"]
    dtv_d = dr["dtv_d"]
    z_d = dr["z_d"]
    x2_d = dr["x2_d"]
    es = ExitStack()
    with es:
        cc5b = _sb(nc, es, "d_cc5b", [128, NOF, 3], F32)
        halo = _sb(nc, es, "d_halo", [128, NOF, 3], F32)
        s_in = _sb(nc, es, "d_sin", [128, NOF], F32)
        keep8 = _sb(nc, es, "d_keep", [128, NCORE], F32)
        notk8 = _sb(nc, es, "d_notk", [128, NCORE], F32)
        A_sb = _sb(nc, es, "d_A", [128, NOF], F32)
        alog = _sb(nc, es, "d_alog", [128, NOF], F32)
        hmask = _sb(nc, es, "d_hm", [128, 1], F32)
        pvi = _sb(nc, es, "d_pvi", [1, 1], I32)
        ps_p = [_psum(nc, es, f"d_psp{i}", [128, 512]) for i in range(2)]
        ps_d = _psum(nc, es, "d_psd", [128, T])
        ps_t = [_psum(nc, es, f"d_pst{i}", [128, 128]) for i in range(2)]

        c_st = k.C_ST
        c_ev = k.C_EV

        es1 = ExitStack()
        with es1:
            x2T = _sb(nc, es1, "d_x2T", [128, 16, T], F32)
            xa = [_sb(nc, es1, f"d_xa{i}", [128, DM], F32) for i in range(2)]
            win = [_sb(nc, es1, f"d_win{i}", [128, 16, 128], F32) for i in range(2)]
            ptile = _sb(nc, es1, "d_ptile", [128, T], F32)
            gtile = _sb(nc, es1, "d_gtile", [128, T], F32)

            tp_at = []
            for tt in range(NTT):
                b = tt % 2
                w = [("st", c_st), ("ev", c_ev)] if tt < 2 else \
                    [("tp", tp_at[tt - 2])]
                k.emit("sync", lambda e, _b=b, _t=tt: e.dma_start(
                    out=xa[_b][:], in_=x2_d[:, _t, :]), waits=w,
                    inc="ld", inc_amt=16)
                ldv = k.val("ld")
                for fc in range(16):
                    pb = fc % 2
                    w2 = [("ld", ldv)]
                    if tt > 0 or fc >= 2:
                        w2.append(("ev", k.val("ev") - 1))
                    k.emit("tensor", lambda e, _b=b, _fc=fc, _pb=pb: e.transpose(
                        ps_t[_pb][:], xa[_b][:, _fc * 128:(_fc + 1) * 128],
                        ident_f[:]), waits=w2, inc="tp")
                    k.emit("vector", lambda e, _t=tt, _fc=fc, _pb=pb: e.tensor_copy(
                        x2T[:, _fc, _t * 128:(_t + 1) * 128], ps_t[_pb][:]),
                        waits=[("tp", k.val("tp"))], inc="ev")
                tp_at.append(k.val("tp"))
            x2T_ev = k.val("ev")

            prg_seq = []
            for of in range(NOF):
                for which in range(2):
                    wsrc = (DI["w_in_t"], DI["w_gt_t"])[which]
                    idx = len(prg_seq)
                    w = [] if idx < 2 else [("mm", prg_seq[idx - 2])]
                    k.emit("sync", lambda e, _b=which, _s=wsrc, _of=of: e.dma_start(
                        out=win[_b][:], in_=_s[_of]), waits=w, inc="lw", inc_amt=16)
                    lwv = k.val("lw")
                    for kc in range(16):
                        for nb in range(2):
                            w2 = []
                            if kc == 0 and nb == 0:
                                w2 = [("lw", lwv), ("ev", x2T_ev)]
                                w2 += [("ev", k.val("ev")), ("ex", k.val("ex"))]
                            k.emit("tensor", lambda e, _b=which, _kc=kc, _nb=nb:
                                   e.matmul(ps_p[_nb][:, :], win[_b][:, _kc, :],
                                            x2T[:, _kc, _nb * 512:(_nb + 1) * 512],
                                            start=(_kc == 0), stop=(_kc == 15)),
                                   waits=w2,
                                   inc=("mm" if (kc == 15 and nb == 1) else None))
                    prg_seq.append(k.val("mm"))
                    mmv = k.val("mm")
                    if which == 0:
                        for nb in range(2):
                            k.emit("vector", lambda e, _nb=nb: e.tensor_copy(
                                ptile[:, _nb * 512:(_nb + 1) * 512], ps_p[_nb][:, :]),
                                waits=([("mm", mmv)] if nb == 0 else []), inc="ev")
                        k.emit("vector", lambda e, _of=of: e.tensor_copy(
                            cc5b[:, _of, :], ptile[:, T - 3:T]),
                            waits=[("ev", k.val("ev"))], inc="ev")
                        pev = k.val("ev")
                        k.emit("scalar", lambda e, _of=of: e.dma_start(
                            out=proj_d[:, _of, :], in_=ptile[:]),
                            waits=[("ev", pev)], inc="st", inc_amt=16)
                    else:
                        for nb in range(2):
                            k.emit("scalar", lambda e, _nb=nb: e.activation(
                                gtile[:, _nb * 512:(_nb + 1) * 512], ps_p[_nb][:, :],
                                AF.Sigmoid, bias=0.0, scale=1.0),
                                waits=([("mm", mmv)] if nb == 0 else []), inc="ex")
                        gev = k.val("ex")
                        k.emit("scalar", lambda e, _of=of: e.dma_start(
                            out=gate_d[:, _of, :], in_=gtile[:]),
                            waits=[("ex", gev)], inc="st", inc_amt=16)
            d1_st = k.val("st")
            d1_ev = k.val("ev")
            d1_ex = k.val("ex")
            k.emit("gpsimd", lambda e: e.dma_start(
                out=dr["cc5_in"].rearrange("o (p c) -> (o p) c", p=128),
                in_=cc5b[:].rearrange("p o c -> p (o c)")),
                waits=[("ev", d1_ev)], inc="gst", inc_amt=16)
            k.emit("gpsimd", lambda e: e.collective_compute(
                "AllGather", ALU.bypass, replica_groups=[list(range(NCORE))],
                ins=[dr["cc5_in"].opt()], outs=[dr["cc5_out"].opt()]),
                waits=[("gst", k.val("gst"))], inc="cc")
            cc5v = k.val("cc")

        es2 = ExitStack()
        with es2:
            ext = _sb(nc, es2, "d_ext", [128, T + 3], F32)
            Xt = _sb(nc, es2, "d_Xt", [128, T], F32)
            acc_a = _sb(nc, es2, "d_acca", [128, T], F32)
            acc_b = _sb(nc, es2, "d_accb", [128, T], F32)
            mx = _sb(nc, es2, "d_mx", [128, T], F32)
            ab = _sb(nc, es2, "d_ab", [128, T], F32)
            en = _sb(nc, es2, "d_en", [128, T], F32)
            dtv = _sb(nc, es2, "d_dtv", [128, T], F32)
            logd = _sb(nc, es2, "d_logd", [128, T], F32)
            decay = _sb(nc, es2, "d_decay", [128, T], F32)
            drive = _sb(nc, es2, "d_drive", [128, T], F32)
            scr = _sb(nc, es2, "d_scr", [128, T], F32)
            cw = _sb(nc, es2, "d_cw", [128, NOF, 4], F32)
            cb = _sb(nc, es2, "d_cb", [128, NOF, 1], F32)
            dtb = _sb(nc, es2, "d_dtb", [128, NOF, 1], F32)
            bp_sb = _sb(nc, es2, "d_bp", [128, NOF, 1], F32)
            cp_sb = _sb(nc, es2, "d_cp", [128, NOF, 1], F32)
            dp_sb = _sb(nc, es2, "d_dp", [128, NOF, 1], F32)
            dtw = [_sb(nc, es2, f"d_dtw{i}", [128, 128], F32) for i in range(2)]
            cc6b = _sb(nc, es2, "d_cc6b", [128, NOF, 2], F32)
            g6 = _sb(nc, es2, "d_g6", [128, 2 * NOF], F32)
            emf = _sb(nc, es2, "d_emf", [128, NOF], F32)
            tscr = _sb(nc, es2, "d_tscr", [128, NOF], F32)

            cst0 = [("st", d1_st), ("ev", d1_ev), ("ex", d1_ex),
                    ("gst", k.val("gst"))]
            for i, (nm, sb) in enumerate((
                    ("conv_w", cw), ("conv_b", cb), ("dtb_pt", dtb),
                    ("bp_pt", bp_sb), ("cp_pt", cp_sb), ("dp_pt", dp_sb),
                    ("halo_m", hmask), ("prev_i", pvi))):
                k.emit("sync", lambda e, _n=nm, _s=sb: e.dma_start(
                    out=_s[:], in_=DI[_n]), waits=(cst0 if i == 0 else ()),
                    inc="ld", inc_amt=16)
            k.emit("sync", lambda e: e.dma_start(
                out=alog[:], in_=DI["alog_pt"].rearrange("p o c -> p (o c)")),
                inc="ld", inc_amt=16)
            k.emit("sync", lambda e: e.dma_start(
                out=keep8[:], in_=DI["keep_b"]), inc="ld", inc_amt=16)
            cst_ld = k.val("ld")
            k.emit("scalar", lambda e: e.activation(A_sb[:], alog[:], AF.Exp,
                                                    bias=0.0, scale=1.0),
                   waits=[("ld", cst_ld)], inc="ex")
            k.emit("vector", lambda e: e.tensor_scalar_mul(A_sb[:], A_sb[:], -1.0),
                   waits=[("ex", k.val("ex"))], inc="ev")
            k.emit("vector", lambda e: e.tensor_scalar(
                notk8[:], keep8[:], -1.0, 1.0, ALU.mult, ALU.add),
                waits=[("ld", cst_ld)], inc="ev")

            def ld_halo(e):
                e.wait_ge(k.S("cc"), cc5v)
                e.wait_ge(k.S("ld"), cst_ld)
                with e.register("d_hr1") as r1:
                    e.load(r1, pvi[0:1, 0:1])
                    e.reg_mul(r1, r1, NOF * 128 * 3)
                    e.dma_start(
                        out=halo[:].rearrange("p o c -> p (o c)"),
                        in_=AP(tensor=dr["cc5_out"].tensor, offset=r1,
                               ap=[[NOF * 3, 128], [1, NOF * 3]])
                    ).then_inc(k.S("gld"), 16)
            k.cnt["gld"] += 16
            k.emit("gpsimd", lambda e: ld_halo(e))
            hl_gld = k.val("gld")
            k.emit("vector", lambda e: e.tensor_scalar_mul(
                halo[:].rearrange("p o c -> p (o c)"),
                halo[:].rearrange("p o c -> p (o c)"), hmask[:]),
                waits=[("gld", hl_gld), ("ev", k.val("ev"))], inc="ev")
            halo_ev = k.val("ev")

            def dt_chain(of, x_ex):
                mmv = k.val("mm")
                k.emit("vector", lambda e, _of=of: e.tensor_scalar(
                    ab[:], ps_d[:], dtb[:, _of, :], 0.0, ALU.add, ALU.max),
                    waits=[("mm", mmv)], inc="ev")
                k.emit("vector", lambda e, _of=of: e.tensor_scalar_add(
                    mx[:], ps_d[:], dtb[:, _of, :]), inc="ev")
                vv = k.val("ev")
                k.emit("scalar", lambda e: e.activation(en[:], mx[:], AF.Abs,
                                                        bias=0.0, scale=1.0),
                       waits=[("ev", vv)], inc="ex")
                k.emit("scalar", lambda e: e.activation(en[:], en[:], AF.Exp,
                                                        bias=0.0, scale=-1.0),
                       waits=[("ex", k.val("ex"))], inc="ex")
                k.emit("vector", lambda e: e.tensor_scalar_add(en[:], en[:], 1.0),
                       waits=[("ex", k.val("ex"))], inc="ev")
                k.emit("scalar", lambda e: e.activation(en[:], en[:], AF.Ln,
                                                        bias=0.0, scale=1.0),
                       waits=[("ev", k.val("ev"))], inc="ex")
                k.emit("vector", lambda e: e.tensor_add(dtv[:], ab[:], en[:]),
                       waits=[("ex", k.val("ex"))], inc="ev")

            dt_seq = []
            for of in range(NOF):
                b = of % 2
                w = [("ev", halo_ev)] if of == 0 else [("ev", k.val("ev"))]
                k.emit("sync", lambda e, _of=of: e.dma_start(
                    out=ext[:, 3:3 + T], in_=proj_d[:, _of, :]),
                    waits=w + ([("st", d1_st)] if of == 0 else []),
                    inc="ld", inc_amt=16)
                ldv = k.val("ld")
                k.emit("vector", lambda e, _of=of: e.tensor_copy(
                    ext[:, 0:3], halo[:, _of, :]), waits=[("ld", ldv)], inc="ev")
                k.emit("vector", lambda e, _of=of: e.tensor_scalar(
                    acc_a[:], ext[:, 0:T], cw[:, _of, 0:1], cb[:, _of, :],
                    ALU.mult, ALU.add), waits=[("ev", k.val("ev"))], inc="ev")
                k.emit("vector", lambda e, _of=of: e.scalar_tensor_tensor(
                    acc_b[:], ext[:, 1:1 + T], cw[:, _of, 1:2], acc_a[:],
                    ALU.mult, ALU.add), inc="ev")
                k.emit("vector", lambda e, _of=of: e.scalar_tensor_tensor(
                    acc_a[:], ext[:, 2:2 + T], cw[:, _of, 2:3], acc_b[:],
                    ALU.mult, ALU.add), inc="ev")
                k.emit("vector", lambda e, _of=of: e.scalar_tensor_tensor(
                    acc_b[:], ext[:, 3:3 + T], cw[:, _of, 3:4], acc_a[:],
                    ALU.mult, ALU.add), inc="ev")
                cv_ev = k.val("ev")
                k.emit("scalar", lambda e: e.activation(Xt[:], acc_b[:], AF.Silu,
                                                        bias=0.0, scale=1.0),
                       waits=[("ev", cv_ev)], inc="ex")
                x_ex = k.val("ex")
                k.emit("scalar", lambda e, _of=of: e.dma_start(
                    out=X_d[:, _of, :], in_=Xt[:]), waits=[("ex", x_ex)],
                    inc="st", inc_amt=16)
                idx = len(dt_seq)
                w3 = [] if idx < 2 else [("mm", dt_seq[idx - 2])]
                k.emit("sync", lambda e, _b=b, _of=of: e.dma_start(
                    out=dtw[_b][:], in_=DI["dtw_blk"][_of]), waits=w3,
                    inc="lw", inc_amt=16)
                lwv = k.val("lw")
                for nb in range(2):
                    w4 = []
                    if nb == 0:
                        w4 = [("lw", lwv), ("ex", x_ex), ("ev", k.val("ev"))]
                    k.emit("tensor", lambda e, _b=b, _nb=nb: e.matmul(
                        ps_d[:, _nb * 512:(_nb + 1) * 512], dtw[_b][:],
                        Xt[:, _nb * 512:(_nb + 1) * 512],
                        start=True, stop=True),
                        waits=w4, inc=("mm" if nb == 1 else None))
                dt_seq.append(k.val("mm"))
                dt_chain(of, x_ex)
                dt_ev = k.val("ev")
                k.emit("scalar", lambda e, _of=of: e.dma_start(
                    out=dtv_d[:, _of, :], in_=dtv[:]), waits=[("ev", dt_ev)],
                    inc="st", inc_amt=16)
                k.emit("vector", lambda e, _of=of: e.tensor_scalar_mul(
                    logd[:], dtv[:], A_sb[:, _of:_of + 1]),
                    waits=[("ev", dt_ev)], inc="ev")
                lg_ev = k.val("ev")
                k.emit("scalar", lambda e: e.activation(decay[:], logd[:], AF.Exp,
                                                        bias=0.0, scale=1.0),
                       waits=[("ev", lg_ev)], inc="ex")
                k.emit("vector", lambda e, _of=of: e.scalar_tensor_tensor(
                    drive[:], dtv[:], bp_sb[:, _of, :], Xt[:], ALU.mult, ALU.mult),
                    inc="ev")
                k.emit("vector", lambda e: e.tensor_tensor_scan(
                    scr[:], decay[:], drive[:], 0.0, ALU.mult, ALU.add),
                    waits=[("ex", k.val("ex")), ("ev", k.val("ev"))], inc="ev")
                k.emit("vector", lambda e, _of=of: e.tensor_copy(
                    cc6b[:, _of, 0:1], scr[:, T - 1:T]),
                    waits=[("ev", k.val("ev"))], inc="ev")
                k.emit("vector", lambda e, _of=of: e.tensor_reduce(
                    cc6b[:, _of, 1:2], logd[:], AXX, ALU.add),
                    waits=[("ev", k.val("ev"))], inc="ev")
            d3_ev = k.val("ev")
            d3_st = k.val("st")

            k.emit("gpsimd", lambda e: e.dma_start(
                out=dr["cc6_in"].rearrange("o (p c) -> (o p) c", p=128),
                in_=cc6b[:].rearrange("p o c -> p (o c)")),
                waits=[("ev", d3_ev)], inc="gst", inc_amt=16)
            k.emit("gpsimd", lambda e: e.collective_compute(
                "AllGather", ALU.bypass, replica_groups=[list(range(NCORE))],
                ins=[dr["cc6_in"].opt()], outs=[dr["cc6_out"].opt()]),
                waits=[("gst", k.val("gst"))], inc="cc")
            cc6v = k.val("cc")
            k.emit("vector", lambda e: e.memset(s_in[:], 0.0), inc="ev")
            for m in range(NCORE):
                w = [("cc", cc6v)] if m == 0 else [("ev", k.val("ev"))]
                k.emit("gpsimd", lambda e, _m=m: e.dma_start(
                    out=g6[:], in_=AP(tensor=dr["cc6_out"].tensor,
                                      offset=_m * NOF * 128 * 2,
                                      ap=[[2 * NOF, 128], [1, 2 * NOF]])),
                    waits=w, inc="gld", inc_amt=16)
                gv = k.val("gld")
                g6v = g6[:].rearrange("p (o c) -> p o c", c=2)
                k.emit("scalar", lambda e, _g=g6v: e.activation(
                    emf[:], _g[:, :, 1], AF.Exp, bias=0.0, scale=1.0),
                    waits=[("gld", gv), ("ev", k.val("ev"))], inc="ex")
                k.emit("vector", lambda e, _m=m: e.tensor_scalar(
                    emf[:], emf[:], keep8[:, _m:_m + 1], notk8[:, _m:_m + 1],
                    ALU.mult, ALU.add), waits=[("ex", k.val("ex"))], inc="ev")
                k.emit("vector", lambda e: e.tensor_mul(s_in[:], s_in[:], emf[:]),
                       waits=[("ev", k.val("ev"))], inc="ev")
                k.emit("vector", lambda e, _m=m, _g=g6v: e.tensor_scalar_mul(
                    tscr[:], _g[:, :, 0], keep8[:, _m:_m + 1]),
                    waits=[("ev", k.val("ev"))], inc="ev")
                k.emit("vector", lambda e: e.tensor_add(s_in[:], s_in[:], tscr[:]),
                       waits=[("ev", k.val("ev"))], inc="ev")
            sin_ev = k.val("ev")

            for of in range(NOF):
                w = ([("ev", sin_ev), ("st", d3_st)] if of == 0
                     else [("ev", k.val("ev"))])
                k.emit("sync", lambda e, _of=of: e.dma_start(
                    out=Xt[:], in_=X_d[:, _of, :]), waits=w, inc="ld", inc_amt=16)
                k.emit("sync", lambda e, _of=of: e.dma_start(
                    out=ext[:, 0:T], in_=gate_d[:, _of, :]), inc="ld", inc_amt=16)
                k.emit("sync", lambda e, _of=of: e.dma_start(
                    out=dtv[:], in_=dtv_d[:, _of, :]), inc="ld", inc_amt=16)
                ldv = k.val("ld")
                k.emit("vector", lambda e, _of=of: e.tensor_scalar_mul(
                    logd[:], dtv[:], A_sb[:, _of:_of + 1]),
                    waits=[("ld", ldv)], inc="ev")
                lg_ev = k.val("ev")
                k.emit("scalar", lambda e: e.activation(decay[:], logd[:], AF.Exp,
                                                        bias=0.0, scale=1.0),
                       waits=[("ev", lg_ev)], inc="ex")
                k.emit("vector", lambda e, _of=of: e.scalar_tensor_tensor(
                    drive[:], dtv[:], bp_sb[:, _of, :], Xt[:], ALU.mult, ALU.mult),
                    inc="ev")
                k.emit("vector", lambda e, _of=of: e.tensor_tensor_scan(
                    scr[:], decay[:], drive[:], s_in[:, _of:_of + 1],
                    ALU.mult, ALU.add),
                    waits=[("ex", k.val("ex")), ("ev", k.val("ev"))], inc="ev")
                k.emit("vector", lambda e, _of=of: e.tensor_scalar_mul(
                    drive[:], Xt[:], dp_sb[:, _of, :]),
                    waits=[("ev", k.val("ev"))], inc="ev")
                k.emit("vector", lambda e, _of=of: e.scalar_tensor_tensor(
                    scr[:], scr[:], cp_sb[:, _of, :], drive[:], ALU.mult, ALU.add),
                    waits=[("ev", k.val("ev"))], inc="ev")
                k.emit("vector", lambda e, _of=of: e.tensor_mul(
                    scr[:], scr[:], ext[:, 0:T]),
                    waits=[("ev", k.val("ev"))], inc="ev")
                zev = k.val("ev")
                k.emit("scalar", lambda e, _of=of: e.dma_start(
                    out=z_d[:, _of, :], in_=scr[:]), waits=[("ev", zev)],
                    inc="st", inc_amt=16)
            d5_st = k.val("st")

        es3 = ExitStack()
        with es3:
            zh = _sb(nc, es3, "d_zh", [128, NOF, 512], F32)
            wod = [_sb(nc, es3, f"d_wod{i}", [128, NOF, 128], F32)
                   for i in range(2)]
            ot = _sb(nc, es3, "d_ot", [128, 512], F32)
            ln_scr = _sb(nc, es3, "d_lnscr", [128, DM], F32)
            sx = _sb(nc, es3, "d_sx", [128, 1], F32)
            sxx = _sb(nc, es3, "d_sxx", [128, 1], F32)
            mu = _sb(nc, es3, "d_mu", [128, 1], F32)
            var = _sb(nc, es3, "d_var", [128, 1], F32)
            inv = _sb(nc, es3, "d_inv", [128, 1], F32)
            t1 = _sb(nc, es3, "d_t1", [128, 1], F32)
            eps5 = _sb(nc, es3, "d_eps5", [128, 1], F32)

            for tt in range(NTT):
                k.emit("sync", lambda e, _t=tt: e.dma_start(
                    out=x3[:, _t, :], in_=x2_d[:, _t, :]),
                    waits=([("st", c_st), ("ev", d3_ev)] if tt == 0 else ()),
                    inc="ld", inc_amt=16)
            x3_ldv = k.val("ld")
            k.emit("vector", lambda e: e.memset(eps5[:], 1e-05))

            wseq = []
            evac_at = []
            first_add = True
            for th in range(2):
                w = [("st", d5_st)] if th == 0 else [("ev", k.val("ev"))]
                zsrc = AP(tensor=z_d.tensor, offset=th * 512,
                          ap=[[NOF * T, 128], [T, NOF], [1, 512]])
                k.emit("sync", lambda e, _s=zsrc: e.dma_start(out=zh[:], in_=_s),
                       waits=w, inc="ld", inc_amt=16)
                zh_ldv = k.val("ld")
                for dm in range(16):
                    pb = dm % 2
                    idx = len(wseq)
                    w2 = [("st", d5_st)] if idx < 2 else [("mm", wseq[idx - 2])]
                    k.emit("sync", lambda e, _b=pb, _dm=dm: e.dma_start(
                        out=wod[_b][:], in_=DI["w_out_t"][_dm].rearrange(
                            "o p q -> p o q")), waits=w2, inc="lw", inc_amt=16)
                    lwv = k.val("lw")
                    for of in range(NOF):
                        w3 = []
                        if of == 0:
                            w3 = [("lw", lwv), ("ld", zh_ldv)]
                            if len(evac_at) >= 2:
                                w3.append(("ev", evac_at[-2]))
                        k.emit("tensor", lambda e, _b=pb, _of=of: e.matmul(
                            ps_p[_b][:, :],
                            wod[_b][:, _of, :],
                            zh[:, _of, :],
                            start=(_of == 0), stop=(_of == NOF - 1)),
                            waits=w3, inc=("mm" if of == NOF - 1 else None))
                    wseq.append(k.val("mm"))
                    mmv = k.val("mm")
                    wt = [("mm", mmv)]
                    if evac_at:
                        wt.append(("tp", k.val("tp")))
                    k.emit("vector", lambda e, _b=pb: e.tensor_copy(
                        ot[:], ps_p[_b][:, :]), waits=wt, inc="ev")
                    evac_at.append(k.val("ev"))
                    oev = k.val("ev")
                    for i4 in range(4):
                        tt = th * 4 + i4
                        tb = i4 % 2
                        w4 = [("ev", oev)]
                        if not (th == 0 and dm == 0 and i4 < 2):
                            w4.append(("ev", k.val("ev") - 1))
                        k.emit("tensor", lambda e, _i=i4, _tb=tb: e.transpose(
                            ps_t[_tb][:], ot[:, _i * 128:(_i + 1) * 128],
                            ident_f[:]), waits=w4, inc="tp")
                        w5 = [("tp", k.val("tp"))]
                        if first_add:
                            w5.append(("ld", x3_ldv))
                            first_add = False
                        k.emit("vector", lambda e, _t=tt, _dm=dm, _tb=tb:
                               e.tensor_add(
                                   x3[:, _t, _dm * 128:(_dm + 1) * 128],
                                   x3[:, _t, _dm * 128:(_dm + 1) * 128],
                                   ps_t[_tb][:]), waits=w5, inc="ev")
            add_ev = k.val("ev")

            for tt in range(NTT):
                w = [("ev", add_ev)] if tt == 0 else []
                k.emit("scalar", lambda e, _tt=tt: e.activation(
                    ln_scr[:], x3[:, _tt, :], AF.Copy, bias=0.0, scale=1.0,
                    accum_out=sx[:]), waits=w + [("ex", k.val("ex"))], inc="ex")
                k.emit("scalar", lambda e, _tt=tt: e.activation(
                    ln_scr[:], x3[:, _tt, :], AF.Square, bias=0.0, scale=1.0,
                    accum_out=sxx[:]), inc="ex")
                exv = k.val("ex")
                k.emit("vector", lambda e: e.tensor_scalar_mul(mu[:], sx[:], 1.0 / DM),
                       waits=[("ex", exv)], inc="ev")
                k.emit("vector", lambda e: e.tensor_scalar_mul(var[:], sxx[:], 1.0 / DM),
                       inc="ev")
                k.emit("vector", lambda e: e.tensor_mul(t1[:], mu[:], mu[:]),
                       waits=[("ev", k.val("ev"))], inc="ev")
                k.emit("vector", lambda e: e.tensor_sub(var[:], var[:], t1[:]),
                       waits=[("ev", k.val("ev"))], inc="ev")
                k.emit("scalar", lambda e: e.activation(t1[:], var[:], AF.Sqrt,
                                                        bias=eps5[:], scale=1.0),
                       waits=[("ev", k.val("ev"))], inc="ex")
                k.emit("vector", lambda e: e.reciprocal(inv[:], t1[:]),
                       waits=[("ex", k.val("ex"))], inc="ev")
                k.emit("vector", lambda e, _tt=tt: e.tensor_scalar(
                    x3[:, _tt, :], x3[:, _tt, :], mu[:], inv[:],
                    ALU.subtract, ALU.mult), waits=[("ev", k.val("ev"))], inc="ev")
        k.D_EV = k.val("ev")


def _phaseE(nc, k, DI, dr, es_de, ident_b, ident_f, ws_b, x3, y_out, taps):
    es = ExitStack()
    with es:
        xqb = _sb(nc, es, "e_xqb", [128, DM], BF16)
        xqT = _sb(nc, es, "e_xqT", [128, 16, 128], BF16)
        g_sb = _sb(nc, es, "e_g", [128, I1], F32)
        hqb = _sb(nc, es, "e_hqb", [128, I1], BF16)
        hqT = _sb(nc, es, "e_hqT", [128, 32, 128], BF16)
        wge = [_sb(nc, es, f"e_wge{i}", [128, 16, 256], BF16) for i in range(2)]
        wde = [_sb(nc, es, f"e_wde{i}", [128, DM], BF16) for i in range(2)]
        vq = _sb(nc, es, "e_vq", [128, DM], F32)
        cq = _sb(nc, es, "e_cq", [128, DM], I32)
        tq = _sb(nc, es, "e_tq", [128, DM], F32)
        uq = _sb(nc, es, "e_uq", [128, DM], F32)
        amax = _sb(nc, es, "e_amax", [128, 1], F32)
        sq = _sb(nc, es, "e_sq", [128, 1], F32)
        sc_g = _sb(nc, es, "e_scg", [128, 1], F32)
        sc_u = _sb(nc, es, "e_scu", [128, 1], F32)
        sc_d = _sb(nc, es, "e_scd", [128, 1], F32)
        sx = _sb(nc, es, "e_sx", [128, 1], F32)
        sxx = _sb(nc, es, "e_sxx", [128, 1], F32)
        mu = _sb(nc, es, "e_mu", [128, 1], F32)
        var = _sb(nc, es, "e_var", [128, 1], F32)
        inv = _sb(nc, es, "e_inv", [128, 1], F32)
        t1 = _sb(nc, es, "e_t1", [128, 1], F32)
        eps5 = _sb(nc, es, "e_eps5", [128, 1], F32)
        ln_scr = _sb(nc, es, "e_lnscr", [128, DM], F32)
        ps_e = [_psum(nc, es, f"e_ps{i}", [128, 512]) for i in range(4)]
        ps_tb = [es.enter_context(nc.psum_tensor(f"e_pstb{i}", [128, 128], BF16))
                 for i in range(2)]

        d_ev = k.D_EV
        k.emit("vector", lambda e: e.memset(eps5[:], 1e-05), waits=[("ev", d_ev)])

        def quant(src3, W, dst, rsc):
            k.emit("vector", lambda e: e.tensor_reduce(
                amax[:], src3, AXX, ALU.max, apply_absolute_value=True),
                waits=[("ex", k.val("ex"))], inc="ev")
            k.emit("vector", lambda e, _r=rsc: e.tensor_scalar(
                _r[:], amax[:], 1.0 / 127.0, EPS_Q, ALU.mult, ALU.add),
                waits=[("ev", k.val("ev"))], inc="ev")
            k.emit("vector", lambda e, _r=rsc: e.reciprocal(sq[:], _r[:]),
                   waits=[("ev", k.val("ev"))], inc="ev")
            rq_ev = k.val("ev")
            for ch in range(W // DM):
                s_ap = src3[:, ch * DM:(ch + 1) * DM]
                d_ap = dst[:, ch * DM:(ch + 1) * DM]
                k.emit("vector", lambda e, _s=s_ap: e.tensor_scalar_mul(
                    vq[:], _s, sq[:]), waits=[("ev", rq_ev)], inc="ev")
                k.emit("vector", lambda e: e.tensor_copy(cq[:], vq[:]), inc="ev")
                k.emit("vector", lambda e: e.tensor_copy(tq[:], cq[:]),
                       waits=[("ev", k.val("ev"))], inc="ev")
                k.emit("vector", lambda e: e.tensor_tensor(
                    uq[:], vq[:], tq[:], ALU.is_lt),
                    waits=[("ev", k.val("ev"))], inc="ev")
                k.emit("vector", lambda e: e.tensor_sub(tq[:], tq[:], uq[:]),
                       inc="ev")
                k.emit("vector", lambda e: e.tensor_sub(vq[:], vq[:], tq[:]),
                       inc="ev")
                k.emit("vector", lambda e: e.tensor_scalar(
                    uq[:], vq[:], 0.5, None, ALU.is_gt), inc="ev")
                k.emit("vector", lambda e: e.tensor_scalar(
                    vq[:], vq[:], 0.5, None, ALU.is_equal), inc="ev")
                k.emit("vector", lambda e: e.tensor_scalar_mul(ln_scr[:], tq[:], 0.5),
                       inc="ev")
                k.emit("vector", lambda e: e.tensor_copy(cq[:], ln_scr[:]), inc="ev")
                k.emit("vector", lambda e: e.tensor_copy(ln_scr[:], cq[:]),
                       waits=[("ev", k.val("ev"))], inc="ev")
                k.emit("vector", lambda e: e.scalar_tensor_tensor(
                    ln_scr[:], tq[:], 0.5, ln_scr[:], ALU.mult, ALU.subtract),
                    inc="ev")
                k.emit("vector", lambda e: e.tensor_mul(ln_scr[:], ln_scr[:], ln_scr[:]),
                       inc="ev")
                k.emit("vector", lambda e: e.tensor_scalar(
                    ln_scr[:], ln_scr[:], 0.1, None, ALU.is_gt), inc="ev")
                k.emit("vector", lambda e: e.tensor_mul(vq[:], vq[:], ln_scr[:]),
                       inc="ev")
                k.emit("vector", lambda e: e.tensor_add(tq[:], tq[:], uq[:]),
                       inc="ev")
                k.emit("vector", lambda e: e.tensor_add(tq[:], tq[:], vq[:]),
                       inc="ev")
                k.emit("vector", lambda e: e.tensor_scalar_min(tq[:], tq[:], 127.0),
                       inc="ev")
                k.emit("vector", lambda e: e.tensor_scalar_max(tq[:], tq[:], -128.0),
                       inc="ev")
                k.emit("vector", lambda e, _d=d_ap: e.tensor_copy(_d, tq[:]),
                       waits=[("ev", k.val("ev"))], inc="ev")

        gu_seq = []
        dn_seq = []
        gu_mm_prev = 0
        dn_mm_prev = 0
        ev_bank = {}
        for tt in range(NTT):
            x3t = x3[:, tt, :]
            quant(x3t, DM, xqb, sc_g)
            k.emit("vector", lambda e: e.tensor_scalar_mul(
                sc_u[:], sc_g[:], ws_b[:, 1:2]),
                waits=[("ev", k.val("ev"))], inc="ev")
            k.emit("vector", lambda e: e.tensor_scalar_mul(
                sc_g[:], sc_g[:], ws_b[:, 0:1]),
                waits=[("ev", k.val("ev"))], inc="ev")
            xq_ev = k.val("ev")
            for fc in range(16):
                pb = fc % 2
                w = [("ev", xq_ev), ("ev", k.val("ev") - 1)]
                if fc == 0 and (tt > 0):
                    w.append(("mm", gu_mm_prev))
                k.emit("tensor", lambda e, _fc=fc, _pb=pb: e.transpose(
                    ps_tb[_pb][:], xqb[:, _fc * 128:(_fc + 1) * 128], ident_b[:]),
                    waits=w, inc="tp")
                k.emit("vector", lambda e, _fc=fc, _pb=pb: e.tensor_copy(
                    xqT[:, _fc, :], ps_tb[_pb][:]), waits=[("tp", k.val("tp"))],
                    inc="ev")
            xqT_ev = k.val("ev")

            for jb in range(32):
                for which, wt_d in ((0, dr["wg1T_d"]), (1, dr["wu1T_d"])):
                    b = which
                    idx = len(gu_seq)
                    w = []
                    if idx >= 2:
                        w = [("mm", gu_seq[idx - 2])]
                    elif tt == 0:
                        w = [("ev", d_ev)]
                    src = AP(tensor=wt_d.tensor, offset=jb * 256,
                             ap=[[I1, 128], [128 * I1, 16], [1, 256]])
                    k.emit("sync", lambda e, _b=b, _s=src: e.dma_start(
                        out=wge[_b][:], in_=_s), waits=w, inc="lw", inc_amt=16)
                    lwv = k.val("lw") + 16
                    bank = idx % 4
                    for kc in range(16):
                        w2 = []
                        if kc == 0:
                            w2 = [("lw", lwv), ("ev", xqT_ev)]
                            if bank in ev_bank:
                                w2.append(ev_bank[bank])
                        k.emit("tensor", lambda e, _b=b, _kc=kc, _bk=bank:
                               e.matmul(ps_e[_bk][:, 0:256], xqT[:, _kc, :],
                                        wge[_b][:, _kc, :],
                                        start=(_kc == 0), stop=(_kc == 15)),
                               waits=w2, inc=("mm" if kc == 15 else None))
                    gu_seq.append(k.val("mm"))
                    mmv = k.val("mm")
                    if which == 0:
                        k.emit("scalar", lambda e, _jb=jb, _bk=bank: e.activation(
                            g_sb[:, _jb * 256:(_jb + 1) * 256], ps_e[_bk][:, 0:256],
                            AF.Silu, bias=0.0, scale=sc_g[:]),
                            waits=[("mm", mmv)], inc="ex")
                        ev_bank[bank] = ("ex", k.val("ex"))
                        g_ex = k.val("ex")
                    else:
                        k.emit("vector", lambda e, _jb=jb, _bk=bank:
                               e.scalar_tensor_tensor(
                                   g_sb[:, _jb * 256:(_jb + 1) * 256],
                                   ps_e[_bk][:, 0:256], sc_u[:],
                                   g_sb[:, _jb * 256:(_jb + 1) * 256],
                                   ALU.mult, ALU.mult),
                               waits=[("mm", mmv), ("ex", g_ex)], inc="ev")
                        ev_bank[bank] = ("ev", k.val("ev"))
            gu_mm_prev = k.val("mm")

            quant(g_sb[:, :], I1, hqb, sc_d)
            k.emit("vector", lambda e: e.tensor_scalar_mul(
                sc_d[:], sc_d[:], ws_b[:, 2:3]),
                waits=[("ev", k.val("ev"))], inc="ev")
            hq_ev = k.val("ev")
            for hh in range(2):
                for fc in range(32):
                    pb = fc % 2
                    w = [("ev", hq_ev), ("ev", k.val("ev") - 1)]
                    if fc == 0:
                        w.append(("mm", k.val("mm") if hh == 1 else dn_mm_prev))
                    k.emit("tensor", lambda e, _fc=fc, _hh=hh, _pb=pb: e.transpose(
                        ps_tb[_pb][:],
                        hqb[:, (_hh * 32 + _fc) * 128:(_hh * 32 + _fc + 1) * 128],
                        ident_b[:]), waits=w, inc="tp")
                    k.emit("vector", lambda e, _fc=fc, _pb=pb: e.tensor_copy(
                        hqT[:, _fc, :], ps_tb[_pb][:]), waits=[("tp", k.val("tp"))],
                        inc="ev")
                hqT_ev = k.val("ev")
                for kci in range(32):
                    kc = hh * 32 + kci
                    b = kc % 2
                    idx = len(dn_seq)
                    w = []
                    if idx >= 2:
                        w = [("mm", dn_seq[idx - 2])]
                    src = AP(tensor=dr["wd1T_d"].tensor, offset=kc * 128 * DM,
                             ap=[[DM, 128], [1, DM]])
                    k.emit("sync", lambda e, _b=b, _s=src: e.dma_start(
                        out=wde[_b][:], in_=_s), waits=w, inc="lw", inc_amt=16)
                    lwv = k.val("lw") + 16
                    for nb in range(4):
                        w2 = []
                        if nb == 0:
                            w2 = [("lw", lwv), ("ev", hqT_ev)]
                            if kc == 0:
                                w2 += [("ev", k.val("ev")), ("ex", k.val("ex"))]
                        k.emit("tensor", lambda e, _b=b, _kci=kci, _kc=kc, _nb=nb:
                               e.matmul(
                                   ps_e[_nb][:, :], hqT[:, _kci, :],
                                   wde[_b][:, _nb * 512:(_nb + 1) * 512],
                                   start=(_kc == 0), stop=(_kc == 63)),
                               waits=w2, inc=("mm" if nb == 3 else None))
                    dn_seq.append(k.val("mm"))
            dn_mm_prev = k.val("mm")
            if tt == NTT - 1:
                k.emit("sync", lambda e: e.dma_start(
                    out=ln_scr[:, 0:2], in_=DI["bq"]),
                    waits=[("mm", dn_seq[-2])], inc="lw", inc_amt=16)
            mmv = k.val("mm")
            for nb in range(4):
                k.emit("vector", lambda e, _tt=tt, _nb=nb: e.scalar_tensor_tensor(
                    x3[:, _tt, _nb * 512:(_nb + 1) * 512], ps_e[_nb][:, :],
                    sc_d[:], x3[:, _tt, _nb * 512:(_nb + 1) * 512],
                    ALU.mult, ALU.add),
                    waits=([("mm", mmv)] if nb == 0 else []), inc="ev")
            ev_bank.clear()
            res_ev = k.val("ev")
            k.emit("scalar", lambda e, _tt=tt: e.activation(
                ln_scr[:], x3[:, _tt, :], AF.Copy, bias=0.0, scale=1.0,
                accum_out=sx[:]), waits=[("ev", res_ev), ("ex", k.val("ex"))],
                inc="ex")
            k.emit("scalar", lambda e, _tt=tt: e.activation(
                ln_scr[:], x3[:, _tt, :], AF.Square, bias=0.0, scale=1.0,
                accum_out=sxx[:]), inc="ex")
            exv = k.val("ex")
            k.emit("vector", lambda e: e.tensor_scalar_mul(mu[:], sx[:], 1.0 / DM),
                   waits=[("ex", exv)], inc="ev")
            k.emit("vector", lambda e: e.tensor_scalar_mul(var[:], sxx[:], 1.0 / DM),
                   inc="ev")
            k.emit("vector", lambda e: e.tensor_mul(t1[:], mu[:], mu[:]),
                   waits=[("ev", k.val("ev"))], inc="ev")
            k.emit("vector", lambda e: e.tensor_sub(var[:], var[:], t1[:]),
                   waits=[("ev", k.val("ev"))], inc="ev")
            k.emit("scalar", lambda e: e.activation(t1[:], var[:], AF.Sqrt,
                                                    bias=eps5[:], scale=1.0),
                   waits=[("ev", k.val("ev"))], inc="ex")
            k.emit("vector", lambda e: e.reciprocal(inv[:], t1[:]),
                   waits=[("ex", k.val("ex"))], inc="ev")
            k.emit("vector", lambda e, _tt=tt: e.tensor_scalar(
                x3[:, _tt, :], x3[:, _tt, :], mu[:], inv[:],
                ALU.subtract, ALU.mult), waits=[("ev", k.val("ev"))], inc="ev")
            y_ev = k.val("ev")
            k.emit("gpsimd", lambda e, _tt=tt: e.dma_start(
                out=y_out[_tt * 128:(_tt + 1) * 128, :], in_=x3[:, _tt, :]),
                waits=[("ev", y_ev)], inc="so", inc_amt=16)


def _in_maps(inputs):
    f32 = np.float32
    x = np.ascontiguousarray(np.asarray(inputs["L0_attn_x"], dtype=f32)[0])
    Wqkv = np.asarray(inputs["L0_attn_Wqkv"], dtype=f32)
    bqkv = np.asarray(inputs["L0_attn_bqkv"], dtype=f32)
    Wproj = np.asarray(inputs["L0_attn_Wproj"], dtype=f32)
    bproj = np.asarray(inputs["L0_attn_bproj"], dtype=f32)
    wgu = np.asarray(inputs["L0_ffn_w_gate_up"], dtype=f32)
    wp0 = np.asarray(inputs["L0_ffn_w_proj"], dtype=f32)
    win = np.asarray(inputs["L1_attn_in_proj_w"], dtype=f32)
    wgt = np.asarray(inputs["L1_attn_gate_proj_w"], dtype=f32)
    wout = np.asarray(inputs["L1_attn_out_proj_w"], dtype=f32)
    convw = np.asarray(inputs["L1_attn_conv_w"], dtype=f32)
    convb = np.asarray(inputs["L1_attn_conv_b"], dtype=f32)
    dtw = np.asarray(inputs["L1_attn_dt_w"], dtype=f32)
    dtb = np.asarray(inputs["L1_attn_dt_b"], dtype=f32)
    alog = np.asarray(inputs["L1_attn_A_log"], dtype=f32)
    Bp = np.asarray(inputs["L1_attn_Bp"], dtype=f32)
    Cp = np.asarray(inputs["L1_attn_Cp"], dtype=f32)
    Dp = np.asarray(inputs["L1_attn_Dp"], dtype=f32)
    wg1 = np.asarray(inputs["L1_ffn_w_gate"], dtype=f32)
    wu1 = np.asarray(inputs["L1_ffn_w_up"], dtype=f32)
    wd1 = np.asarray(inputs["L1_ffn_w_down"], dtype=f32)

    C = np.ascontiguousarray
    tri = np.where(np.arange(128)[None, :] <= np.arange(128)[:, None],
                   np.zeros((128, 128), f32), np.full((128, 128), MSK, f32))
    idf = np.eye(128, dtype=f32)
    idb = np.eye(128, dtype=f32).astype(bfloat16)

    wproj_t = C(Wproj.reshape(16, 128, DM))
    bproj_b = C(np.broadcast_to(bproj, (128, DM)))
    w_in_t = C(win.reshape(NOF, 128, 16, 128).transpose(0, 3, 2, 1))
    w_gt_t = C(wgt.reshape(NOF, 128, 16, 128).transpose(0, 3, 2, 1))
    w_out_t = C(wout.reshape(16, 128, NOF, 128).transpose(0, 2, 3, 1))
    conv_w_pt = C(convw.reshape(NOF, 128, 4).transpose(1, 0, 2))
    conv_b_pt = C(convb.reshape(NOF, 128, 1).transpose(1, 0, 2))
    dtb_pt = C(np.repeat(dtb, 64).reshape(NOF, 128, 1).transpose(1, 0, 2))
    alog_pt = C(alog.reshape(NOF, 128, 1).transpose(1, 0, 2))
    bp_pt = C(Bp.reshape(NOF, 128, 1).transpose(1, 0, 2))
    cp_pt = C(Cp.reshape(NOF, 128, 1).transpose(1, 0, 2))
    dp_pt = C(Dp.reshape(NOF, 128, 1).transpose(1, 0, 2))
    dtw_blk = np.zeros((NOF, 128, 128), f32)
    for ct in range(NOF):
        for j in range(2):
            h = 2 * ct + j
            dtw_blk[ct, 64 * j:64 * j + 64, 64 * j:64 * j + 64] = dtw[h][:, None]
    dtw_blk = C(dtw_blk)

    ims = []
    for c in range(NCORE):
        t0 = c * T
        d = dict(
            x_full=x,
            x_own=C(x[t0:t0 + T]),
            wq=C(Wqkv[:, 256 * c:256 * c + 256].reshape(16, 128, 256).transpose(1, 0, 2)),
            wk=C(Wqkv[:, DM + 256 * c:DM + 256 * c + 256].reshape(16, 128, 256).transpose(1, 0, 2)),
            wv=C(Wqkv[:, 2 * DM + 256 * c:2 * DM + 256 * c + 256].reshape(16, 128, 256).transpose(1, 0, 2)),
            bq=C(bqkv[256 * c:256 * c + 256].reshape(2, 128).T),
            bk=C(bqkv[DM + 256 * c:DM + 256 * c + 256].reshape(2, 128).T),
            bv=C(bqkv[2 * DM + 256 * c:2 * DM + 256 * c + 256].reshape(2, 128).T),
            wproj_t=wproj_t,
            bproj_b=bproj_b,
            w_gu_sl=C(wgu[2048 * c:2048 * c + 2048].reshape(4, 4, 128, DM).transpose(0, 2, 1, 3)),
            w_p0_sl=C(wp0[256 * c:256 * c + 256].reshape(2, 128, H0)),
            w_in_t=w_in_t,
            w_gt_t=w_gt_t,
            w_out_t=w_out_t,
            conv_w=conv_w_pt,
            conv_b=conv_b_pt,
            dtw_blk=dtw_blk,
            dtb_pt=dtb_pt,
            alog_pt=alog_pt,
            bp_pt=bp_pt,
            cp_pt=cp_pt,
            dp_pt=dp_pt,
            w_g1_sl=C(wg1[T * c:T * c + T].reshape(8, 128, DM)),
            w_u1_sl=C(wu1[T * c:T * c + T].reshape(8, 128, DM)),
            w_d1_sl=C(wd1[256 * c:256 * c + 256].reshape(2, 128, I1)),
            tri=tri,
            idf=idf,
            idb=idb,
            rank_i=np.array([[c]], dtype=np.int32),
            prev_i=np.array([[(c + NCORE - 1) % NCORE]], dtype=np.int32),
            keep_b=C(np.broadcast_to((np.arange(NCORE) < c).astype(f32)[None, :],
                                     (128, NCORE))),
            halo_m=np.full((128, 1), 1.0 if c > 0 else 0.0, f32),
        )
        ims.append(d)
    return ims


def kernel(**inputs):
    ims = _in_maps(inputs)
    nc = build_nc()
    res = run_bass_kernel_spmd(nc, ims, core_ids=list(range(NCORE)))
    y = np.concatenate([np.asarray(res.results[c]["y"]) for c in range(NCORE)],
                       axis=0)
    return y.reshape(1, S, DM).astype(np.float32)


# version 06
# speedup vs baseline: 1.1162x; 1.1162x over previous
import numpy as np
import ml_dtypes
from contextlib import ExitStack

import concourse.bass as bass
import concourse.mybir as mybir
from concourse.ap import AP
from concourse.bass_utils import run_bass_kernel_spmd

F32 = mybir.dt.float32
BF16 = mybir.dt.bfloat16
I32 = mybir.dt.int32
AF = mybir.ActivationFunctionType
ALU = mybir.AluOpType
AXX = mybir.AxisListType.X
bfloat16 = ml_dtypes.bfloat16

S = 8192
DM = 2048
NCORE = 8
T = 1024
NTT = 8
HPC = 2
DH = 128
GU = 16384
H0 = 8192
INNER = 4096
NOF = 32
I1 = 8192
MSK = -1.0e38
EPS_Q = 1.0e-05
EPS_RMS = float(np.float32(1.1920929e-07))
BF_HALF = 0.5009765625
QK_SCALE = float(np.float32(1.0) / np.float32(np.sqrt(np.float32(128.0))))

_DEBUG_TAPS = ()
_DBG_DRAM = ()
_DBG_Y = None
_SKIP_PH0 = False
_SKIP_BG = False
_NO_PIPE = False
_DBG_QKV = False
_SKIP_LN = False
_DBG_PROJONLY = False
_BUILD_UPTO = "E"


class KB:
    ENGS = ("tensor", "vector", "scalar", "sync", "gpsimd")

    def __init__(self, nc, es):
        self.nc = nc
        self.es = es
        self.ops = {e: [] for e in self.ENGS}
        self.cnt = {}
        self._names = {}
        self.bg = []

    def sem(self, name):
        s = self.es.enter_context(self.nc.semaphore(name))
        self.cnt[name] = 0
        self._names[name] = s
        return s

    def S(self, name):
        return self._names[name]

    def bump(self, name, amt=1):
        self.cnt[name] += amt
        return self.cnt[name]

    def val(self, name):
        return self.cnt[name]

    def emit(self, eng, build, waits=(), inc=None, inc_amt=1):
        waits = tuple((self._names[w], v) for (w, v) in waits)
        tgt = None
        if inc is not None:
            self.bump(inc, inc_amt)
            tgt = self._names[inc]

        def f(e, _b=build, _w=waits, _t=tgt, _a=inc_amt):
            for (s, v) in _w:
                e.wait_ge(s, v)
            ins = _b(e)
            if _t is not None:
                ins.then_inc(_t, _a)

        self.ops[eng].append(f)

    def wait(self, eng, sem_name, value):
        s = self._names[sem_name]
        self.ops[eng].append(lambda e, _s=s, _v=value: e.wait_ge(_s, _v))

    def add_bg(self, eng, build, waits=(), inc=None, inc_amt=1):
        waits = tuple((self._names[w], v) for (w, v) in waits)
        tgt = None
        if inc is not None:
            self.bump(inc, inc_amt)
            tgt = self._names[inc]

        def f(e, _b=build, _w=waits, _t=tgt, _a=inc_amt):
            for (s, v) in _w:
                e.wait_ge(s, v)
            ins = _b(e)
            if _t is not None:
                ins.then_inc(_t, _a)

        self.bg.append((eng, f))

    def drain_bg(self, n=None):
        take = self.bg if n is None else self.bg[:n]
        self.bg = [] if n is None else self.bg[n:]
        for eng, f in take:
            self.ops[eng].append(f)


def _sb(nc, es, name, shape, dt):
    return es.enter_context(nc.sbuf_tensor(name, shape, dt))


def _psum(nc, es, name, shape):
    return es.enter_context(nc.psum_tensor(name, shape, F32))


def build_nc():
    nc = bass.Bass("TRN2", target_bir_lowering=False, debug=False,
                   detect_race_conditions=False)
    es = ExitStack()
    k = KB(nc, es)

    DI = {}

    def din(name, shape, dt=F32):
        DI[name] = nc.dram_tensor(name, shape, dt, kind="ExternalInput").ap()

    din("x_full", [S, DM])
    din("x_own", [T, DM])
    din("wq", [128, 16, 256])
    din("wk", [128, 16, 256])
    din("wv", [128, 16, 256])
    din("bq", [128, 2])
    din("bk", [128, 2])
    din("bv", [128, 2])
    din("wproj_t", [16, 128, DM])
    din("bproj_b", [128, DM])
    din("w_gu_sl", [4, 128, 4, DM])
    din("w_p0_sl", [2, 128, H0])
    din("w_in_t", [NOF, 128, 16, 128])
    din("w_gt_t", [NOF, 128, 16, 128])
    din("w_out_t", [16, NOF, 128, 128])
    din("conv_w", [128, NOF, 4])
    din("conv_b", [128, NOF, 1])
    din("dtw_blk", [NOF, 128, 128])
    din("dtb_pt", [128, NOF, 1])
    din("alog_pt", [128, NOF, 1])
    din("bp_pt", [128, NOF, 1])
    din("cp_pt", [128, NOF, 1])
    din("dp_pt", [128, NOF, 1])
    din("w_g1_sl", [8, 128, DM])
    din("w_u1_sl", [8, 128, DM])
    din("w_d1_sl", [2, 128, I1])
    din("tri", [128, 128])
    din("idf", [128, 128])
    din("idb", [128, 128], BF16)
    din("rank_i", [1, 1], I32)
    din("prev_i", [1, 1], I32)
    din("keep_b", [128, NCORE])
    din("halo_m", [128, 1])

    y_out = nc.dram_tensor("y", [T, DM], F32, kind="ExternalOutput").ap()
    taps = {}
    for tname, tshape in _DEBUG_TAPS:
        taps[tname] = nc.dram_tensor("tap_" + tname, tshape, F32,
                                     kind="ExternalOutput").ap()

    dr = {}

    def dint(name, shape, dt=F32):
        if name in _DBG_DRAM:
            dr[name] = nc.dram_tensor(name, shape, dt, kind="ExternalOutput").ap()
        else:
            dr[name] = nc.dram_tensor(name, shape, dt).ap()

    dint("xT_d", [DM, S])
    dint("ao_in", [S, 256])
    dint("ao_out", [NCORE * S, 256])
    dint("q1a_in", [DM, DM], BF16)
    dint("q1a_out", [GU, DM], BF16)
    dint("q1b_in", [256, H0], BF16)
    dint("q1b_out", [DM, H0], BF16)
    dint("whT_d", [DM, GU], BF16)
    dint("wp0T_d", [H0, DM], BF16)
    dint("cc2_in", [1, 4])
    dint("cc2_out", [NCORE, 4])
    dint("q3a_in", [T, DM], BF16)
    dint("q3a_out", [I1, DM], BF16)
    dint("q3b_in", [T, DM], BF16)
    dint("q3b_out", [I1, DM], BF16)
    dint("q3c_in", [256, I1], BF16)
    dint("q3c_out", [DM, I1], BF16)
    dint("wg1T_d", [DM, I1], BF16)
    dint("wu1T_d", [DM, I1], BF16)
    dint("wd1T_d", [I1, DM], BF16)
    dint("cc5_in", [1, NOF * 128 * 3])
    dint("cc5_out", [NCORE, NOF * 128 * 3])
    dint("cc6_in", [1, NOF * 128 * 2])
    dint("cc6_out", [NCORE, NOF * 128 * 2])
    dint("ao_my", [NCORE, T, 256])
    if _DBG_QKV:
        dr["dbg_q"] = nc.dram_tensor("dbg_q", [128, S], F32, kind="ExternalOutput").ap()
        dr["dbg_k"] = nc.dram_tensor("dbg_k", [128, S], F32, kind="ExternalOutput").ap()
        dr["dbg_v"] = nc.dram_tensor("dbg_v", [128, 64, 128], F32, kind="ExternalOutput").ap()
    dint("x2_d", [128, NTT, DM])
    if "x1_d" in _DBG_DRAM:
        dint("x1_d", [128, NTT, DM])
    if "aoin_dbg" in _DBG_DRAM:
        dint("aoin_dbg", [S, 256])
    dint("gate_d", [128, NOF, T])
    dint("proj_d", [128, NOF, T])
    dint("X_d", [128, NOF, T])
    dint("dtv_d", [128, NOF, T])
    dint("z_d", [128, NOF, T])

    for s in ("ld", "lw", "mm", "ev", "ex", "tp", "pv", "mg", "st", "cc",
              "qa", "qb", "qc", "wt", "so", "gd", "gld", "gst"):
        k.sem(s)

    with es:
        ident_f = _sb(nc, es, "ident_f", [128, 128], F32)
        tri_sb = _sb(nc, es, "tri_sb", [128, 128], F32)
        ident_b = _sb(nc, es, "ident_b", [128, 128], BF16)
        k.emit("sync", lambda e: e.dma_start(out=ident_f[:], in_=DI["idf"]),
               inc="ld", inc_amt=16)
        k.emit("sync", lambda e: e.dma_start(out=tri_sb[:], in_=DI["tri"]),
               inc="ld", inc_amt=16)
        k.emit("sync", lambda e: e.dma_start(out=ident_b[:], in_=DI["idb"]),
               inc="ld", inc_amt=16)
        consts_ld = k.val("ld")
        ws_b = _sb(nc, es, "ws_b", [128, 4], F32)
        ps_x = _psum(nc, es, "ps_x", [128, 16])

        if not _SKIP_PH0:
            _phase0(nc, k, DI, dr, es, ws_b, ps_x)
        p0_gate = [("qb", k.val("qb")), ("gst", k.val("gst"))]
        st_a0 = _phaseA0(nc, k, DI, dr, es, ident_f, consts_ld, p0_gate)
        if not (_SKIP_PH0 or _SKIP_BG):
            _queue_pretranspose(nc, k, dr, es)
        st_attn = _phaseA(nc, k, DI, dr, es, ident_f, tri_sb, taps, st_a0)
        if _BUILD_UPTO == "A":
            _finish_dump(nc, k, y_out, None, dr)
        else:
            es_bc = ExitStack()
            with es_bc:
                strm = _sb(nc, es_bc, "strm", [128, NTT, DM], F32)
                _phaseB(nc, k, DI, dr, es_bc, ident_f, strm, taps, st_attn)
                if _BUILD_UPTO == "B":
                    _finish_dump(nc, k, y_out, strm, dr)
                    done = True
                else:
                    _phaseC(nc, k, DI, dr, es_bc, ident_f, ident_b, strm, taps)
                    done = False
            if not done and _BUILD_UPTO == "C":
                _finish_dump(nc, k, y_out, None, dr)
            elif not done:
                es_de = ExitStack()
                with es_de:
                    x3 = _sb(nc, es_de, "x3buf", [128, NTT, DM], F32)
                    _phaseD(nc, k, DI, dr, es_de, ident_f, x3, taps)
                    if _BUILD_UPTO == "D":
                        _finish_dump(nc, k, y_out, x3, dr)
                    else:
                        _phaseE(nc, k, DI, dr, es_de, ident_b, ident_f, ws_b, x3, y_out, taps)
                        k.wait("gpsimd", "so", k.val("so"))

        with nc.Block() as block:
            @block.tensor
            def _(e):
                for f in k.ops["tensor"]:
                    f(e)

            @block.vector
            def _(e):
                for f in k.ops["vector"]:
                    f(e)

            @block.scalar
            def _(e):
                for f in k.ops["scalar"]:
                    f(e)

            @block.sync
            def _(e):
                for f in k.ops["sync"]:
                    f(e)

            @block.gpsimd
            def _(e):
                for f in k.ops["gpsimd"]:
                    f(e)
    return nc


def _finish_dump(nc, k, y_out, strm, dr):
    if _DBG_Y == "ao_my":
        src_ = dr["ao_my"].rearrange("m t c -> (m t) c")
        k.emit("gpsimd", lambda e: e.dma_start(
            out=y_out[:].rearrange("t d -> (t d)"),
            in_=src_.rearrange("a b -> (a b)")),
            waits=[("ev", k.val("ev")), ("gld", k.val("gld"))],
            inc="so", inc_amt=16)
        k.wait("gpsimd", "so", k.val("so"))
        return
    if strm is not None:
        for tt in range(NTT):
            k.emit("gpsimd",
                   lambda e, _t=tt: e.dma_start(
                       out=y_out[_t * 128:(_t + 1) * 128, :], in_=strm[:, _t, :]),
                   waits=[("ev", k.val("ev"))], inc="so", inc_amt=16)
    else:
        for tt in range(NTT):
            k.emit("gpsimd",
                   lambda e, _t=tt: e.dma_start(
                       out=y_out[_t * 128:(_t + 1) * 128, :],
                       in_=dr["x2_d"][:, _t, :]),
                   waits=([("st", k.val("st"))] if tt == 0 else []),
                   inc="so", inc_amt=16)
    k.wait("gpsimd", "so", k.val("so"))


def _phaseA0(nc, k, DI, dr, es_top, ident_f, consts_ld, p0_gate):
    es = ExitStack()
    x_full = DI["x_full"]
    xT_d = dr["xT_d"]
    with es:
        xa = [_sb(nc, es, f"a0_x{i}", [128, DM], F32) for i in range(2)]
        xt = [_sb(nc, es, f"a0_t{i}", [128, 16, 128], F32) for i in range(2)]
        pst = [_psum(nc, es, f"a0_ps{i}", [128, 128]) for i in range(2)]
        tp_at = []
        st_at = []
        for g in range(64):
            b = g % 2
            w = ([("ld", consts_ld)] + p0_gate) if g < 2 else [("tp", tp_at[g - 2])]
            k.emit("sync", lambda e, _b=b, _g=g: e.dma_start(
                out=xa[_b][:], in_=x_full[_g * 128:(_g + 1) * 128, :]),
                waits=w, inc="ld", inc_amt=16)
            ldv = k.val("ld")
            for fc in range(16):
                pb = fc % 2
                w2 = [("ld", ldv)] if fc < 2 else [("ld", ldv), ("ev", k.val("ev") - 1)]
                k.emit("tensor", lambda e, _b=b, _fc=fc, _pb=pb: e.transpose(
                    pst[_pb][:], xa[_b][:, _fc * 128:(_fc + 1) * 128], ident_f[:]),
                    waits=w2, inc="tp")
                tpv = k.val("tp")
                w3 = [("tp", tpv)]
                if g >= 2 and fc == 0:
                    w3.append(("st", st_at[g - 2]))
                k.emit("vector", lambda e, _b=b, _fc=fc, _pb=pb: e.tensor_copy(
                    xt[_b][:, _fc, :], pst[_pb][:]), waits=w3, inc="ev")
            tp_at.append(k.val("tp"))
            evv = k.val("ev")
            dst = AP(tensor=xT_d.tensor, offset=g * 128,
                     ap=[[S, 128], [128 * S, 16], [1, 128]])
            k.emit("sync", lambda e, _b=b, _d=dst: e.dma_start(out=_d, in_=xt[_b][:]),
                   waits=[("ev", evv)], inc="st", inc_amt=16)
            st_at.append(k.val("st"))
    return k.val("st")


def _phase0(nc, k, DI, dr, es_top, ws_b, ps_x):
    es = ExitStack()
    with es:
        wsl = _sb(nc, es, "p0_wsl", [128, 8192], F32)
        wbf = _sb(nc, es, "p0_wbf", [128, 8192], BF16)
        scl = _sb(nc, es, "p0_scl", [128, 128], F32)
        sclb = _sb(nc, es, "p0_sclb", [128, 128], BF16)
        thr = _sb(nc, es, "p0_thr", [128, 128], F32)
        t_a = _sb(nc, es, "p0_ta", [128, 8192], F32)
        t_b = _sb(nc, es, "p0_tb", [128, 8192], F32)
        what = _sb(nc, es, "p0_what", [128, 8192], BF16)
        dtmp = _sb(nc, es, "p0_dtmp", [128, 8192], BF16)
        acc3 = _sb(nc, es, "p0_acc3", [128, 4], F32)
        red1 = _sb(nc, es, "p0_red1", [128, 4], F32)
        part4 = _sb(nc, es, "p0_part4", [1, 4], F32)
        g8 = _sb(nc, es, "p0_g8", [1, 4, 8], F32)
        ones_r = _sb(nc, es, "p0_ones", [1, 128], F32)
        ones_c = _sb(nc, es, "p0_onec", [128, 1], F32)

        def quant_l0(src_ap, dst_ap, width, src_3d=False):
            ngrp = width // 64
            w = [("qb", k.val("qb"))]
            dst_sb = wsl[:, :width]
            if src_3d:
                dst_sb = wsl[:, :width].rearrange("p (j d) -> p j d", d=DM)
            k.emit("gpsimd", lambda e, _s=src_ap, _d=dst_sb: e.dma_start(out=_d, in_=_s),
                   waits=w, inc="gld", inc_amt=16)
            ldv = k.val("gld")
            W = wsl[:, :width]
            WB = wbf[:, :width]
            k.emit("vector", lambda e: e.tensor_copy(WB, W), waits=[("gld", ldv)])
            gv = wbf[:, :width].rearrange("p (g q) -> p g q", q=64)
            SC = scl[:, :ngrp]
            SCB = sclb[:, :ngrp]
            TH = thr[:, :ngrp]
            k.emit("vector", lambda e: e.tensor_reduce(SC, gv, AXX, ALU.add,
                                                       apply_absolute_value=True),
                   inc="qa")
            k.emit("vector", lambda e: e.tensor_scalar_mul(SCB, SC, 1.0 / 64.0),
                   waits=[("qa", k.val("qa"))], inc="qa")
            k.emit("vector", lambda e: e.tensor_scalar_max(SCB, SCB, 1e-08),
                   waits=[("qa", k.val("qa"))], inc="qa")
            k.emit("vector", lambda e: e.tensor_scalar_mul(TH, SCB, BF_HALF),
                   waits=[("qa", k.val("qa"))], inc="qa")
            A = t_a[:, :width]
            B = t_b[:, :width]
            WH = what[:, :width]
            DT = dtmp[:, :width]
            thb = TH.rearrange("p (g o) -> p g o", o=1).broadcast_to([128, ngrp, 64])
            scb = SCB.rearrange("p (g o) -> p g o", o=1).broadcast_to([128, ngrp, 64])
            wbg = WB.rearrange("p (g q) -> p g q", q=64)
            Ag = A.rearrange("p (g q) -> p g q", q=64)
            Bg = B.rearrange("p (g q) -> p g q", q=64)
            WHg = WH.rearrange("p (g q) -> p g q", q=64)
            k.emit("vector", lambda e: e.tensor_tensor(Ag, wbg, thb, ALU.is_gt),
                   waits=[("qa", k.val("qa"))], inc="qa")
            k.emit("vector", lambda e: e.tensor_scalar_mul(thr[:, :ngrp], thr[:, :ngrp], -1.0),
                   waits=[("qa", k.val("qa"))], inc="qa")
            k.emit("vector", lambda e: e.tensor_tensor(Bg, wbg, thb, ALU.is_lt),
                   waits=[("qa", k.val("qa"))], inc="qa")
            k.emit("vector", lambda e: e.tensor_sub(A, A, B))
            k.emit("vector", lambda e: e.tensor_tensor(WHg, Ag, scb, ALU.mult),
                   waits=[("gst", k.val("gst"))])
            k.emit("vector", lambda e: e.tensor_sub(DT, WH, WB))
            k.emit("vector", lambda e: e.tensor_add(WH, WB, DT), inc="qb")
            qv = k.val("qb")
            WHout = WH if not src_3d else WH.rearrange("p (j d) -> p j d", d=DM)
            k.emit("gpsimd", lambda e, _d=dst_ap, _w=WHout: e.dma_start(out=_d, in_=_w),
                   waits=[("qb", qv)], inc="gst", inc_amt=16)

        gu_dst = dr["q1a_in"].rearrange("(u j p) d -> u p j d", u=4, j=4, p=128)
        for part in range(4):
            quant_l0(DI["w_gu_sl"][part], gu_dst[part], 8192, src_3d=True)
        p0_dst = dr["q1b_in"].rearrange("(u p) h -> u p h", u=2)
        for part in range(2):
            quant_l0(DI["w_p0_sl"][part], p0_dst[part], H0)
        st_q1 = k.val("gst")
        k.emit("gpsimd", lambda e: e.collective_compute(
            "AllGather", ALU.bypass, replica_groups=[list(range(NCORE))],
            ins=[dr["q1a_in"].opt()], outs=[dr["q1a_out"].opt()]),
            waits=[("gst", st_q1)], inc="cc")
        k.emit("gpsimd", lambda e: e.collective_compute(
            "AllGather", ALU.bypass, replica_groups=[list(range(NCORE))],
            ins=[dr["q1b_in"].opt()], outs=[dr["q1b_out"].opt()]),
            inc="cc")
        k.CC1 = k.val("cc")

        k.emit("vector", lambda e: e.memset(acc3[:], 0.0))
        k.emit("vector", lambda e: e.memset(ones_r[:], 1.0))
        k.emit("vector", lambda e: e.memset(ones_c[:], 1.0))

        def accum_abs(src_ap, col, width, src_3d=False):
            dst_sb = wsl[:, :width]
            if src_3d:
                dst_sb = wsl[:, :width].rearrange("p (j d) -> p j d", d=DM)
            k.emit("gpsimd", lambda e, _s=src_ap, _d=dst_sb: e.dma_start(out=_d, in_=_s),
                   waits=[("qa", k.val("qa"))], inc="gld", inc_amt=16)
            ldv = k.val("gld")
            k.emit("vector", lambda e, _w=width: e.tensor_reduce(
                red1[:, 0:1], wsl[:, :_w], AXX, ALU.add, apply_absolute_value=True),
                waits=[("gld", ldv)], inc="qa")
            k.emit("vector", lambda e, _c=col: e.tensor_add(
                acc3[:, _c:_c + 1], acc3[:, _c:_c + 1], red1[:, 0:1]),
                waits=[("qa", k.val("qa"))], inc="qa")

        for part in range(2):
            accum_abs(DI["w_g1_sl"].rearrange("(u v) p d -> u p v d", u=2)[part],
                      0, 4 * DM, src_3d=True)
            accum_abs(DI["w_u1_sl"].rearrange("(u v) p d -> u p v d", u=2)[part],
                      1, 4 * DM, src_3d=True)
        for part in range(2):
            accum_abs(DI["w_d1_sl"][part], 2, I1)
        qv = k.val("qa")
        k.emit("tensor", lambda e: e.matmul(ps_x[0:1, 0:4], ones_c[:],
                                            acc3[:], start=True, stop=True),
               waits=[("qa", qv)], inc="mm")
        mmv = k.val("mm")
        k.emit("vector", lambda e: e.tensor_copy(part4[:], ps_x[0:1, 0:4]),
               waits=[("mm", mmv)], inc="qb")
        qbv = k.val("qb")
        k.emit("gpsimd", lambda e: e.dma_start(out=dr["cc2_in"], in_=part4[:]),
               waits=[("qb", qbv)], inc="gst", inc_amt=16)
        stv = k.val("gst")
        k.emit("gpsimd", lambda e: e.collective_compute(
            "AllGather", ALU.bypass, replica_groups=[list(range(NCORE))],
            ins=[dr["cc2_in"].opt()], outs=[dr["cc2_out"].opt()]),
            waits=[("gst", stv)], inc="cc")
        cc2v = k.val("cc")
        k.emit("gpsimd", lambda e: e.dma_start(
            out=g8[:],
            in_=AP(tensor=dr["cc2_out"].tensor, offset=0, ap=[[32, 1], [1, 4], [4, 8]]),
            allow_slow_non_contiguous=True),
            waits=[("cc", cc2v)], inc="gld", inc_amt=16)
        ldv = k.val("gld")
        k.emit("vector", lambda e: e.tensor_reduce(part4[:], g8[:], AXX, ALU.add),
               waits=[("gld", ldv)], inc="qa")
        k.emit("vector", lambda e: e.tensor_scalar(part4[:], part4[:],
                                                   1.0 / 16777216.0, EPS_Q,
                                                   ALU.mult, ALU.add),
               waits=[("qa", k.val("qa"))], inc="qa")
        qv2 = k.val("qa")
        k.emit("tensor", lambda e: e.matmul(ps_x[:, 0:4], ones_r[0:1, :], part4[0:1, :],
                                            start=True, stop=True),
               waits=[("qa", qv2)], inc="mm")
        mmv2 = k.val("mm")
        k.emit("vector", lambda e: e.tensor_copy(ws_b[:], ps_x[:, 0:4]),
               waits=[("mm", mmv2)], inc="qb")
        k.WS_QB = k.val("qb")

        def quant_l1(src_ap, dst_ap, col, width, src_3d=False):
            dst_sb = wsl[:, :width]
            if src_3d:
                dst_sb = wsl[:, :width].rearrange("p (j d) -> p j d", d=DM)
                dst_ap = dst_ap
            k.emit("gpsimd", lambda e, _s=src_ap, _d=dst_sb: e.dma_start(out=_d, in_=_s),
                   waits=[("qb", k.val("qb"))], inc="gld", inc_amt=16)
            ldv = k.val("gld")
            W = wsl[:, :width]
            A = t_a[:, :width]
            B = t_b[:, :width]
            WH = what[:, :width]
            WHout = WH if not src_3d else what[:, :width].rearrange(
                "p (j d) -> p j d", d=DM)
            k.emit("vector", lambda e, _c=col: e.tensor_scalar_mul(
                red1[:, 1:2], ws_b[:, _c:_c + 1], 0.5),
                waits=[("gld", ldv), ("qb", k.WS_QB)], inc="qa")
            k.emit("vector", lambda e: e.tensor_scalar(A, W, red1[:, 1:2], None, ALU.is_gt),
                   waits=[("qa", k.val("qa"))], inc="qa")
            k.emit("vector", lambda e: e.tensor_scalar_mul(red1[:, 2:3], red1[:, 1:2], -1.0),
                   waits=[("qa", k.val("qa"))], inc="qa")
            k.emit("vector", lambda e: e.tensor_scalar(B, W, red1[:, 2:3], None, ALU.is_lt),
                   waits=[("qa", k.val("qa"))], inc="qa")
            k.emit("vector", lambda e: e.tensor_sub(WH, A, B),
                   waits=[("gst", k.val("gst"))], inc="qb")
            qv_ = k.val("qb")
            k.emit("gpsimd", lambda e, _d=dst_ap, _w=WHout: e.dma_start(out=_d, in_=_w),
                   waits=[("qb", qv_)], inc="gst", inc_amt=16)

        g1_dst = dr["q3a_in"].rearrange("(u v p) d -> u p v d", u=2, v=4, p=128)
        u1_dst = dr["q3b_in"].rearrange("(u v p) d -> u p v d", u=2, v=4, p=128)
        d1_dst = dr["q3c_in"].rearrange("(u p) h -> u p h", u=2)
        for part in range(2):
            quant_l1(DI["w_g1_sl"].rearrange("(u v) p d -> u p v d", u=2)[part],
                     g1_dst[part], 0, 4 * DM, src_3d=True)
        for part in range(2):
            quant_l1(DI["w_u1_sl"].rearrange("(u v) p d -> u p v d", u=2)[part],
                     u1_dst[part], 1, 4 * DM, src_3d=True)
        for part in range(2):
            quant_l1(DI["w_d1_sl"][part], d1_dst[part], 2, I1)
        st_q3 = k.val("gst")
        for a, b in (("q3a_in", "q3a_out"), ("q3b_in", "q3b_out"),
                     ("q3c_in", "q3c_out")):
            k.emit("gpsimd", lambda e, _a=a, _b=b: e.collective_compute(
                "AllGather", ALU.bypass, replica_groups=[list(range(NCORE))],
                ins=[dr[_a].opt()], outs=[dr[_b].opt()]),
                waits=[("gst", st_q3)], inc="cc")
        k.CC3 = k.val("cc")


def _queue_pretranspose(nc, k, dr, es_top):
    RT = 2048
    tsb = [_sb(nc, es_top, f"ptr_t{i}", [128, RT], BF16) for i in range(2)]

    def queue_mat(src, dst, R, Cc, wait_sem, wait_val):
        for rb in range(R // RT):
            for cb in range(Cc // 128):
                idx = rb * (Cc // 128) + cb
                b = idx % 2
                if k.val("wt") >= 32:
                    w = [(wait_sem, wait_val), ("wt", k.val("wt") - 32)]
                else:
                    w = [(wait_sem, wait_val)]
                sap = AP(tensor=src.tensor, offset=rb * RT * Cc + cb * 128,
                         ap=[[Cc, RT], [1, 128]])
                k.add_bg("sync", lambda e, _s=sap, _b=b: e.dma_start(
                    out=tsb[_b][:], in_=_s, transpose=True), waits=w,
                    inc="wt", inc_amt=16)
                wtv = k.val("wt")
                dap = AP(tensor=dst.tensor, offset=cb * 128 * R + rb * RT,
                         ap=[[R, 128], [1, RT]])
                k.add_bg("sync", lambda e, _d=dap, _b=b: e.dma_start(
                    out=_d, in_=tsb[_b][:]), waits=[("wt", wtv)],
                    inc="wt", inc_amt=16)

    queue_mat(dr["q1a_out"], dr["whT_d"], GU, DM, "cc", k.CC1)
    queue_mat(dr["q1b_out"], dr["wp0T_d"], DM, H0, "cc", k.CC1)
    k.WT_L0 = k.val("wt")
    queue_mat(dr["q3a_out"], dr["wg1T_d"], I1, DM, "cc", k.CC3)
    queue_mat(dr["q3b_out"], dr["wu1T_d"], I1, DM, "cc", k.CC3)
    queue_mat(dr["q3c_out"], dr["wd1T_d"], DM, I1, "cc", k.CC3)
    k.WT_L1 = k.val("wt")


def _phaseA(nc, k, DI, dr, es_top, ident_f, tri_sb, taps, st_a0):
    es = ExitStack()
    with es:
        qT = _sb(nc, es, "a_qT", [128, S], F32)
        kT = _sb(nc, es, "a_kT", [128, S], F32)
        vS = _sb(nc, es, "a_vS", [128, 64, 128], F32)
        wq_sb = _sb(nc, es, "a_wq", [128, 16, 256], F32)
        wk_sb = _sb(nc, es, "a_wk", [128, 16, 256], F32)
        wv_sb = _sb(nc, es, "a_wv", [128, 16, 256], F32)
        bq_sb = _sb(nc, es, "a_bq", [128, 2], F32)
        bk_sb = _sb(nc, es, "a_bk", [128, 2], F32)
        bv_sb = _sb(nc, es, "a_bv", [128, 2], F32)
        dum_sb = _sb(nc, es, "a_dum", [128, 2], F32)
        xc = [_sb(nc, es, f"a_xc{i}", [128, 16, 256], F32) for i in range(2)]
        vtmp = _sb(nc, es, "a_vtmp", [128, 256], F32)
        probs = [_sb(nc, es, f"a_pb{i}", [128, 512], F32) for i in range(2)]
        pT = _sb(nc, es, "a_pT", [128, 512], F32)
        oacc = _sb(nc, es, "a_oacc", [128, 128], F32)
        rmax = _sb(nc, es, "a_rmax", [128, 1], F32)
        rsum = _sb(nc, es, "a_rsum", [128, 1], F32)
        tmax = [_sb(nc, es, f"a_tmax{i}", [128, 1], F32) for i in range(2)]
        nmax = [_sb(nc, es, f"a_nmax{i}", [128, 1], F32) for i in range(2)]
        nneg = [_sb(nc, es, f"a_nneg{i}", [128, 1], F32) for i in range(2)]
        facin = [_sb(nc, es, f"a_facin{i}", [128, 1], F32) for i in range(2)]
        fac = [_sb(nc, es, f"a_fac{i}", [128, 1], F32) for i in range(2)]
        tsum = [_sb(nc, es, f"a_tsum{i}", [128, 1], F32) for i in range(2)]
        rcp = _sb(nc, es, "a_rcp", [128, 1], F32)
        ofin = [_sb(nc, es, f"a_ofin{i}", [128, 128], F32) for i in range(2)]
        ps_s = [_psum(nc, es, f"a_pss{i}", [128, 512]) for i in range(2)]
        ps_t = [_psum(nc, es, f"a_pst{i}", [128, 128]) for i in range(2)]
        ps_o = [_psum(nc, es, f"a_pso{i}", [128, 128]) for i in range(2)]
        ps_v = _psum(nc, es, "a_psv", [128, 256])

        for nm, sb in (("wq", wq_sb), ("wk", wk_sb), ("wv", wv_sb),
                       ("bq", bq_sb), ("bk", bk_sb), ("bv", bv_sb)):
            k.emit("sync", lambda e, _n=nm, _s=sb: e.dma_start(out=_s[:], in_=DI[_n]),
                   inc="ld", inc_amt=16)
        w_ld = k.val("ld")
        xT_d = dr["xT_d"]
        ao_in = dr["ao_in"]
        bg_per_qt = 8

        qkv_mm_last = 0
        for h in range(HPC):
            mm_at_tb = []
            ev_at_tb = []
            for tb in range(32):
                b = tb % 2
                if h == 0 and tb < 2:
                    w = [("st", st_a0), ("ld", w_ld)]
                elif tb < 2:
                    w = [("mm", qkv_mm_last)]
                else:
                    w = [("ev", ev_at_tb[tb - 2])]
                src = AP(tensor=xT_d.tensor, offset=tb * 256,
                         ap=[[S, 128], [128 * S, 16], [1, 256]])
                k.emit("sync", lambda e, _b=b, _s=src: e.dma_start(out=xc[_b][:], in_=_s),
                       waits=w, inc="ld", inc_amt=16)
                ldv = k.val("ld") + 16
                for oi in range(3):
                    wsb, bsb = ((wq_sb, bq_sb), (wk_sb, bk_sb), (wv_sb, bv_sb))[oi]
                    pq = (ps_s[0], ps_s[1], ps_v)[oi]
                    for kc in range(16):
                        wts = []
                        if kc == 0:
                            wts = [("ld", ldv)]
                            if tb >= 1 or h > 0:
                                wts.append(("ev", k.val("ev")))
                        k.emit("tensor", lambda e, _w=wsb, _kc=kc, _h=h, _b=b, _p=pq,
                               _sl=pq is ps_v: e.matmul(
                                   (_p[:, 0:256]), _w[:, _kc, 128 * _h:128 * _h + 128],
                                   xc[_b][:, _kc, :], start=(_kc == 0), stop=(_kc == 15)),
                               waits=wts, inc=("mm" if kc == 15 else None))
                    mmv = k.val("mm")
                    if oi < 2:
                        dst = (qT, kT)[oi]
                        scale = QK_SCALE if oi == 0 else 1.0
                        k.emit("vector", lambda e, _d=dst, _tb=tb, _p=pq, _b2=bsb, _h=h,
                               _sc=scale: e.tensor_scalar(
                                   _d[:, _tb * 256:(_tb + 1) * 256], _p[:, 0:256],
                                   _b2[:, _h:_h + 1], _sc, ALU.add, ALU.mult),
                               waits=[("mm", mmv)], inc="ev")
                    else:
                        k.emit("vector", lambda e, _p=pq, _b2=bv_sb, _h=h: e.tensor_scalar_add(
                            vtmp[:], _p[:, 0:256], _b2[:, _h:_h + 1]),
                            waits=[("mm", mmv), ("tp", k.val("tp"))], inc="ev")
                        evv = k.val("ev")
                        for j in range(2):
                            pb = j % 2
                            k.emit("tensor", lambda e, _j=j, _pb=pb: e.transpose(
                                ps_t[_pb][:], vtmp[:, _j * 128:(_j + 1) * 128], ident_f[:]),
                                waits=[("ev", evv), ("ev", k.val("ev"))], inc="tp")
                            tpv = k.val("tp")
                            k.emit("vector", lambda e, _tb=tb, _j=j, _pb=pb: e.tensor_copy(
                                vS[:, _tb * 2 + _j, :], ps_t[_pb][:]),
                                waits=[("tp", tpv)], inc="ev")
                mm_at_tb.append(k.val("mm"))
                ev_at_tb.append(k.val("ev"))
            k.emit("sync", lambda e: e.dma_start(out=dum_sb[:], in_=DI["bq"]),
                   inc="ld", inc_amt=16)
            qkv_mm_last = k.val("mm")
            qkv_ev = k.val("ev")
            if _DBG_QKV and h == 0:
                k.emit("gpsimd", lambda e: e.dma_start(out=dr["dbg_q"], in_=qT[:]),
                       waits=[("ev", qkv_ev)], inc="gst", inc_amt=16)
                k.emit("gpsimd", lambda e: e.dma_start(out=dr["dbg_k"], in_=kT[:]),
                       inc="gst", inc_amt=16)
                k.emit("gpsimd", lambda e: e.dma_start(out=dr["dbg_v"], in_=vS[:]),
                       inc="gst", inc_amt=16)

            for qt in range(64):
                groups = []
                nfull = (qt * 128) // 512
                rem = qt * 128 - nfull * 512
                for i in range(nfull):
                    groups.append((i * 512, 512, False))
                if rem:
                    groups.append((nfull * 512, rem, False))
                groups.append((qt * 128, 128, True))
                ng = len(groups)

                k.emit("vector", lambda e: e.memset(oacc[:], 0.0),
                       waits=([("ev", qkv_ev)] if qt == 0 else []))
                k.emit("vector", lambda e: e.memset(rmax[:], MSK))
                k.emit("vector", lambda e: e.memset(rsum[:], 0.0), inc="ev")

                st_g = {}

                def emit_scores(gi):
                    k0, kn, masked = groups[gi]
                    sb_ = gi % 2
                    if gi < 2:
                        w = [("ev", k.val("ev")), ("ex", k.val("ex"))]
                    else:
                        w = [("ex", st_g[gi - 2]["ex"])]
                    k.emit("tensor", lambda e, _q=qt, _k0=k0, _kn=kn, _s=sb_: e.matmul(
                        ps_s[_s][:, :_kn], qT[:, _q * 128:(_q + 1) * 128],
                        kT[:, _k0:_k0 + _kn], start=True, stop=True),
                        waits=w, inc="mm")
                    st_g[gi] = {"mm": k.val("mm")}

                def emit_rest(gi):
                    k0, kn, masked = groups[gi]
                    sb_ = gi % 2
                    mmv = st_g[gi]["mm"]
                    wst = [("mm", mmv)]
                    if gi >= 2:
                        wst.append(("ex", st_g[gi - 2]["ex"]))
                    if masked:
                        k.emit("vector", lambda e, _s=sb_, _kn=kn: e.tensor_add(
                            ps_s[_s][:, :_kn], ps_s[_s][:, :_kn], tri_sb[:]),
                            waits=wst, inc="ev")
                        mw = [("ev", k.val("ev"))]
                    else:
                        mw = wst
                    k.emit("vector", lambda e, _s=sb_, _kn=kn: e.reduce_max(
                        tmax[_s][:], ps_s[_s][:, :_kn], AXX), waits=mw, inc="ev")
                    k.emit("vector", lambda e, _s=sb_: e.tensor_max(
                        nmax[_s][:], rmax[:], tmax[_s][:]),
                        waits=[("ev", k.val("ev"))], inc="ev")
                    k.emit("vector", lambda e, _s=sb_: e.tensor_sub(
                        facin[_s][:], rmax[:], nmax[_s][:]),
                        waits=[("ev", k.val("ev"))], inc="ev")
                    k.emit("vector", lambda e, _s=sb_: e.tensor_scalar_mul(
                        nneg[_s][:], nmax[_s][:], -1.0),
                        waits=[("ev", k.val("ev"))], inc="ev")
                    k.emit("vector", lambda e, _s=sb_: e.tensor_copy(
                        rmax[:], nmax[_s][:]),
                        waits=[("ev", k.val("ev"))], inc="ev")
                    evv = k.val("ev")
                    wex = [("ev", evv)]
                    if gi >= 2:
                        wex.append(("tp", st_g[gi - 2]["tp"]))
                    k.emit("scalar", lambda e, _s=sb_: e.activation(
                        fac[_s][:], facin[_s][:], AF.Exp, bias=0.0, scale=1.0),
                        waits=wex)
                    k.emit("scalar", lambda e, _s=sb_, _kn=kn: e.activation(
                        probs[_s][:, :_kn], ps_s[_s][:, :_kn], AF.Exp, bias=nneg[_s][:],
                        scale=1.0, accum_out=tsum[_s][:]), inc="ex")
                    exv = k.val("ex")
                    st_g[gi]["ex"] = exv
                    k.emit("vector", lambda e, _s=sb_: e.scalar_tensor_tensor(
                        rsum[:], rsum[:], fac[_s][:], tsum[_s][:], ALU.mult, ALU.add),
                        waits=[("ex", exv)], inc="ev")
                    nch = (kn + 127) // 128
                    for j in range(nch):
                        pb = j % 2
                        w2 = [("ex", exv), ("ev", k.val("ev"))]
                        if j == 0:
                            w2.append(("pv", k.val("pv")))
                        k.emit("tensor", lambda e, _s=sb_, _j=j, _pb=pb: e.transpose(
                            ps_t[_pb][:], probs[_s][:, _j * 128:(_j + 1) * 128], ident_f[:]),
                            waits=w2, inc="tp")
                        tpv = k.val("tp")
                        k.emit("vector", lambda e, _j=j, _pb=pb: e.tensor_copy(
                            pT[:, _j * 128:(_j + 1) * 128], ps_t[_pb][:]),
                            waits=[("tp", tpv)], inc="ev")
                    st_g[gi]["tp"] = k.val("tp")
                    evv2 = k.val("ev")
                    ob = gi % 2
                    for j in range(nch):
                        w3 = []
                        if j == 0:
                            w3 = [("ev", evv2), ("mg", max(0, k.val("mg") - 1))]
                        k.emit("tensor", lambda e, _j=j, _k0=k0, _ob=ob, _n=nch: e.matmul(
                            ps_o[_ob][:, :], pT[:, _j * 128:(_j + 1) * 128],
                            vS[:, _k0 // 128 + _j, :], start=(_j == 0), stop=(_j == _n - 1)),
                            waits=w3, inc=("pv" if j == nch - 1 else None))
                    pvv = k.val("pv")
                    k.emit("vector", lambda e, _ob=ob, _s=sb_: e.scalar_tensor_tensor(
                        oacc[:], oacc[:], fac[_s][:], ps_o[_ob][:, :], ALU.mult, ALU.add),
                        waits=[("pv", pvv)], inc="mg")

                if _NO_PIPE:
                    for gi in range(ng):
                        emit_scores(gi)
                        emit_rest(gi)
                else:
                    emit_scores(0)
                    for gi in range(ng):
                        if gi + 1 < ng:
                            emit_scores(gi + 1)
                        emit_rest(gi)
                mgv = k.val("mg")
                k.emit("vector", lambda e: e.reciprocal(rcp[:], rsum[:]),
                       waits=[("mg", mgv)])
                ob2 = (h * 64 + qt) % 2
                wfin = [("st", max(st_a0, k.val("st") - 16))]
                k.emit("vector", lambda e, _o=ob2: e.tensor_scalar_mul(
                    ofin[_o][:], oacc[:], rcp[:]), waits=wfin, inc="ev")
                evf = k.val("ev")
                k.emit("scalar", lambda e, _q=qt, _h=h, _o=ob2: e.dma_start(
                    out=ao_in[_q * 128:(_q + 1) * 128, _h * 128:(_h + 1) * 128],
                    in_=ofin[_o][:]), waits=[("ev", evf)], inc="st", inc_amt=16)
                if (h == 0 and qt >= 16) or h == 1:
                    k.drain_bg(bg_per_qt)
        k.drain_bg(None)
        return k.val("st")


def _phaseB(nc, k, DI, dr, es_top, ident_f, strm, taps, st_attn):
    es = ExitStack()
    with es:
        aob = _sb(nc, es, "b_aob", [128, NCORE, 256], F32)
        aoT = _sb(nc, es, "b_aoT", [128, 16, T], F32)
        wpt = [_sb(nc, es, f"b_wp{i}", [128, DM], F32) for i in range(2)]
        xow = _sb(nc, es, "b_xow", [128, 4, DM], F32)
        bpr = _sb(nc, es, "b_bpr", [128, DM], F32)
        sx = _sb(nc, es, "b_sx", [128, 1], F32)
        sxx = _sb(nc, es, "b_sxx", [128, 1], F32)
        mu = _sb(nc, es, "b_mu", [128, 1], F32)
        var = _sb(nc, es, "b_var", [128, 1], F32)
        inv = _sb(nc, es, "b_inv", [128, 1], F32)
        t1 = _sb(nc, es, "b_t1", [128, 1], F32)
        epsv = _sb(nc, es, "b_eps", [128, 1], F32)
        b_dum = _sb(nc, es, "b_dum", [128, 2], F32)
        rki = _sb(nc, es, "b_rki", [1, 1], I32)
        ps_t = [_psum(nc, es, f"b_pst{i}", [128, 128]) for i in range(2)]
        ps_p = [_psum(nc, es, f"b_psp{i}", [128, 512]) for i in range(4)]

        k.emit("sync", lambda e: e.dma_start(out=bpr[:], in_=DI["bproj_b"]),
               waits=[("st", st_attn)], inc="ld", inc_amt=16)
        k.emit("sync", lambda e: e.dma_start(out=rki[:], in_=DI["rank_i"]),
               inc="ld", inc_amt=16)
        ld_res = k.val("ld")
        k.emit("vector", lambda e: e.memset(epsv[:], 1e-05))

        if "aoin_dbg" in _DBG_DRAM:
            k.emit("gpsimd", lambda e: e.dma_start(
                out=dr["aoin_dbg"], in_=dr["ao_in"]),
                waits=[("st", st_attn)], inc="gst", inc_amt=16)
        k.emit("gpsimd", lambda e: e.collective_compute(
            "AllGather", ALU.bypass, replica_groups=[list(range(NCORE))],
            ins=[dr["ao_in"].opt()], outs=[dr["ao_out"].opt()]),
            waits=[("st", st_attn)], inc="cc")
        cc4 = k.val("cc")

        ao_t = dr["ao_out"].tensor
        ao_my = dr["ao_my"]

        def ld_slice(e):
            with e.register("r_b0") as r1, e.register("r_o0") as r2:
                e.load(r1, rki[0:1, 0:1])
                e.reg_mul(r1, r1, T * 256)
                for m in range(NCORE):
                    e.reg_add(r2, r1, m * S * 256)
                    e.dma_start(
                        out=ao_my[m],
                        in_=AP(tensor=ao_t, offset=r2, ap=[[256, T], [1, 256]])
                    ).then_inc(k.S("gld"), 16)
        k.cnt["gld"] += 16 * NCORE
        k.emit("gpsimd", lambda e: ld_slice(e), waits=[("cc", cc4), ("ld", ld_res)])
        ld_slc = k.val("gld")

        tp_at_tt = []
        for tt in range(NTT):
            wv = [("gld", ld_slc)]
            if tt >= 1:
                wv.append(("tp", tp_at_tt[tt - 1]))
            first = True
            for m in range(NCORE):
                k.emit("gpsimd", lambda e, _m=m, _tt=tt: e.dma_start(
                    out=aob[:, _m, :],
                    in_=ao_my[_m, _tt * 128:(_tt + 1) * 128, :]),
                    waits=(wv if first else []), inc="gld", inc_amt=16)
                first = False
            ldv = k.val("gld")
            aof = aob[:].rearrange("p m c -> p (m c)")
            for fc in range(16):
                pb = fc % 2
                w = [("gld", ldv)]
                if tt > 0 or fc >= 2:
                    w.append(("ev", k.val("ev") - 1))
                k.emit("tensor", lambda e, _fc=fc, _pb=pb, _a=aof: e.transpose(
                    ps_t[_pb][:], _a[:, _fc * 128:(_fc + 1) * 128], ident_f[:]),
                    waits=w, inc="tp")
                tpv = k.val("tp")
                k.emit("vector", lambda e, _tt=tt, _fc=fc, _pb=pb: e.tensor_copy(
                    aoT[:, _fc, _tt * 128:(_tt + 1) * 128], ps_t[_pb][:]),
                    waits=[("tp", tpv)], inc="ev")
            tp_at_tt.append(k.val("tp"))
        aoT_ev = k.val("ev")

        mm_seq = []
        for half in range(2):
            for i4 in range(4):
                tt = half * 4 + i4
                w0 = []
                if half == 0 and i4 == 0:
                    w0 = [("st", st_attn)]
                elif half == 1 and i4 == 0:
                    w0 = [("ev", k.val("ev"))]
                k.emit("sync", lambda e, _i=i4, _tt=tt: e.dma_start(
                    out=xow[:, _i, :],
                    in_=DI["x_own"][_tt * 128:(_tt + 1) * 128, :]),
                    waits=w0, inc="ld", inc_amt=16)
            xow_ld = k.val("ld")
            for nb in range(4):
                for kc in range(16):
                    b = kc % 2
                    idx = len(mm_seq)
                    w = []
                    if idx == 0:
                        w.append(("st", st_attn))
                    if idx >= 2:
                        w.append(("mm", mm_seq[idx - 2]))
                    k.emit("sync", lambda e, _kc=kc, _b=b: e.dma_start(
                        out=wpt[_b][:], in_=DI["wproj_t"][_kc]), waits=w,
                        inc="lw", inc_amt=16)
                    lwv = k.val("lw") + 16
                    for i4 in range(4):
                        tt = half * 4 + i4
                        w2 = []
                        if i4 == 0:
                            w2 = [("lw", lwv)]
                            if kc == 0:
                                w2.append(("ev", aoT_ev if (nb == 0 and half == 0)
                                           else k.val("ev")))
                        k.emit("tensor", lambda e, _i=i4, _tt=tt, _kc=kc, _nb=nb,
                               _b=b: e.matmul(
                                   ps_p[_i][:, :], aoT[:, _kc, _tt * 128:(_tt + 1) * 128],
                                   wpt[_b][:, _nb * 512:(_nb + 1) * 512],
                                   start=(_kc == 0), stop=(_kc == 15)),
                               waits=w2, inc=("mm" if i4 == 3 else None))
                    mm_seq.append(k.val("mm"))
                if half == 0 and nb == 3:
                    k.emit("sync", lambda e: e.dma_start(
                        out=b_dum[:], in_=DI["bq"]), inc="lw", inc_amt=16)
                mmv = k.val("mm")
                for i4 in range(4):
                    tt = half * 4 + i4
                    if _DBG_PROJONLY == "proj":
                        k.emit("vector", lambda e, _i=i4, _tt=tt, _nb=nb: e.tensor_copy(
                            strm[:, _tt, _nb * 512:(_nb + 1) * 512], ps_p[_i][:, :]),
                            waits=([("mm", mmv), ("ld", xow_ld)] if i4 == 0 else []),
                            inc="ev")
                        continue
                    if _DBG_PROJONLY == "resid":
                        k.emit("vector", lambda e, _i=i4, _tt=tt, _nb=nb: e.tensor_add(
                            strm[:, _tt, _nb * 512:(_nb + 1) * 512], ps_p[_i][:, :],
                            xow[:, _i, _nb * 512:(_nb + 1) * 512]),
                            waits=([("mm", mmv), ("ld", xow_ld)] if i4 == 0 else []),
                            inc="ev")
                        continue
                    if _DBG_PROJONLY == "bias":
                        k.emit("vector", lambda e, _i=i4, _tt=tt, _nb=nb: e.tensor_add(
                            strm[:, _tt, _nb * 512:(_nb + 1) * 512], ps_p[_i][:, :],
                            bpr[:, _nb * 512:(_nb + 1) * 512]),
                            waits=([("mm", mmv), ("ld", xow_ld)] if i4 == 0 else []),
                            inc="ev")
                        continue
                    k.emit("vector", lambda e, _i=i4, _tt=tt, _nb=nb: e.tensor_add(
                        strm[:, _tt, _nb * 512:(_nb + 1) * 512], ps_p[_i][:, :],
                        xow[:, _i, _nb * 512:(_nb + 1) * 512]),
                        waits=([("mm", mmv), ("ld", xow_ld)] if i4 == 0 else []))
                    k.emit("vector", lambda e, _tt=tt, _nb=nb: e.tensor_add(
                        strm[:, _tt, _nb * 512:(_nb + 1) * 512],
                        strm[:, _tt, _nb * 512:(_nb + 1) * 512],
                        bpr[:, _nb * 512:(_nb + 1) * 512]), inc="ev")
        k.emit("sync", lambda e: e.dma_start(out=b_dum[:], in_=DI["bq"]),
               inc="lw", inc_amt=16)
        prj_ev = k.val("ev")

        if _SKIP_LN:
            return
        for tt in range(NTT):
            w = [("ev", prj_ev)] if tt == 0 else [("ev", k.val("ev"))]
            k.emit("scalar", lambda e, _tt=tt: e.activation(
                wpt[0][:], strm[:, _tt, :], AF.Copy, bias=0.0, scale=1.0,
                accum_out=sx[:]), waits=w, inc="ex")
            k.emit("scalar", lambda e, _tt=tt: e.activation(
                wpt[1][:], strm[:, _tt, :], AF.Square, bias=0.0, scale=1.0,
                accum_out=sxx[:]), inc="ex")
            exv = k.val("ex")
            k.emit("vector", lambda e: e.tensor_scalar_mul(mu[:], sx[:], 1.0 / DM),
                   waits=[("ex", exv)], inc="ev")
            k.emit("vector", lambda e: e.tensor_scalar_mul(var[:], sxx[:], 1.0 / DM),
                   inc="ev")
            k.emit("vector", lambda e: e.tensor_mul(t1[:], mu[:], mu[:]),
                   waits=[("ev", k.val("ev"))], inc="ev")
            k.emit("vector", lambda e: e.tensor_sub(var[:], var[:], t1[:]),
                   waits=[("ev", k.val("ev"))], inc="ev")
            evv = k.val("ev")
            k.emit("scalar", lambda e: e.activation(t1[:], var[:], AF.Sqrt,
                                                    bias=epsv[:], scale=1.0),
                   waits=[("ev", evv)], inc="ex")
            exv2 = k.val("ex")
            k.emit("vector", lambda e: e.reciprocal(inv[:], t1[:]),
                   waits=[("ex", exv2)], inc="ev")
            k.emit("vector", lambda e, _tt=tt: e.tensor_scalar(
                strm[:, _tt, :], strm[:, _tt, :], mu[:], inv[:],
                ALU.subtract, ALU.mult), waits=[("ev", k.val("ev"))], inc="ev")
            if "x1_d" in _DBG_DRAM:
                k.emit("scalar", lambda e, _tt=tt: e.dma_start(
                    out=dr["x1_d"][:, _tt, :], in_=strm[:, _tt, :]),
                    waits=[("ev", k.val("ev"))], inc="st", inc_amt=16)



def _phaseC(nc, k, DI, dr, es_top, ident_f, ident_b, strm, taps):
    es = ExitStack()
    with es:
        x1Th = _sb(nc, es, "c_x1th", [128, 16, 128], BF16)
        x1Tl = _sb(nc, es, "c_x1tl", [128, 16, 128], BF16)
        x1hi = _sb(nc, es, "c_x1hi", [128, DM], BF16)
        x1lo = _sb(nc, es, "c_x1lo", [128, DM], BF16)
        wqd = [_sb(nc, es, f"c_wqd{i}", [128, 16, 256], BF16) for i in range(3)]
        wdd = [_sb(nc, es, f"c_wdd{i}", [128, DM], BF16) for i in range(3)]
        g_sb = _sb(nc, es, "c_g", [128, H0], F32)
        h_sb = g_sb
        hnh = _sb(nc, es, "c_hnh", [128, H0], BF16)
        hnl = _sb(nc, es, "c_hnl", [128, H0], BF16)
        hTh = _sb(nc, es, "c_hth", [128, 32, 128], BF16)
        hTl = _sb(nc, es, "c_htl", [128, 32, 128], BF16)
        acc16 = _sb(nc, es, "c_acc16", [128, 32], F32)
        sq_scr = _sb(nc, es, "c_sqscr", [128, 256], F32)
        s1 = _sb(nc, es, "c_s1", [128, 1], F32)
        epsr = _sb(nc, es, "c_epsr", [128, 1], F32)
        sx = _sb(nc, es, "c_sx", [128, 1], F32)
        sxx = _sb(nc, es, "c_sxx", [128, 1], F32)
        mu = _sb(nc, es, "c_mu", [128, 1], F32)
        var = _sb(nc, es, "c_var", [128, 1], F32)
        inv = _sb(nc, es, "c_inv", [128, 1], F32)
        t1 = _sb(nc, es, "c_t1", [128, 1], F32)
        eps5 = _sb(nc, es, "c_eps5", [128, 1], F32)
        c_dum = _sb(nc, es, "c_dum", [128, 2], F32)

        ps_g = [_psum(nc, es, f"c_psg{i}", [128, 512]) for i in range(4)]
        ps_t = [es.enter_context(nc.psum_tensor(f"c_pst{i}", [128, 128], BF16))
                for i in range(2)]
        b_done = k.val("ev")
        whT = dr["whT_d"].tensor
        wp0T = dr["wp0T_d"].tensor
        x2_d = dr["x2_d"]

        k.emit("vector", lambda e: e.memset(epsr[:], EPS_RMS), waits=[("ev", b_done)])
        k.emit("vector", lambda e: e.memset(eps5[:], 1e-05), inc="ev")
        dn_seq = []

        for tt in range(NTT):
            wv = [("ev", k.val("ev"))]
            if tt == 0:
                wv.append(("wt", k.WT_L0))
            k.emit("vector", lambda e, _tt=tt: e.tensor_copy(x1hi[:], strm[:, _tt, :]),
                   waits=wv)
            k.emit("vector", lambda e, _tt=tt: e.tensor_tensor(
                x1lo[:], strm[:, _tt, :], x1hi[:], ALU.subtract), inc="ev")
            sp_ev = k.val("ev")
            for fc in range(16):
                for which, (srcb, dstb) in enumerate(((x1hi, x1Th), (x1lo, x1Tl))):
                    pb = (2 * fc + which) % 2
                    w = [("ev", sp_ev)]
                    if fc > 0 or which > 0 or tt > 0:
                        w.append(("ev", k.val("ev") - 1))
                    k.emit("tensor", lambda e, _fc=fc, _pb=pb, _s=srcb: e.transpose(
                        ps_t[_pb][:], _s[:, _fc * 128:(_fc + 1) * 128], ident_b[:]),
                        waits=w, inc="tp")
                    k.emit("vector", lambda e, _fc=fc, _pb=pb, _d=dstb: e.tensor_copy(
                        _d[:, _fc, :], ps_t[_pb][:]), waits=[("tp", k.val("tp"))],
                        inc="ev")
            xT_ev = k.val("ev")

            for up in range(2):
                for jq in range(32):
                    b = (up * 32 + jq + tt * 64) % 3
                    pb_ = jq % 4
                    w = []
                    if tt == 0 and up == 0 and jq < 2:
                        w = [("ev", b_done)]
                    else:
                        w = [("mm", k.val("mm") - 2)]
                    src = AP(tensor=whT, offset=up * H0 + jq * 256,
                             ap=[[GU, 128], [128 * GU, 16], [1, 256]])
                    k.emit("sync", lambda e, _b=b, _s=src: e.dma_start(
                        out=wqd[_b][:], in_=_s), waits=w, inc="lw", inc_amt=16)
                    lwv = k.val("lw") + 16
                    for pas in range(2):
                        xop = (x1Th, x1Tl)[pas]
                        for kc in range(16):
                            w2 = []
                            if pas == 0 and kc == 0:
                                w2 = [("lw", lwv), ("ev", xT_ev)]
                                w2.append(("ev", k.val("ev")))
                            k.emit("tensor", lambda e, _x=xop, _kc=kc, _b=b, _p=pb_,
                                   _pas=pas: e.matmul(
                                       ps_g[_p][:, 0:256], _x[:, _kc, :],
                                       wqd[_b][:, _kc, :],
                                       start=(_kc == 0 and _pas == 0),
                                       stop=(_kc == 15 and _pas == 1)),
                                   waits=w2,
                                   inc=("mm" if (kc == 15 and pas == 1) else None))
                    mmv = k.val("mm")
                    if up == 0:
                        k.emit("scalar", lambda e, _jq=jq, _p=pb_: e.activation(
                            g_sb[:, _jq * 256:(_jq + 1) * 256], ps_g[_p][:, 0:256],
                            AF.Silu, bias=0.0, scale=1.0),
                            waits=[("mm", mmv)], inc="ev")
                    else:
                        k.emit("vector", lambda e, _jq=jq, _p=pb_: e.tensor_tensor(
                            h_sb[:, _jq * 256:(_jq + 1) * 256], ps_g[_p][:, 0:256],
                            g_sb[:, _jq * 256:(_jq + 1) * 256], ALU.mult),
                            waits=[("mm", mmv)], inc="ev")
                        k.emit("scalar", lambda e, _jq=jq: e.activation(
                            sq_scr[:],
                            h_sb[:, _jq * 256:(_jq + 1) * 256],
                            AF.Square, bias=0.0, scale=1.0,
                            accum_out=acc16[:, _jq:_jq + 1]),
                            waits=[("ev", k.val("ev"))], inc="ex")
            h_ex = k.val("ex")

            k.emit("vector", lambda e: e.tensor_reduce(s1[:], acc16[:], AXX, ALU.add),
                   waits=[("ex", h_ex)], inc="ev")
            k.emit("vector", lambda e: e.tensor_scalar(
                s1[:], s1[:], 1.0 / H0, epsr[:], ALU.mult, ALU.add),
                waits=[("ev", k.val("ev"))], inc="ev")
            sev = k.val("ev")
            k.emit("scalar", lambda e: e.activation(t1[:], s1[:], AF.Sqrt,
                                                    bias=0.0, scale=1.0),
                   waits=[("ev", sev)], inc="ex")
            k.emit("vector", lambda e: e.reciprocal(inv[:], t1[:]),
                   waits=[("ex", k.val("ex"))], inc="ev")
            k.emit("vector", lambda e: e.tensor_scalar_mul(hnh[:], h_sb[:], inv[:]),
                   waits=[("ev", k.val("ev"))], inc="ev")
            k.emit("vector", lambda e: e.scalar_tensor_tensor(
                hnl[:], h_sb[:], inv[:], hnh[:], ALU.mult, ALU.subtract), inc="ev")
            hn_ev = k.val("ev")
            for hh in range(2):
                for fc in range(32):
                    for which, (srcb, dstb) in enumerate(((hnh, hTh), (hnl, hTl))):
                        pb = (2 * fc + which) % 2
                        w = [("ev", hn_ev), ("ev", k.val("ev") - 1)]
                        if fc == 0 and which == 0 and hh == 1:
                            w.append(("mm", k.val("mm")))
                        k.emit("tensor", lambda e, _fc=fc, _hh=hh, _pb=pb, _s=srcb:
                               e.transpose(ps_t[_pb][:],
                                           _s[:, (_hh * 32 + _fc) * 128:
                                              (_hh * 32 + _fc + 1) * 128],
                                           ident_b[:]),
                               waits=w, inc="tp")
                        k.emit("vector", lambda e, _fc=fc, _pb=pb, _d=dstb:
                               e.tensor_copy(_d[:, _fc, :], ps_t[_pb][:]),
                               waits=[("tp", k.val("tp"))], inc="ev")
                hT_ev = k.val("ev")

                for kci in range(32):
                    kc = hh * 32 + kci
                    b = (tt * 64 + kc) % 3
                    idx = len(dn_seq)
                    w = []
                    if idx >= 3:
                        w.append(("mm", dn_seq[idx - 3]))
                    src = AP(tensor=wp0T, offset=kc * 128 * DM,
                             ap=[[DM, 128], [1, DM]])
                    k.emit("sync", lambda e, _b=b, _s=src: e.dma_start(
                        out=wdd[_b][:], in_=_s), waits=w, inc="lw", inc_amt=16)
                    lwv = k.val("lw") + 16
                    for pas in range(2):
                        hop = (hTh, hTl)[pas]
                        for nb in range(4):
                            w2 = []
                            if pas == 0 and nb == 0:
                                w2 = [("lw", lwv), ("ev", hT_ev)]
                                if kc == 0:
                                    w2.append(("ev", k.val("ev")))
                            k.emit("tensor", lambda e, _h=hop, _kci=kci, _nb=nb, _b=b,
                                   _pas=pas, _kc=kc: e.matmul(
                                       ps_g[_nb][:, :], _h[:, _kci, :],
                                       wdd[_b][:, _nb * 512:(_nb + 1) * 512],
                                       start=(_kc == 0 and _pas == 0),
                                       stop=(_kc == 63 and _pas == 1)),
                                   waits=w2,
                                   inc=("mm" if (pas == 1 and nb == 3) else None))
                    dn_seq.append(k.val("mm"))
            if tt == NTT - 1:
                k.emit("sync", lambda e: e.dma_start(
                    out=c_dum[:], in_=DI["bq"]), inc="lw", inc_amt=16)
            mmv = k.val("mm")
            for nb in range(4):
                k.emit("vector", lambda e, _tt=tt, _nb=nb: e.tensor_add(
                    strm[:, _tt, _nb * 512:(_nb + 1) * 512],
                    strm[:, _tt, _nb * 512:(_nb + 1) * 512], ps_g[_nb][:, :]),
                    waits=([("mm", mmv)] if nb == 0 else []), inc="ev")
            res_ev = k.val("ev")
            k.emit("scalar", lambda e, _tt=tt: e.activation(
                g_sb[:, 0:DM], strm[:, _tt, :], AF.Copy, bias=0.0, scale=1.0,
                accum_out=sx[:]), waits=[("ev", res_ev)], inc="ex")
            k.emit("scalar", lambda e, _tt=tt: e.activation(
                g_sb[:, 0:DM], strm[:, _tt, :], AF.Square, bias=0.0, scale=1.0,
                accum_out=sxx[:]), inc="ex")
            exv = k.val("ex")
            k.emit("vector", lambda e: e.tensor_scalar_mul(mu[:], sx[:], 1.0 / DM),
                   waits=[("ex", exv)], inc="ev")
            k.emit("vector", lambda e: e.tensor_scalar_mul(var[:], sxx[:], 1.0 / DM),
                   inc="ev")
            k.emit("vector", lambda e: e.tensor_mul(t1[:], mu[:], mu[:]),
                   waits=[("ev", k.val("ev"))], inc="ev")
            k.emit("vector", lambda e: e.tensor_sub(var[:], var[:], t1[:]),
                   waits=[("ev", k.val("ev"))], inc="ev")
            k.emit("scalar", lambda e: e.activation(t1[:], var[:], AF.Sqrt,
                                                    bias=eps5[:], scale=1.0),
                   waits=[("ev", k.val("ev"))], inc="ex")
            k.emit("vector", lambda e: e.reciprocal(inv[:], t1[:]),
                   waits=[("ex", k.val("ex"))], inc="ev")
            k.emit("vector", lambda e, _tt=tt: e.tensor_scalar(
                strm[:, _tt, :], strm[:, _tt, :], mu[:], inv[:],
                ALU.subtract, ALU.mult), waits=[("ev", k.val("ev"))], inc="ev")
            x2_ev = k.val("ev")
            k.emit("scalar", lambda e, _tt=tt: e.dma_start(
                out=x2_d[:, _tt, :], in_=strm[:, _tt, :]),
                waits=[("ev", x2_ev)], inc="st", inc_amt=16)
        k.C_ST = k.val("st")
        k.C_EV = k.val("ev")



def _phaseD(nc, k, DI, dr, es_d, ident_f, x3, taps):
    proj_d = dr["proj_d"]
    X_d = dr["X_d"]
    gate_d = dr["gate_d"]
    dtv_d = dr["dtv_d"]
    z_d = dr["z_d"]
    x2_d = dr["x2_d"]
    es = ExitStack()
    with es:
        cc5b = _sb(nc, es, "d_cc5b", [128, NOF, 3], F32)
        halo = _sb(nc, es, "d_halo", [128, NOF, 3], F32)
        s_in = _sb(nc, es, "d_sin", [128, NOF], F32)
        keep8 = _sb(nc, es, "d_keep", [128, NCORE], F32)
        notk8 = _sb(nc, es, "d_notk", [128, NCORE], F32)
        A_sb = _sb(nc, es, "d_A", [128, NOF], F32)
        alog = _sb(nc, es, "d_alog", [128, NOF], F32)
        hmask = _sb(nc, es, "d_hm", [128, 1], F32)
        pvi = _sb(nc, es, "d_pvi", [1, 1], I32)
        ps_p = [_psum(nc, es, f"d_psp{i}", [128, 512]) for i in range(2)]
        ps_d = _psum(nc, es, "d_psd", [128, T])
        ps_t = [_psum(nc, es, f"d_pst{i}", [128, 128]) for i in range(2)]

        c_st = k.C_ST
        c_ev = k.C_EV

        es1 = ExitStack()
        with es1:
            x2T = _sb(nc, es1, "d_x2T", [128, 16, T], F32)
            xa = [_sb(nc, es1, f"d_xa{i}", [128, DM], F32) for i in range(2)]
            win = [_sb(nc, es1, f"d_win{i}", [128, 16, 128], F32) for i in range(2)]
            ptile = _sb(nc, es1, "d_ptile", [128, T], F32)
            gtile = _sb(nc, es1, "d_gtile", [128, T], F32)

            tp_at = []
            for tt in range(NTT):
                b = tt % 2
                w = [("st", c_st), ("ev", c_ev)] if tt < 2 else \
                    [("tp", tp_at[tt - 2])]
                k.emit("sync", lambda e, _b=b, _t=tt: e.dma_start(
                    out=xa[_b][:], in_=x2_d[:, _t, :]), waits=w,
                    inc="ld", inc_amt=16)
                ldv = k.val("ld")
                for fc in range(16):
                    pb = fc % 2
                    w2 = [("ld", ldv)]
                    if tt > 0 or fc >= 2:
                        w2.append(("ev", k.val("ev") - 1))
                    k.emit("tensor", lambda e, _b=b, _fc=fc, _pb=pb: e.transpose(
                        ps_t[_pb][:], xa[_b][:, _fc * 128:(_fc + 1) * 128],
                        ident_f[:]), waits=w2, inc="tp")
                    k.emit("vector", lambda e, _t=tt, _fc=fc, _pb=pb: e.tensor_copy(
                        x2T[:, _fc, _t * 128:(_t + 1) * 128], ps_t[_pb][:]),
                        waits=[("tp", k.val("tp"))], inc="ev")
                tp_at.append(k.val("tp"))
            x2T_ev = k.val("ev")

            prg_seq = []
            for of in range(NOF):
                for which in range(2):
                    wsrc = (DI["w_in_t"], DI["w_gt_t"])[which]
                    idx = len(prg_seq)
                    w = [] if idx < 2 else [("mm", prg_seq[idx - 2])]
                    k.emit("sync", lambda e, _b=which, _s=wsrc, _of=of: e.dma_start(
                        out=win[_b][:], in_=_s[_of]), waits=w, inc="lw", inc_amt=16)
                    lwv = k.val("lw")
                    for kc in range(16):
                        for nb in range(2):
                            w2 = []
                            if kc == 0 and nb == 0:
                                w2 = [("lw", lwv), ("ev", x2T_ev)]
                                w2 += [("ev", k.val("ev")), ("ex", k.val("ex"))]
                            k.emit("tensor", lambda e, _b=which, _kc=kc, _nb=nb:
                                   e.matmul(ps_p[_nb][:, :], win[_b][:, _kc, :],
                                            x2T[:, _kc, _nb * 512:(_nb + 1) * 512],
                                            start=(_kc == 0), stop=(_kc == 15)),
                                   waits=w2,
                                   inc=("mm" if (kc == 15 and nb == 1) else None))
                    prg_seq.append(k.val("mm"))
                    mmv = k.val("mm")
                    if which == 0:
                        for nb in range(2):
                            k.emit("vector", lambda e, _nb=nb: e.tensor_copy(
                                ptile[:, _nb * 512:(_nb + 1) * 512], ps_p[_nb][:, :]),
                                waits=([("mm", mmv)] if nb == 0 else []), inc="ev")
                        k.emit("vector", lambda e, _of=of: e.tensor_copy(
                            cc5b[:, _of, :], ptile[:, T - 3:T]),
                            waits=[("ev", k.val("ev"))], inc="ev")
                        pev = k.val("ev")
                        k.emit("scalar", lambda e, _of=of: e.dma_start(
                            out=proj_d[:, _of, :], in_=ptile[:]),
                            waits=[("ev", pev)], inc="st", inc_amt=16)
                    else:
                        for nb in range(2):
                            k.emit("scalar", lambda e, _nb=nb: e.activation(
                                gtile[:, _nb * 512:(_nb + 1) * 512], ps_p[_nb][:, :],
                                AF.Sigmoid, bias=0.0, scale=1.0),
                                waits=([("mm", mmv)] if nb == 0 else []), inc="ex")
                        gev = k.val("ex")
                        k.emit("scalar", lambda e, _of=of: e.dma_start(
                            out=gate_d[:, _of, :], in_=gtile[:]),
                            waits=[("ex", gev)], inc="st", inc_amt=16)
            d1_st = k.val("st")
            d1_ev = k.val("ev")
            d1_ex = k.val("ex")
            k.emit("gpsimd", lambda e: e.dma_start(
                out=dr["cc5_in"].rearrange("o (p c) -> (o p) c", p=128),
                in_=cc5b[:].rearrange("p o c -> p (o c)")),
                waits=[("ev", d1_ev)], inc="gst", inc_amt=16)
            k.emit("gpsimd", lambda e: e.collective_compute(
                "AllGather", ALU.bypass, replica_groups=[list(range(NCORE))],
                ins=[dr["cc5_in"].opt()], outs=[dr["cc5_out"].opt()]),
                waits=[("gst", k.val("gst"))], inc="cc")
            cc5v = k.val("cc")

        es2 = ExitStack()
        with es2:
            ext = _sb(nc, es2, "d_ext", [128, T + 3], F32)
            Xt = _sb(nc, es2, "d_Xt", [128, T], F32)
            acc_a = _sb(nc, es2, "d_acca", [128, T], F32)
            acc_b = _sb(nc, es2, "d_accb", [128, T], F32)
            mx = _sb(nc, es2, "d_mx", [128, T], F32)
            ab = _sb(nc, es2, "d_ab", [128, T], F32)
            en = _sb(nc, es2, "d_en", [128, T], F32)
            dtv = _sb(nc, es2, "d_dtv", [128, T], F32)
            logd = _sb(nc, es2, "d_logd", [128, T], F32)
            decay = _sb(nc, es2, "d_decay", [128, T], F32)
            drive = _sb(nc, es2, "d_drive", [128, T], F32)
            scr = _sb(nc, es2, "d_scr", [128, T], F32)
            cw = _sb(nc, es2, "d_cw", [128, NOF, 4], F32)
            cb = _sb(nc, es2, "d_cb", [128, NOF, 1], F32)
            dtb = _sb(nc, es2, "d_dtb", [128, NOF, 1], F32)
            bp_sb = _sb(nc, es2, "d_bp", [128, NOF, 1], F32)
            cp_sb = _sb(nc, es2, "d_cp", [128, NOF, 1], F32)
            dp_sb = _sb(nc, es2, "d_dp", [128, NOF, 1], F32)
            dtw = [_sb(nc, es2, f"d_dtw{i}", [128, 128], F32) for i in range(2)]
            cc6b = _sb(nc, es2, "d_cc6b", [128, NOF, 2], F32)
            g6 = _sb(nc, es2, "d_g6", [128, 2 * NOF], F32)
            emf = _sb(nc, es2, "d_emf", [128, NOF], F32)
            tscr = _sb(nc, es2, "d_tscr", [128, NOF], F32)

            cst0 = [("st", d1_st), ("ev", d1_ev), ("ex", d1_ex),
                    ("gst", k.val("gst"))]
            for i, (nm, sb) in enumerate((
                    ("conv_w", cw), ("conv_b", cb), ("dtb_pt", dtb),
                    ("bp_pt", bp_sb), ("cp_pt", cp_sb), ("dp_pt", dp_sb),
                    ("halo_m", hmask), ("prev_i", pvi))):
                k.emit("sync", lambda e, _n=nm, _s=sb: e.dma_start(
                    out=_s[:], in_=DI[_n]), waits=(cst0 if i == 0 else ()),
                    inc="ld", inc_amt=16)
            k.emit("sync", lambda e: e.dma_start(
                out=alog[:], in_=DI["alog_pt"].rearrange("p o c -> p (o c)")),
                inc="ld", inc_amt=16)
            k.emit("sync", lambda e: e.dma_start(
                out=keep8[:], in_=DI["keep_b"]), inc="ld", inc_amt=16)
            cst_ld = k.val("ld")
            k.emit("scalar", lambda e: e.activation(A_sb[:], alog[:], AF.Exp,
                                                    bias=0.0, scale=1.0),
                   waits=[("ld", cst_ld)], inc="ex")
            k.emit("vector", lambda e: e.tensor_scalar_mul(A_sb[:], A_sb[:], -1.0),
                   waits=[("ex", k.val("ex"))], inc="ev")
            k.emit("vector", lambda e: e.tensor_scalar(
                notk8[:], keep8[:], -1.0, 1.0, ALU.mult, ALU.add),
                waits=[("ld", cst_ld)], inc="ev")

            def ld_halo(e):
                e.wait_ge(k.S("cc"), cc5v)
                e.wait_ge(k.S("ld"), cst_ld)
                with e.register("d_hr1") as r1:
                    e.load(r1, pvi[0:1, 0:1])
                    e.reg_mul(r1, r1, NOF * 128 * 3)
                    e.dma_start(
                        out=halo[:].rearrange("p o c -> p (o c)"),
                        in_=AP(tensor=dr["cc5_out"].tensor, offset=r1,
                               ap=[[NOF * 3, 128], [1, NOF * 3]])
                    ).then_inc(k.S("gld"), 16)
            k.cnt["gld"] += 16
            k.emit("gpsimd", lambda e: ld_halo(e))
            hl_gld = k.val("gld")
            k.emit("vector", lambda e: e.tensor_scalar_mul(
                halo[:].rearrange("p o c -> p (o c)"),
                halo[:].rearrange("p o c -> p (o c)"), hmask[:]),
                waits=[("gld", hl_gld), ("ev", k.val("ev"))], inc="ev")
            halo_ev = k.val("ev")

            def dt_chain(of, x_ex):
                mmv = k.val("mm")
                k.emit("vector", lambda e, _of=of: e.tensor_scalar(
                    ab[:], ps_d[:], dtb[:, _of, :], 0.0, ALU.add, ALU.max),
                    waits=[("mm", mmv)], inc="ev")
                k.emit("vector", lambda e, _of=of: e.tensor_scalar_add(
                    mx[:], ps_d[:], dtb[:, _of, :]), inc="ev")
                vv = k.val("ev")
                k.emit("scalar", lambda e: e.activation(en[:], mx[:], AF.Abs,
                                                        bias=0.0, scale=1.0),
                       waits=[("ev", vv)], inc="ex")
                k.emit("scalar", lambda e: e.activation(en[:], en[:], AF.Exp,
                                                        bias=0.0, scale=-1.0),
                       waits=[("ex", k.val("ex"))], inc="ex")
                k.emit("vector", lambda e: e.tensor_scalar_add(en[:], en[:], 1.0),
                       waits=[("ex", k.val("ex"))], inc="ev")
                k.emit("scalar", lambda e: e.activation(en[:], en[:], AF.Ln,
                                                        bias=0.0, scale=1.0),
                       waits=[("ev", k.val("ev"))], inc="ex")
                k.emit("vector", lambda e: e.tensor_add(dtv[:], ab[:], en[:]),
                       waits=[("ex", k.val("ex"))], inc="ev")

            dt_seq = []
            for of in range(NOF):
                b = of % 2
                w = [("ev", halo_ev)] if of == 0 else [("ev", k.val("ev"))]
                k.emit("sync", lambda e, _of=of: e.dma_start(
                    out=ext[:, 3:3 + T], in_=proj_d[:, _of, :]),
                    waits=w + ([("st", d1_st)] if of == 0 else []),
                    inc="ld", inc_amt=16)
                ldv = k.val("ld")
                k.emit("vector", lambda e, _of=of: e.tensor_copy(
                    ext[:, 0:3], halo[:, _of, :]), waits=[("ld", ldv)], inc="ev")
                k.emit("vector", lambda e, _of=of: e.tensor_scalar(
                    acc_a[:], ext[:, 0:T], cw[:, _of, 0:1], cb[:, _of, :],
                    ALU.mult, ALU.add), waits=[("ev", k.val("ev"))], inc="ev")
                k.emit("vector", lambda e, _of=of: e.scalar_tensor_tensor(
                    acc_b[:], ext[:, 1:1 + T], cw[:, _of, 1:2], acc_a[:],
                    ALU.mult, ALU.add), inc="ev")
                k.emit("vector", lambda e, _of=of: e.scalar_tensor_tensor(
                    acc_a[:], ext[:, 2:2 + T], cw[:, _of, 2:3], acc_b[:],
                    ALU.mult, ALU.add), inc="ev")
                k.emit("vector", lambda e, _of=of: e.scalar_tensor_tensor(
                    acc_b[:], ext[:, 3:3 + T], cw[:, _of, 3:4], acc_a[:],
                    ALU.mult, ALU.add), inc="ev")
                cv_ev = k.val("ev")
                k.emit("scalar", lambda e: e.activation(Xt[:], acc_b[:], AF.Silu,
                                                        bias=0.0, scale=1.0),
                       waits=[("ev", cv_ev)], inc="ex")
                x_ex = k.val("ex")
                k.emit("scalar", lambda e, _of=of: e.dma_start(
                    out=X_d[:, _of, :], in_=Xt[:]), waits=[("ex", x_ex)],
                    inc="st", inc_amt=16)
                idx = len(dt_seq)
                w3 = [] if idx < 2 else [("mm", dt_seq[idx - 2])]
                k.emit("sync", lambda e, _b=b, _of=of: e.dma_start(
                    out=dtw[_b][:], in_=DI["dtw_blk"][_of]), waits=w3,
                    inc="lw", inc_amt=16)
                lwv = k.val("lw")
                for nb in range(2):
                    w4 = []
                    if nb == 0:
                        w4 = [("lw", lwv), ("ex", x_ex), ("ev", k.val("ev"))]
                    k.emit("tensor", lambda e, _b=b, _nb=nb: e.matmul(
                        ps_d[:, _nb * 512:(_nb + 1) * 512], dtw[_b][:],
                        Xt[:, _nb * 512:(_nb + 1) * 512],
                        start=True, stop=True),
                        waits=w4, inc=("mm" if nb == 1 else None))
                dt_seq.append(k.val("mm"))
                dt_chain(of, x_ex)
                dt_ev = k.val("ev")
                k.emit("scalar", lambda e, _of=of: e.dma_start(
                    out=dtv_d[:, _of, :], in_=dtv[:]), waits=[("ev", dt_ev)],
                    inc="st", inc_amt=16)
                k.emit("vector", lambda e, _of=of: e.tensor_scalar_mul(
                    logd[:], dtv[:], A_sb[:, _of:_of + 1]),
                    waits=[("ev", dt_ev)], inc="ev")
                lg_ev = k.val("ev")
                k.emit("scalar", lambda e: e.activation(decay[:], logd[:], AF.Exp,
                                                        bias=0.0, scale=1.0),
                       waits=[("ev", lg_ev)], inc="ex")
                k.emit("vector", lambda e, _of=of: e.scalar_tensor_tensor(
                    drive[:], dtv[:], bp_sb[:, _of, :], Xt[:], ALU.mult, ALU.mult),
                    inc="ev")
                k.emit("vector", lambda e: e.tensor_tensor_scan(
                    scr[:], decay[:], drive[:], 0.0, ALU.mult, ALU.add),
                    waits=[("ex", k.val("ex")), ("ev", k.val("ev"))], inc="ev")
                k.emit("vector", lambda e, _of=of: e.tensor_copy(
                    cc6b[:, _of, 0:1], scr[:, T - 1:T]),
                    waits=[("ev", k.val("ev"))], inc="ev")
                k.emit("vector", lambda e, _of=of: e.tensor_reduce(
                    cc6b[:, _of, 1:2], logd[:], AXX, ALU.add),
                    waits=[("ev", k.val("ev"))], inc="ev")
            d3_ev = k.val("ev")
            d3_st = k.val("st")

            k.emit("gpsimd", lambda e: e.dma_start(
                out=dr["cc6_in"].rearrange("o (p c) -> (o p) c", p=128),
                in_=cc6b[:].rearrange("p o c -> p (o c)")),
                waits=[("ev", d3_ev)], inc="gst", inc_amt=16)
            k.emit("gpsimd", lambda e: e.collective_compute(
                "AllGather", ALU.bypass, replica_groups=[list(range(NCORE))],
                ins=[dr["cc6_in"].opt()], outs=[dr["cc6_out"].opt()]),
                waits=[("gst", k.val("gst"))], inc="cc")
            cc6v = k.val("cc")
            k.emit("vector", lambda e: e.memset(s_in[:], 0.0), inc="ev")
            for m in range(NCORE):
                w = [("cc", cc6v)] if m == 0 else [("ev", k.val("ev"))]
                k.emit("gpsimd", lambda e, _m=m: e.dma_start(
                    out=g6[:], in_=AP(tensor=dr["cc6_out"].tensor,
                                      offset=_m * NOF * 128 * 2,
                                      ap=[[2 * NOF, 128], [1, 2 * NOF]])),
                    waits=w, inc="gld", inc_amt=16)
                gv = k.val("gld")
                g6v = g6[:].rearrange("p (o c) -> p o c", c=2)
                k.emit("scalar", lambda e, _g=g6v: e.activation(
                    emf[:], _g[:, :, 1], AF.Exp, bias=0.0, scale=1.0),
                    waits=[("gld", gv), ("ev", k.val("ev"))], inc="ex")
                k.emit("vector", lambda e, _m=m: e.tensor_scalar(
                    emf[:], emf[:], keep8[:, _m:_m + 1], notk8[:, _m:_m + 1],
                    ALU.mult, ALU.add), waits=[("ex", k.val("ex"))], inc="ev")
                k.emit("vector", lambda e: e.tensor_mul(s_in[:], s_in[:], emf[:]),
                       waits=[("ev", k.val("ev"))], inc="ev")
                k.emit("vector", lambda e, _m=m, _g=g6v: e.tensor_scalar_mul(
                    tscr[:], _g[:, :, 0], keep8[:, _m:_m + 1]),
                    waits=[("ev", k.val("ev"))], inc="ev")
                k.emit("vector", lambda e: e.tensor_add(s_in[:], s_in[:], tscr[:]),
                       waits=[("ev", k.val("ev"))], inc="ev")
            sin_ev = k.val("ev")

            for of in range(NOF):
                w = ([("ev", sin_ev), ("st", d3_st)] if of == 0
                     else [("ev", k.val("ev"))])
                k.emit("sync", lambda e, _of=of: e.dma_start(
                    out=Xt[:], in_=X_d[:, _of, :]), waits=w, inc="ld", inc_amt=16)
                k.emit("sync", lambda e, _of=of: e.dma_start(
                    out=ext[:, 0:T], in_=gate_d[:, _of, :]), inc="ld", inc_amt=16)
                k.emit("sync", lambda e, _of=of: e.dma_start(
                    out=dtv[:], in_=dtv_d[:, _of, :]), inc="ld", inc_amt=16)
                ldv = k.val("ld")
                k.emit("vector", lambda e, _of=of: e.tensor_scalar_mul(
                    logd[:], dtv[:], A_sb[:, _of:_of + 1]),
                    waits=[("ld", ldv)], inc="ev")
                lg_ev = k.val("ev")
                k.emit("scalar", lambda e: e.activation(decay[:], logd[:], AF.Exp,
                                                        bias=0.0, scale=1.0),
                       waits=[("ev", lg_ev)], inc="ex")
                k.emit("vector", lambda e, _of=of: e.scalar_tensor_tensor(
                    drive[:], dtv[:], bp_sb[:, _of, :], Xt[:], ALU.mult, ALU.mult),
                    inc="ev")
                k.emit("vector", lambda e, _of=of: e.tensor_tensor_scan(
                    scr[:], decay[:], drive[:], s_in[:, _of:_of + 1],
                    ALU.mult, ALU.add),
                    waits=[("ex", k.val("ex")), ("ev", k.val("ev"))], inc="ev")
                k.emit("vector", lambda e, _of=of: e.tensor_scalar_mul(
                    drive[:], Xt[:], dp_sb[:, _of, :]),
                    waits=[("ev", k.val("ev"))], inc="ev")
                k.emit("vector", lambda e, _of=of: e.scalar_tensor_tensor(
                    scr[:], scr[:], cp_sb[:, _of, :], drive[:], ALU.mult, ALU.add),
                    waits=[("ev", k.val("ev"))], inc="ev")
                k.emit("vector", lambda e, _of=of: e.tensor_mul(
                    scr[:], scr[:], ext[:, 0:T]),
                    waits=[("ev", k.val("ev"))], inc="ev")
                zev = k.val("ev")
                k.emit("scalar", lambda e, _of=of: e.dma_start(
                    out=z_d[:, _of, :], in_=scr[:]), waits=[("ev", zev)],
                    inc="st", inc_amt=16)
            d5_st = k.val("st")

        es3 = ExitStack()
        with es3:
            zh = _sb(nc, es3, "d_zh", [128, NOF, 512], F32)
            wod = [_sb(nc, es3, f"d_wod{i}", [128, NOF, 128], F32)
                   for i in range(2)]
            ot = _sb(nc, es3, "d_ot", [128, 512], F32)
            ln_scr = _sb(nc, es3, "d_lnscr", [128, DM], F32)
            sx = _sb(nc, es3, "d_sx", [128, 1], F32)
            sxx = _sb(nc, es3, "d_sxx", [128, 1], F32)
            mu = _sb(nc, es3, "d_mu", [128, 1], F32)
            var = _sb(nc, es3, "d_var", [128, 1], F32)
            inv = _sb(nc, es3, "d_inv", [128, 1], F32)
            t1 = _sb(nc, es3, "d_t1", [128, 1], F32)
            eps5 = _sb(nc, es3, "d_eps5", [128, 1], F32)

            for tt in range(NTT):
                k.emit("sync", lambda e, _t=tt: e.dma_start(
                    out=x3[:, _t, :], in_=x2_d[:, _t, :]),
                    waits=([("st", c_st), ("ev", d3_ev)] if tt == 0 else ()),
                    inc="ld", inc_amt=16)
            x3_ldv = k.val("ld")
            k.emit("vector", lambda e: e.memset(eps5[:], 1e-05))

            wseq = []
            evac_at = []
            first_add = True
            for th in range(2):
                w = [("st", d5_st)] if th == 0 else [("ev", k.val("ev"))]
                zsrc = AP(tensor=z_d.tensor, offset=th * 512,
                          ap=[[NOF * T, 128], [T, NOF], [1, 512]])
                k.emit("sync", lambda e, _s=zsrc: e.dma_start(out=zh[:], in_=_s),
                       waits=w, inc="ld", inc_amt=16)
                zh_ldv = k.val("ld")
                for dm in range(16):
                    pb = dm % 2
                    idx = len(wseq)
                    w2 = [("st", d5_st)] if idx < 2 else [("mm", wseq[idx - 2])]
                    k.emit("sync", lambda e, _b=pb, _dm=dm: e.dma_start(
                        out=wod[_b][:], in_=DI["w_out_t"][_dm].rearrange(
                            "o p q -> p o q")), waits=w2, inc="lw", inc_amt=16)
                    lwv = k.val("lw")
                    for of in range(NOF):
                        w3 = []
                        if of == 0:
                            w3 = [("lw", lwv), ("ld", zh_ldv)]
                            if len(evac_at) >= 2:
                                w3.append(("ev", evac_at[-2]))
                        k.emit("tensor", lambda e, _b=pb, _of=of: e.matmul(
                            ps_p[_b][:, :],
                            wod[_b][:, _of, :],
                            zh[:, _of, :],
                            start=(_of == 0), stop=(_of == NOF - 1)),
                            waits=w3, inc=("mm" if of == NOF - 1 else None))
                    wseq.append(k.val("mm"))
                    mmv = k.val("mm")
                    wt = [("mm", mmv)]
                    if evac_at:
                        wt.append(("tp", k.val("tp")))
                    k.emit("vector", lambda e, _b=pb: e.tensor_copy(
                        ot[:], ps_p[_b][:, :]), waits=wt, inc="ev")
                    evac_at.append(k.val("ev"))
                    oev = k.val("ev")
                    for i4 in range(4):
                        tt = th * 4 + i4
                        tb = i4 % 2
                        w4 = [("ev", oev)]
                        if not (th == 0 and dm == 0 and i4 < 2):
                            w4.append(("ev", k.val("ev") - 1))
                        k.emit("tensor", lambda e, _i=i4, _tb=tb: e.transpose(
                            ps_t[_tb][:], ot[:, _i * 128:(_i + 1) * 128],
                            ident_f[:]), waits=w4, inc="tp")
                        w5 = [("tp", k.val("tp"))]
                        if first_add:
                            w5.append(("ld", x3_ldv))
                            first_add = False
                        k.emit("vector", lambda e, _t=tt, _dm=dm, _tb=tb:
                               e.tensor_add(
                                   x3[:, _t, _dm * 128:(_dm + 1) * 128],
                                   x3[:, _t, _dm * 128:(_dm + 1) * 128],
                                   ps_t[_tb][:]), waits=w5, inc="ev")
            add_ev = k.val("ev")

            for tt in range(NTT):
                w = [("ev", add_ev)] if tt == 0 else []
                k.emit("scalar", lambda e, _tt=tt: e.activation(
                    ln_scr[:], x3[:, _tt, :], AF.Copy, bias=0.0, scale=1.0,
                    accum_out=sx[:]), waits=w + [("ex", k.val("ex"))], inc="ex")
                k.emit("scalar", lambda e, _tt=tt: e.activation(
                    ln_scr[:], x3[:, _tt, :], AF.Square, bias=0.0, scale=1.0,
                    accum_out=sxx[:]), inc="ex")
                exv = k.val("ex")
                k.emit("vector", lambda e: e.tensor_scalar_mul(mu[:], sx[:], 1.0 / DM),
                       waits=[("ex", exv)], inc="ev")
                k.emit("vector", lambda e: e.tensor_scalar_mul(var[:], sxx[:], 1.0 / DM),
                       inc="ev")
                k.emit("vector", lambda e: e.tensor_mul(t1[:], mu[:], mu[:]),
                       waits=[("ev", k.val("ev"))], inc="ev")
                k.emit("vector", lambda e: e.tensor_sub(var[:], var[:], t1[:]),
                       waits=[("ev", k.val("ev"))], inc="ev")
                k.emit("scalar", lambda e: e.activation(t1[:], var[:], AF.Sqrt,
                                                        bias=eps5[:], scale=1.0),
                       waits=[("ev", k.val("ev"))], inc="ex")
                k.emit("vector", lambda e: e.reciprocal(inv[:], t1[:]),
                       waits=[("ex", k.val("ex"))], inc="ev")
                k.emit("vector", lambda e, _tt=tt: e.tensor_scalar(
                    x3[:, _tt, :], x3[:, _tt, :], mu[:], inv[:],
                    ALU.subtract, ALU.mult), waits=[("ev", k.val("ev"))], inc="ev")
        k.D_EV = k.val("ev")


def _phaseE(nc, k, DI, dr, es_de, ident_b, ident_f, ws_b, x3, y_out, taps):
    es = ExitStack()
    with es:
        xqb = _sb(nc, es, "e_xqb", [128, DM], BF16)
        xqT = _sb(nc, es, "e_xqT", [128, 16, 128], BF16)
        g_sb = _sb(nc, es, "e_g", [128, I1], F32)
        hqb = _sb(nc, es, "e_hqb", [128, I1], BF16)
        hqT = _sb(nc, es, "e_hqT", [128, 32, 128], BF16)
        wge = [_sb(nc, es, f"e_wge{i}", [128, 16, 256], BF16) for i in range(2)]
        wde = [_sb(nc, es, f"e_wde{i}", [128, DM], BF16) for i in range(2)]
        vq = _sb(nc, es, "e_vq", [128, DM], F32)
        cq = _sb(nc, es, "e_cq", [128, DM], I32)
        tq = _sb(nc, es, "e_tq", [128, DM], F32)
        uq = _sb(nc, es, "e_uq", [128, DM], F32)
        amax = _sb(nc, es, "e_amax", [128, 1], F32)
        sq = _sb(nc, es, "e_sq", [128, 1], F32)
        sc_g = _sb(nc, es, "e_scg", [128, 1], F32)
        sc_u = _sb(nc, es, "e_scu", [128, 1], F32)
        sc_d = _sb(nc, es, "e_scd", [128, 1], F32)
        sx = _sb(nc, es, "e_sx", [128, 1], F32)
        sxx = _sb(nc, es, "e_sxx", [128, 1], F32)
        mu = _sb(nc, es, "e_mu", [128, 1], F32)
        var = _sb(nc, es, "e_var", [128, 1], F32)
        inv = _sb(nc, es, "e_inv", [128, 1], F32)
        t1 = _sb(nc, es, "e_t1", [128, 1], F32)
        eps5 = _sb(nc, es, "e_eps5", [128, 1], F32)
        ln_scr = _sb(nc, es, "e_lnscr", [128, DM], F32)
        ps_e = [_psum(nc, es, f"e_ps{i}", [128, 512]) for i in range(4)]
        ps_tb = [es.enter_context(nc.psum_tensor(f"e_pstb{i}", [128, 128], BF16))
                 for i in range(2)]

        d_ev = k.D_EV
        k.emit("vector", lambda e: e.memset(eps5[:], 1e-05), waits=[("ev", d_ev)])

        def quant(src3, W, dst, rsc):
            k.emit("vector", lambda e: e.tensor_reduce(
                amax[:], src3, AXX, ALU.max, apply_absolute_value=True),
                waits=[("ex", k.val("ex"))], inc="ev")
            k.emit("vector", lambda e, _r=rsc: e.tensor_scalar(
                _r[:], amax[:], 1.0 / 127.0, EPS_Q, ALU.mult, ALU.add),
                waits=[("ev", k.val("ev"))], inc="ev")
            k.emit("vector", lambda e, _r=rsc: e.reciprocal(sq[:], _r[:]),
                   waits=[("ev", k.val("ev"))], inc="ev")
            rq_ev = k.val("ev")
            for ch in range(W // DM):
                s_ap = src3[:, ch * DM:(ch + 1) * DM]
                d_ap = dst[:, ch * DM:(ch + 1) * DM]
                k.emit("vector", lambda e, _s=s_ap: e.tensor_scalar_mul(
                    vq[:], _s, sq[:]), waits=[("ev", rq_ev)], inc="ev")
                k.emit("vector", lambda e: e.tensor_copy(cq[:], vq[:]), inc="ev")
                k.emit("vector", lambda e: e.tensor_copy(tq[:], cq[:]),
                       waits=[("ev", k.val("ev"))], inc="ev")
                k.emit("vector", lambda e: e.tensor_tensor(
                    uq[:], vq[:], tq[:], ALU.is_lt),
                    waits=[("ev", k.val("ev"))], inc="ev")
                k.emit("vector", lambda e: e.tensor_sub(tq[:], tq[:], uq[:]),
                       inc="ev")
                k.emit("vector", lambda e: e.tensor_sub(vq[:], vq[:], tq[:]),
                       inc="ev")
                k.emit("vector", lambda e: e.tensor_scalar(
                    uq[:], vq[:], 0.5, None, ALU.is_gt), inc="ev")
                k.emit("vector", lambda e: e.tensor_scalar(
                    vq[:], vq[:], 0.5, None, ALU.is_equal), inc="ev")
                k.emit("vector", lambda e: e.tensor_scalar_mul(ln_scr[:], tq[:], 0.5),
                       inc="ev")
                k.emit("vector", lambda e: e.tensor_copy(cq[:], ln_scr[:]), inc="ev")
                k.emit("vector", lambda e: e.tensor_copy(ln_scr[:], cq[:]),
                       waits=[("ev", k.val("ev"))], inc="ev")
                k.emit("vector", lambda e: e.scalar_tensor_tensor(
                    ln_scr[:], tq[:], 0.5, ln_scr[:], ALU.mult, ALU.subtract),
                    inc="ev")
                k.emit("vector", lambda e: e.tensor_mul(ln_scr[:], ln_scr[:], ln_scr[:]),
                       inc="ev")
                k.emit("vector", lambda e: e.tensor_scalar(
                    ln_scr[:], ln_scr[:], 0.1, None, ALU.is_gt), inc="ev")
                k.emit("vector", lambda e: e.tensor_mul(vq[:], vq[:], ln_scr[:]),
                       inc="ev")
                k.emit("vector", lambda e: e.tensor_add(tq[:], tq[:], uq[:]),
                       inc="ev")
                k.emit("vector", lambda e: e.tensor_add(tq[:], tq[:], vq[:]),
                       inc="ev")
                k.emit("vector", lambda e: e.tensor_scalar_min(tq[:], tq[:], 127.0),
                       inc="ev")
                k.emit("vector", lambda e: e.tensor_scalar_max(tq[:], tq[:], -128.0),
                       inc="ev")
                k.emit("vector", lambda e, _d=d_ap: e.tensor_copy(_d, tq[:]),
                       waits=[("ev", k.val("ev"))], inc="ev")

        gu_seq = []
        dn_seq = []
        gu_mm_prev = 0
        dn_mm_prev = 0
        ev_bank = {}
        for tt in range(NTT):
            x3t = x3[:, tt, :]
            quant(x3t, DM, xqb, sc_g)
            k.emit("vector", lambda e: e.tensor_scalar_mul(
                sc_u[:], sc_g[:], ws_b[:, 1:2]),
                waits=[("ev", k.val("ev"))], inc="ev")
            k.emit("vector", lambda e: e.tensor_scalar_mul(
                sc_g[:], sc_g[:], ws_b[:, 0:1]),
                waits=[("ev", k.val("ev"))], inc="ev")
            xq_ev = k.val("ev")
            for fc in range(16):
                pb = fc % 2
                w = [("ev", xq_ev), ("ev", k.val("ev") - 1)]
                if fc == 0 and (tt > 0):
                    w.append(("mm", gu_mm_prev))
                k.emit("tensor", lambda e, _fc=fc, _pb=pb: e.transpose(
                    ps_tb[_pb][:], xqb[:, _fc * 128:(_fc + 1) * 128], ident_b[:]),
                    waits=w, inc="tp")
                k.emit("vector", lambda e, _fc=fc, _pb=pb: e.tensor_copy(
                    xqT[:, _fc, :], ps_tb[_pb][:]), waits=[("tp", k.val("tp"))],
                    inc="ev")
            xqT_ev = k.val("ev")

            for jb in range(32):
                for which, wt_d in ((0, dr["wg1T_d"]), (1, dr["wu1T_d"])):
                    b = which
                    idx = len(gu_seq)
                    w = []
                    if idx >= 2:
                        w = [("mm", gu_seq[idx - 2])]
                    elif tt == 0:
                        w = [("ev", d_ev)]
                    src = AP(tensor=wt_d.tensor, offset=jb * 256,
                             ap=[[I1, 128], [128 * I1, 16], [1, 256]])
                    k.emit("sync", lambda e, _b=b, _s=src: e.dma_start(
                        out=wge[_b][:], in_=_s), waits=w, inc="lw", inc_amt=16)
                    lwv = k.val("lw") + 16
                    bank = idx % 4
                    for kc in range(16):
                        w2 = []
                        if kc == 0:
                            w2 = [("lw", lwv), ("ev", xqT_ev)]
                            if bank in ev_bank:
                                w2.append(ev_bank[bank])
                        k.emit("tensor", lambda e, _b=b, _kc=kc, _bk=bank:
                               e.matmul(ps_e[_bk][:, 0:256], xqT[:, _kc, :],
                                        wge[_b][:, _kc, :],
                                        start=(_kc == 0), stop=(_kc == 15)),
                               waits=w2, inc=("mm" if kc == 15 else None))
                    gu_seq.append(k.val("mm"))
                    mmv = k.val("mm")
                    if which == 0:
                        k.emit("scalar", lambda e, _jb=jb, _bk=bank: e.activation(
                            g_sb[:, _jb * 256:(_jb + 1) * 256], ps_e[_bk][:, 0:256],
                            AF.Silu, bias=0.0, scale=sc_g[:]),
                            waits=[("mm", mmv)], inc="ex")
                        ev_bank[bank] = ("ex", k.val("ex"))
                        g_ex = k.val("ex")
                    else:
                        k.emit("vector", lambda e, _jb=jb, _bk=bank:
                               e.scalar_tensor_tensor(
                                   g_sb[:, _jb * 256:(_jb + 1) * 256],
                                   ps_e[_bk][:, 0:256], sc_u[:],
                                   g_sb[:, _jb * 256:(_jb + 1) * 256],
                                   ALU.mult, ALU.mult),
                               waits=[("mm", mmv), ("ex", g_ex)], inc="ev")
                        ev_bank[bank] = ("ev", k.val("ev"))
            gu_mm_prev = k.val("mm")

            quant(g_sb[:, :], I1, hqb, sc_d)
            k.emit("vector", lambda e: e.tensor_scalar_mul(
                sc_d[:], sc_d[:], ws_b[:, 2:3]),
                waits=[("ev", k.val("ev"))], inc="ev")
            hq_ev = k.val("ev")
            for hh in range(2):
                for fc in range(32):
                    pb = fc % 2
                    w = [("ev", hq_ev), ("ev", k.val("ev") - 1)]
                    if fc == 0:
                        w.append(("mm", k.val("mm") if hh == 1 else dn_mm_prev))
                    k.emit("tensor", lambda e, _fc=fc, _hh=hh, _pb=pb: e.transpose(
                        ps_tb[_pb][:],
                        hqb[:, (_hh * 32 + _fc) * 128:(_hh * 32 + _fc + 1) * 128],
                        ident_b[:]), waits=w, inc="tp")
                    k.emit("vector", lambda e, _fc=fc, _pb=pb: e.tensor_copy(
                        hqT[:, _fc, :], ps_tb[_pb][:]), waits=[("tp", k.val("tp"))],
                        inc="ev")
                hqT_ev = k.val("ev")
                for kci in range(32):
                    kc = hh * 32 + kci
                    b = kc % 2
                    idx = len(dn_seq)
                    w = []
                    if idx >= 2:
                        w = [("mm", dn_seq[idx - 2])]
                    src = AP(tensor=dr["wd1T_d"].tensor, offset=kc * 128 * DM,
                             ap=[[DM, 128], [1, DM]])
                    k.emit("sync", lambda e, _b=b, _s=src: e.dma_start(
                        out=wde[_b][:], in_=_s), waits=w, inc="lw", inc_amt=16)
                    lwv = k.val("lw") + 16
                    for nb in range(4):
                        w2 = []
                        if nb == 0:
                            w2 = [("lw", lwv), ("ev", hqT_ev)]
                            if kc == 0:
                                w2 += [("ev", k.val("ev")), ("ex", k.val("ex"))]
                        k.emit("tensor", lambda e, _b=b, _kci=kci, _kc=kc, _nb=nb:
                               e.matmul(
                                   ps_e[_nb][:, :], hqT[:, _kci, :],
                                   wde[_b][:, _nb * 512:(_nb + 1) * 512],
                                   start=(_kc == 0), stop=(_kc == 63)),
                               waits=w2, inc=("mm" if nb == 3 else None))
                    dn_seq.append(k.val("mm"))
            dn_mm_prev = k.val("mm")
            if tt == NTT - 1:
                k.emit("sync", lambda e: e.dma_start(
                    out=ln_scr[:, 0:2], in_=DI["bq"]),
                    waits=[("mm", dn_seq[-2])], inc="lw", inc_amt=16)
            mmv = k.val("mm")
            for nb in range(4):
                k.emit("vector", lambda e, _tt=tt, _nb=nb: e.scalar_tensor_tensor(
                    x3[:, _tt, _nb * 512:(_nb + 1) * 512], ps_e[_nb][:, :],
                    sc_d[:], x3[:, _tt, _nb * 512:(_nb + 1) * 512],
                    ALU.mult, ALU.add),
                    waits=([("mm", mmv)] if nb == 0 else []), inc="ev")
            ev_bank.clear()
            res_ev = k.val("ev")
            k.emit("scalar", lambda e, _tt=tt: e.activation(
                ln_scr[:], x3[:, _tt, :], AF.Copy, bias=0.0, scale=1.0,
                accum_out=sx[:]), waits=[("ev", res_ev), ("ex", k.val("ex"))],
                inc="ex")
            k.emit("scalar", lambda e, _tt=tt: e.activation(
                ln_scr[:], x3[:, _tt, :], AF.Square, bias=0.0, scale=1.0,
                accum_out=sxx[:]), inc="ex")
            exv = k.val("ex")
            k.emit("vector", lambda e: e.tensor_scalar_mul(mu[:], sx[:], 1.0 / DM),
                   waits=[("ex", exv)], inc="ev")
            k.emit("vector", lambda e: e.tensor_scalar_mul(var[:], sxx[:], 1.0 / DM),
                   inc="ev")
            k.emit("vector", lambda e: e.tensor_mul(t1[:], mu[:], mu[:]),
                   waits=[("ev", k.val("ev"))], inc="ev")
            k.emit("vector", lambda e: e.tensor_sub(var[:], var[:], t1[:]),
                   waits=[("ev", k.val("ev"))], inc="ev")
            k.emit("scalar", lambda e: e.activation(t1[:], var[:], AF.Sqrt,
                                                    bias=eps5[:], scale=1.0),
                   waits=[("ev", k.val("ev"))], inc="ex")
            k.emit("vector", lambda e: e.reciprocal(inv[:], t1[:]),
                   waits=[("ex", k.val("ex"))], inc="ev")
            k.emit("vector", lambda e, _tt=tt: e.tensor_scalar(
                x3[:, _tt, :], x3[:, _tt, :], mu[:], inv[:],
                ALU.subtract, ALU.mult), waits=[("ev", k.val("ev"))], inc="ev")
            y_ev = k.val("ev")
            k.emit("gpsimd", lambda e, _tt=tt: e.dma_start(
                out=y_out[_tt * 128:(_tt + 1) * 128, :], in_=x3[:, _tt, :]),
                waits=[("ev", y_ev)], inc="so", inc_amt=16)


def _in_maps(inputs):
    f32 = np.float32
    x = np.ascontiguousarray(np.asarray(inputs["L0_attn_x"], dtype=f32)[0])
    Wqkv = np.asarray(inputs["L0_attn_Wqkv"], dtype=f32)
    bqkv = np.asarray(inputs["L0_attn_bqkv"], dtype=f32)
    Wproj = np.asarray(inputs["L0_attn_Wproj"], dtype=f32)
    bproj = np.asarray(inputs["L0_attn_bproj"], dtype=f32)
    wgu = np.asarray(inputs["L0_ffn_w_gate_up"], dtype=f32)
    wp0 = np.asarray(inputs["L0_ffn_w_proj"], dtype=f32)
    win = np.asarray(inputs["L1_attn_in_proj_w"], dtype=f32)
    wgt = np.asarray(inputs["L1_attn_gate_proj_w"], dtype=f32)
    wout = np.asarray(inputs["L1_attn_out_proj_w"], dtype=f32)
    convw = np.asarray(inputs["L1_attn_conv_w"], dtype=f32)
    convb = np.asarray(inputs["L1_attn_conv_b"], dtype=f32)
    dtw = np.asarray(inputs["L1_attn_dt_w"], dtype=f32)
    dtb = np.asarray(inputs["L1_attn_dt_b"], dtype=f32)
    alog = np.asarray(inputs["L1_attn_A_log"], dtype=f32)
    Bp = np.asarray(inputs["L1_attn_Bp"], dtype=f32)
    Cp = np.asarray(inputs["L1_attn_Cp"], dtype=f32)
    Dp = np.asarray(inputs["L1_attn_Dp"], dtype=f32)
    wg1 = np.asarray(inputs["L1_ffn_w_gate"], dtype=f32)
    wu1 = np.asarray(inputs["L1_ffn_w_up"], dtype=f32)
    wd1 = np.asarray(inputs["L1_ffn_w_down"], dtype=f32)

    C = np.ascontiguousarray
    tri = np.where(np.arange(128)[None, :] <= np.arange(128)[:, None],
                   np.zeros((128, 128), f32), np.full((128, 128), MSK, f32))
    idf = np.eye(128, dtype=f32)
    idb = np.eye(128, dtype=f32).astype(bfloat16)

    wproj_t = C(Wproj.reshape(16, 128, DM))
    bproj_b = C(np.broadcast_to(bproj, (128, DM)))
    w_in_t = C(win.reshape(NOF, 128, 16, 128).transpose(0, 3, 2, 1))
    w_gt_t = C(wgt.reshape(NOF, 128, 16, 128).transpose(0, 3, 2, 1))
    w_out_t = C(wout.reshape(16, 128, NOF, 128).transpose(0, 2, 3, 1))
    conv_w_pt = C(convw.reshape(NOF, 128, 4).transpose(1, 0, 2))
    conv_b_pt = C(convb.reshape(NOF, 128, 1).transpose(1, 0, 2))
    dtb_pt = C(np.repeat(dtb, 64).reshape(NOF, 128, 1).transpose(1, 0, 2))
    alog_pt = C(alog.reshape(NOF, 128, 1).transpose(1, 0, 2))
    bp_pt = C(Bp.reshape(NOF, 128, 1).transpose(1, 0, 2))
    cp_pt = C(Cp.reshape(NOF, 128, 1).transpose(1, 0, 2))
    dp_pt = C(Dp.reshape(NOF, 128, 1).transpose(1, 0, 2))
    dtw_blk = np.zeros((NOF, 128, 128), f32)
    for ct in range(NOF):
        for j in range(2):
            h = 2 * ct + j
            dtw_blk[ct, 64 * j:64 * j + 64, 64 * j:64 * j + 64] = dtw[h][:, None]
    dtw_blk = C(dtw_blk)

    ims = []
    for c in range(NCORE):
        t0 = c * T
        d = dict(
            x_full=x,
            x_own=C(x[t0:t0 + T]),
            wq=C(Wqkv[:, 256 * c:256 * c + 256].reshape(16, 128, 256).transpose(1, 0, 2)),
            wk=C(Wqkv[:, DM + 256 * c:DM + 256 * c + 256].reshape(16, 128, 256).transpose(1, 0, 2)),
            wv=C(Wqkv[:, 2 * DM + 256 * c:2 * DM + 256 * c + 256].reshape(16, 128, 256).transpose(1, 0, 2)),
            bq=C(bqkv[256 * c:256 * c + 256].reshape(2, 128).T),
            bk=C(bqkv[DM + 256 * c:DM + 256 * c + 256].reshape(2, 128).T),
            bv=C(bqkv[2 * DM + 256 * c:2 * DM + 256 * c + 256].reshape(2, 128).T),
            wproj_t=wproj_t,
            bproj_b=bproj_b,
            w_gu_sl=C(wgu[2048 * c:2048 * c + 2048].reshape(4, 4, 128, DM).transpose(0, 2, 1, 3)),
            w_p0_sl=C(wp0[256 * c:256 * c + 256].reshape(2, 128, H0)),
            w_in_t=w_in_t,
            w_gt_t=w_gt_t,
            w_out_t=w_out_t,
            conv_w=conv_w_pt,
            conv_b=conv_b_pt,
            dtw_blk=dtw_blk,
            dtb_pt=dtb_pt,
            alog_pt=alog_pt,
            bp_pt=bp_pt,
            cp_pt=cp_pt,
            dp_pt=dp_pt,
            w_g1_sl=C(wg1[T * c:T * c + T].reshape(8, 128, DM)),
            w_u1_sl=C(wu1[T * c:T * c + T].reshape(8, 128, DM)),
            w_d1_sl=C(wd1[256 * c:256 * c + 256].reshape(2, 128, I1)),
            tri=tri,
            idf=idf,
            idb=idb,
            rank_i=np.array([[c]], dtype=np.int32),
            prev_i=np.array([[(c + NCORE - 1) % NCORE]], dtype=np.int32),
            keep_b=C(np.broadcast_to((np.arange(NCORE) < c).astype(f32)[None, :],
                                     (128, NCORE))),
            halo_m=np.full((128, 1), 1.0 if c > 0 else 0.0, f32),
        )
        ims.append(d)
    return ims


def kernel(**inputs):
    ims = _in_maps(inputs)
    nc = build_nc()
    res = run_bass_kernel_spmd(nc, ims, core_ids=list(range(NCORE)))
    y = np.concatenate([np.asarray(res.results[c]["y"]) for c in range(NCORE)],
                       axis=0)
    return y.reshape(1, S, DM).astype(np.float32)
